# version 15
# baseline (speedup 1.0000x reference)
import contextlib
import numpy as np
import concourse.bass as bass
import concourse.mybir as mybir
from concourse.bass_utils import run_bass_kernel_spmd

F32 = mybir.dt.float32
BF16 = mybir.dt.bfloat16
AF = mybir.ActivationFunctionType
ALU = mybir.AluOpType
AX = mybir.AxisListType

B, T, C, H, HS = 4, 4096, 1024, 16, 64
CH, NH, NPAIR = 512, 8, 4
CK, NCK = 256, 16
L = 64
NSUB = CK // L
NTTG = CK // 128
TH = 2048
INNER = 4096
NSLAB, FTT = 8, 16
MAGIC = float(np.float32(1.5 * 2 ** 23))
MG4 = float(np.float32(1.5 * 2 ** 19))
EPS_GN = 0.00064
EPS_LN = 1e-5
EXPM05 = float(np.exp(np.float32(-0.5)))

ENGINES = ("tensor", "vector", "scalar", "gpsimd", "sync")


class Sem:
    __slots__ = ("h", "n", "name")

    def __init__(self, h, name):
        self.h, self.n, self.name = h, 0, name

    def evt(self):
        return (self, self.n)


class Pipe:

    def __init__(self, nc, es):
        self.nc, self.es = nc, es
        self.block = None
        self.self_sem = {}
        self.seen = {e: {} for e in ENGINES}
        self._pend = None

    def sem(self, name):
        return Sem(self.es.enter_context(self.nc.semaphore(name)), name)

    def begin(self, block):
        self.block = block

    def _emit(self, eng, fn, w, inc, amt):
        seen = self.seen[eng]

        def body(engine):
            for (s_, c_) in w:
                if seen.get(id(s_), -1) < c_:
                    engine.wait_ge(s_.h, c_)
                    seen[id(s_)] = c_
            instr = fn(engine)
            if inc is not None:
                instr.then_inc(inc.h, amt)
        getattr(self.block, eng)(body)

    def _flush(self):
        if self._pend is not None:
            fn, w, inc, amt = self._pend
            self._pend = None
            self._emit("tensor", fn, w, inc, amt)

    def do(self, eng, fn, waits=(), inc=None, amt=1):
        w = []
        for x in waits:
            if x is None:
                continue
            if isinstance(x, list):
                for (s_, c_) in x:
                    if c_ > 0:
                        w.append((s_, c_))
            else:
                (s_, c_) = x
                if c_ > 0:
                    w.append((s_, c_))
        ss = self.self_sem.get(eng)
        if ss is not None and ss.n > 0:
            w.append((ss, ss.n))
        if eng == "tensor":
            self._flush()
            self._pend = (fn, w, inc, amt)
            if inc is not None:
                inc.n += amt
                return (inc, inc.n)
            return None
        self._emit(eng, fn, w, inc, amt)
        if inc is not None:
            inc.n += amt
            return (inc, inc.n)
        return None

    def attach_pe(self, pe):
        assert self._pend is not None
        fn, w, inc, amt = self._pend
        assert inc is None
        self._pend = (fn, w, pe, 1)
        pe.n += 1
        return (pe, pe.n)

    def finalize(self, block=None):
        self._flush()


class QRing:

    def __init__(self, pipe, name, n=4):
        self.sems = [pipe.sem(f"{name}{i}") for i in range(n)]
        self.i = 0

    def next(self):
        s = self.sems[self.i % len(self.sems)]
        self.i += 1
        return s

    def evt(self):
        return [(s, s.n) for s in self.sems]


class Bank:
    __slots__ = ("ap", "war")

    def __init__(self, t):
        self.ap, self.war = t, None


def host_prep(ins):
    x = np.asarray(ins["L0_attn_x"], np.float32)
    maps = []
    wgT = np.ascontiguousarray(np.asarray(ins["L0_ffn_w_gate"], np.float32).T)
    wuT = np.ascontiguousarray(np.asarray(ins["L0_ffn_w_up"], np.float32).T)
    wdT = np.ascontiguousarray(np.asarray(ins["L0_ffn_w_down"], np.float32).T)
    for core in range(8):
        b, hg = divmod(core, 2)
        ch = slice(hg * CH, (hg + 1) * CH)
        th = slice(hg * TH, (hg + 1) * TH)
        g = lambda k: np.asarray(ins["L0_attn_" + k], np.float32)
        r16a2 = np.zeros((65, 1024), np.float32)
        r16a2[0:16, 0:512] = g("aaa_w2")[:, ch]
        r16a2[32:48, 512:1024] = g("ma_w2")[:, ch]
        r16a2[64, 0:512] = g("time_aaaaa")[ch]
        r16a2[64, 512:1024] = g("time_misc_a")[ch]
        r16b2 = np.zeros((65, 1024), np.float32)
        r16b2[0:16, 0:512] = g("kkk_w2")[:, ch]
        r16b2[32:48, 512:1024] = g("mk_w2")[:, ch]
        r16b2[64, 512:1024] = g("time_misc_k")[ch]
        r16aT = np.zeros((C, 48), np.float32)
        r16aT[:, 0:16] = g("aaa_w1").T
        r16aT[:, 32:48] = g("ma_w1").T
        r16bT = np.zeros((C, 48), np.float32)
        r16bT[:, 0:16] = g("kkk_w1").T
        r16bT[:, 32:48] = g("mk_w1").T
        dec2b = np.concatenate([g("decay_w2")[:, ch], g("time_decay")[None, ch]], 0)
        sc = slice(core * 512, (core + 1) * 512)
        wstat = np.stack([wgT[:, sc].reshape(128, 4096),
                          wuT[:, sc].reshape(128, 4096),
                          wdT[sc, :].reshape(128, 4096)], axis=1)
        m = {
            "xT": np.ascontiguousarray(x[b].T),
            "xtm": np.ascontiguousarray(x[b, th]),
            "tmx_c": np.ascontiguousarray(g("time_maa_x").reshape(8, 128).T),
            "tmg_c": np.ascontiguousarray(
                g("time_maa").reshape(4, 8, 128).transpose(2, 1, 0)),
            "maa1T": np.ascontiguousarray(g("maa_w1").T),
            "maa2": np.ascontiguousarray(g("maa_w2")),
            "wrT": np.ascontiguousarray(g("Wr").T[:, ch]),
            "wkT": np.ascontiguousarray(g("Wk").T[:, ch]),
            "wvT": np.ascontiguousarray(g("Wv").T[:, ch]),
            "woT": np.ascontiguousarray(g("Wo").T[ch, :]),
            "gate1T": np.ascontiguousarray(g("gate_w1").T),
            "gate2": np.ascontiguousarray(g("gate_w2")[:, ch]),
            "dec1T": np.ascontiguousarray(g("decay_w1").T),
            "dec2b": np.ascontiguousarray(dec2b),
            "r16aT": r16aT, "r16bT": r16bT, "r16a2": r16a2, "r16b2": r16b2,
            "faa": np.ascontiguousarray(g("time_faaaa")[None, ch]),
            "lnw": np.ascontiguousarray(g("ln_w")[None, ch]),
            "lnb": np.ascontiguousarray(g("ln_b")[None, ch]),
            "wstat": np.ascontiguousarray(wstat),
            "wgT": wgT, "wuT": wuT, "wdT": wdT,
        }
        maps.append(m)
    return maps


def build_program(dbg=(), nchunks=NCK, phases="ABCD", sub=99):
    nc = bass.Bass("TRN2", target_bir_lowering=False)
    es = contextlib.ExitStack()
    d = {}

    def din(name, shape):
        d[name] = nc.dram_tensor(name, shape, F32, kind="ExternalInput").ap()

    din("xT", [C, T]); din("xtm", [TH, C])
    din("tmx_c", [128, 8]); din("tmg_c", [128, 8, 4])
    din("maa1T", [C, 128]); din("maa2", [4, 32, C])
    din("wrT", [C, CH]); din("wkT", [C, CH]); din("wvT", [C, CH])
    din("woT", [CH, C])
    din("gate1T", [C, 128]); din("gate2", [128, CH])
    din("dec1T", [C, 64]); din("dec2b", [65, CH])
    din("r16aT", [C, 48]); din("r16bT", [C, 48])
    din("r16a2", [65, 1024]); din("r16b2", [65, 1024])
    din("faa", [1, CH]); din("lnw", [1, CH]); din("lnb", [1, CH])
    din("wstat", [128, 3, 4096])
    din("wgT", [C, INNER]); din("wuT", [C, INNER]); din("wdT", [INNER, C])
    out = nc.dram_tensor("out", [TH, C], F32, kind="ExternalOutput").ap()

    opart = nc.dram_tensor("opart", [T, C], F32).ap()
    osum = nc.dram_tensor("osum", [TH, C], F32).ap()
    x1sp = nc.dram_tensor("x1sp", [TH, C], F32).ap()
    hsp = nc.dram_tensor("hsp", [TH, INNER], F32).ap()
    stat_dram = nc.dram_tensor("stat_dram", [1, 8], F32).ap()
    stin = nc.dram_tensor("stin", [8, 8], F32).ap()
    stout = nc.dram_tensor("stout", [1, 8], F32).ap()

    dbg_out = {}
    for name, shape in dbg:
        dbg_out[name] = nc.dram_tensor("dbg_" + name, shape, F32,
                                       kind="ExternalOutput").ap()

    with es:
        P = Pipe(nc, es)

        def sb_outer(name, shape, dt=F32):
            return es.enter_context(nc.sbuf_tensor(name, shape, dt))

        sl = QRing(P, "sl"); sg = QRing(P, "sg"); st = QRing(P, "st")
        pe = P.sem("pe"); dv = P.sem("dv"); ac = P.sem("ac")
        po = P.sem("po"); cc = P.sem("cc"); bar = P.sem("bar")
        P.self_sem = {"vector": dv, "scalar": ac, "gpsimd": po}
        qrings = {"sync": sl, "gpsimd": sg, "st": st}
        P.begin(es.enter_context(nc.Block(no_gpsimd_drain=False)))

        banks = [Bank(es.enter_context(nc.psum_tensor(f"pb{i}", [128, 512],
                                                      F32)))
                 for i in range(8)]
        rr = [0]

        def grab():
            b = banks[rr[0] % 8]
            rr[0] += 1
            return b

        def mm(bank, out_ap, lhsT, rhs, start, stop, tp=None, waits=(),
               inc=None, skip=False):
            w = list(waits)
            if bank is not None and start and bank.war is not None:
                w.append(bank.war)
            def fn(t, o=out_ap, l=lhsT, r=rhs, s1=start, s2=stop, tp=tp,
                   sk=skip):
                return t.matmul(o, l, r, start=s1, stop=s2, tile_position=tp,
                                skip_group_check=sk)
            return P.do("tensor", fn, w, inc)

        def penop(waits=()):
            return P.attach_pe(pe)

        def dma(queue, out_ap, in_ap, waits=(), sem=None, transpose=False):
            ring = st if transpose else qrings[queue]
            s_ = ring.next()
            w = list(waits) + [(s_, s_.n)]
            def fn(e, o=out_ap, i=in_ap, tr=transpose):
                if tr:
                    return e.dma_start(out=o, in_=i, transpose=True)
                return e.dma_start(out=o, in_=i)
            return P.do(queue, fn, w, s_, 16)

        def act(fn, waits=()):
            return P.do("scalar", fn, waits, ac)

        def vec(fn, waits=()):
            return P.do("vector", fn, waits, dv)

        def pool(fn, waits=()):
            return P.do("gpsimd", fn, waits, po)

        tmx = sb_outer("tmx", [128, 8])
        tmg = sb_outer("tmg", [128, 8, 4])
        dma("sync", tmx[:], d["tmx_c"])
        dma("sync", tmg[:], d["tmg_c"])
        rows_bf = sb_outer("rows_bf", [1, 3, CH], BF16)
        dma("gpsimd", rows_bf[:, 0, :], d["faa"])
        dma("gpsimd", rows_bf[:, 1, :], d["lnw"])
        dma("gpsimd", rows_bf[:, 2, :], d["lnb"])
        vload = sl.evt()
        rows_ld = sg.evt()

        ident = sb_outer("ident", [128, 64], BF16)
        tri = sb_outer("tri", [128, 64], BF16)
        maskS = sb_outer("maskS", [128, 64])
        maskSU = sb_outer("maskSU", [128, 64])
        maskIU = sb_outer("maskIU", [128, 64])
        sel = sb_outer("sel", [128, NTTG, NSUB], BF16)
        iscr = sb_outer("iscr", [128, 64])
        consts = sb_outer("consts", [128, 8])
        ones1 = sb_outer("ones1", [1, 128], BF16)
        z1 = sb_outer("z1", [1, 128], BF16)
        zr = sb_outer("zr", [1, 512], BF16)
        onesc = sb_outer("onesc", [128, 4], BF16)
        rows_bc = sb_outer("rows_bc", [128, 3, CH], BF16)
        swcol = sb_outer("swcol", [128, 8])

        pool(lambda g: g.iota(iscr[0:64, :], [[-1, 64]], base=0,
                              channel_multiplier=1,
                              allow_small_or_imprecise_dtypes=True))
        pool(lambda g: g.iota(iscr[64:128, :], [[-1, 64]], base=0,
                              channel_multiplier=1,
                              allow_small_or_imprecise_dtypes=True))
        io1 = po.evt()
        vec(lambda v: v.tensor_scalar(ident[:], iscr[:], 0.0, None,
                                      ALU.is_equal), waits=[io1])
        vec(lambda v: v.tensor_scalar(tri[:], iscr[:], 0.0, None, ALU.is_le))
        vec(lambda v: v.tensor_scalar(maskS[:], iscr[:], 0.0, None, ALU.is_gt))
        vec(lambda v: v.tensor_scalar(maskSU[:], iscr[:], 0.0, None,
                                      ALU.is_lt))
        vec(lambda v: v.tensor_scalar(maskIU[:], iscr[:], 0.0, None,
                                      ALU.is_le))
        vec(lambda v: v.memset(ones1[:], 1.0))
        vec(lambda v: v.memset(z1[:], 0.0))
        vec(lambda v: v.memset(zr[:], 0.0))
        vec(lambda v: v.memset(onesc[:], 0.0))
        vec(lambda v: v.memset(onesc[:, 0:1], 1.0))
        for j, val in enumerate([-1.0, -0.5, EPS_GN, EPS_LN, 1e-24,
                                 1.0 / HS, 127.0, 1e-5]):
            vec(lambda v, j=j, val=val: v.memset(consts[:, j:j + 1], val))
        CN1 = consts[:, 0:1]; CNH = consts[:, 1:2]; CEG = consts[:, 2:3]
        CEL = consts[:, 3:4]
        pool(lambda g: g.iota(iscr[:, 0:NSUB], [[1, NSUB]], base=0,
                              channel_multiplier=0,
                              allow_small_or_imprecise_dtypes=True),
             waits=[dv.evt()])
        pool(lambda g: g.iota(iscr[:, 8:9], [[0, 1]], base=0,
                              channel_multiplier=1,
                              allow_small_or_imprecise_dtypes=True))
        io2 = po.evt()
        vec(lambda v: v.tensor_scalar(iscr[:, 9:10], iscr[:, 8:9], 63.5, None,
                                      ALU.is_gt), waits=[io2])
        for tt in range(NTTG):
            vec(lambda v, tt=tt: v.tensor_scalar(
                iscr[:, 10:11], iscr[:, 9:10], 1.0, float(2 * tt),
                ALU.mult, ALU.add))
            vec(lambda v, tt=tt: v.scalar_tensor_tensor(
                iscr[:, 11:11 + NSUB],
                iscr[:, 10:11].to_broadcast((128, NSUB)), -1.0,
                iscr[:, 0:NSUB], ALU.mult, ALU.add))
            vec(lambda v, tt=tt: v.tensor_scalar(
                sel[:, tt, :], iscr[:, 11:11 + NSUB], 0.0, None,
                ALU.is_equal))
        consts_done = dv.evt()

        if sub == -4:
            P.do("vector", lambda v: v.engine_nop(),
                 [sl.evt(), sg.evt(), st.evt(), ac.evt(), po.evt(),
                  pe.evt(), dv.evt()], dv)
            P.finalize()
            return nc

        rbf = rows_ld
        for j in range(3):
            bb = grab()
            mm(bb, bb.ap[:, 0:512], ones1[:], rows_bf[:, j, :], True, True,
               waits=[rbf, consts_done])
            penop()
            vec(lambda v, bb=bb, j=j: v.tensor_copy(rows_bc[:, j, :],
                                                    bb.ap[:, 0:512]),
                waits=[pe.evt()])
            bb.war = dv.evt()

        if sub == -3:
            P.do("vector", lambda v: v.engine_nop(),
                 [sl.evt(), sg.evt(), st.evt(), ac.evt(), po.evt(),
                  pe.evt(), dv.evt()], dv)
            P.finalize()
            return nc
        bc_done = dv.evt()
        FAA_BC = rows_bc[:, 0, :]
        LNW_BC = rows_bc[:, 1, :]
        LNB_BC = rows_bc[:, 2, :]

        wst = sb_outer("wst", [128, 512])
        stat4 = sb_outer("stat4", [128, 4])
        stat_c = sb_outer("stat_c", [128, 1])
        stat_bf = sb_outer("stat_bf", [128, 12], BF16)
        statr = sb_outer("statr", [128, 4])
        strow = sb_outer("strow", [4, 2])
        stlo = sb_outer("stlo", [4, 1])
        swsum = sb_outer("swsum", [1, 8])
        vec(lambda v: v.memset(stat4[:], 0.0))
        for wi in range(3):
            for pc in range(8):
                dma("sync", wst[:],
                    d["wstat"][:, wi, pc * 512:(pc + 1) * 512],
                    waits=[dv.evt()])
                vec(lambda v: v.tensor_reduce(
                    stat_c[:], wst[:], AX.X, ALU.add,
                    apply_absolute_value=True), waits=[sl.evt()])
                vec(lambda v, wi=wi: v.tensor_tensor(
                    stat4[:, wi:wi + 1], stat4[:, wi:wi + 1], stat_c[:],
                    ALU.add))
        vec(lambda v: v.tensor_copy(stat_bf[:, 0:4], stat4[:]))
        vec(lambda v: v.tensor_tensor(statr[:], stat4[:],
                                      stat_bf[:, 0:4], ALU.subtract))
        vec(lambda v: v.tensor_copy(stat_bf[:, 4:8], statr[:]))
        vec(lambda v: v.tensor_tensor(stat_bf[:, 8:12], statr[:],
                                      stat_bf[:, 4:8], ALU.subtract))
        bs = grab()
        mm(bs, bs.ap[0:4, 0:1], stat_bf[:, 0:4], onesc[:, 0:1], True, True,
           waits=[dv.evt(), consts_done])
        mm(bs, bs.ap[0:4, 1:2], stat_bf[:, 4:8], onesc[:, 0:1], True, True)
        mm(bs, bs.ap[0:4, 2:3], stat_bf[:, 8:12], onesc[:, 0:1], True, True)
        penop()
        vec(lambda v, bs=bs: v.tensor_scalar(
            strow[:, 0:1], bs.ap[0:4, 0:1], MG4, MG4, ALU.add, ALU.subtract),
            waits=[pe.evt()])
        vec(lambda v, bs=bs: v.tensor_tensor(
            stlo[:], bs.ap[0:4, 0:1], strow[:, 0:1], ALU.subtract))
        vec(lambda v, bs=bs: v.tensor_tensor(
            stlo[:], stlo[:], bs.ap[0:4, 1:2], ALU.add))
        vec(lambda v, bs=bs: v.tensor_tensor(
            strow[:, 1:2], stlo[:], bs.ap[0:4, 2:3], ALU.add))
        bs.war = dv.evt()
        dma("gpsimd", stat_dram[0, 0:4], strow[:, 0], waits=[dv.evt()])
        dma("gpsimd", stat_dram[0, 4:8], strow[:, 1], waits=[dv.evt()])
        dma("gpsimd", stin[:, :], stat_dram.to_broadcast((8, 8)),
            waits=[sg.evt()])
        P.do("gpsimd", lambda g: g.collective_compute(
            "ReduceScatter", ALU.add, replica_groups=[list(range(8))],
            ins=[stin[:, :]], outs=[stout[:, :]]), [sg.evt()], cc)
        dma("gpsimd", swsum[:], stout[:, :], waits=[cc.evt()])
        swr = sb_outer("swr", [1, 24], BF16)
        swf = sb_outer("swf", [1, 8])
        swfr = sb_outer("swfr", [1, 8])
        vec(lambda v: v.tensor_tensor(
            swsum[:, 0:4], swsum[:, 0:4], swsum[:, 4:8], ALU.add),
            waits=[sg.evt()])
        vec(lambda v: v.tensor_scalar(
            swsum[:, 0:4], swsum[:, 0:4], 1.0 / (C * INNER), 1e-5, ALU.mult,
            ALU.max))
        vec(lambda v: v.tensor_copy(swf[:, 4:8], swsum[:, 0:4]))
        vec(lambda v: v.reciprocal(swsum[:, 0:4], swsum[:, 0:4]))
        vec(lambda v: v.tensor_copy(swf[:, 0:4], swsum[:, 0:4]))
        vec(lambda v: v.tensor_copy(swr[:, 0:8], swf[:]))
        vec(lambda v: v.tensor_tensor(swfr[:], swf[:], swr[:, 0:8],
                                      ALU.subtract))
        vec(lambda v: v.tensor_copy(swr[:, 8:16], swfr[:]))
        vec(lambda v: v.tensor_tensor(swr[:, 16:24], swfr[:], swr[:, 8:16],
                                      ALU.subtract))
        bs2 = grab()
        mm(bs2, bs2.ap[:, 0:24], ones1[:], swr[:], True, True,
           waits=[dv.evt()])
        penop()
        vec(lambda v, bs2=bs2: v.tensor_copy(swcol[:], bs2.ap[:, 0:8]),
            waits=[pe.evt()])
        vec(lambda v, bs2=bs2: v.tensor_tensor(
            swcol[:], swcol[:], bs2.ap[:, 8:16], ALU.add))
        vec(lambda v, bs2=bs2: v.tensor_tensor(
            swcol[:], swcol[:], bs2.ap[:, 16:24], ALU.add))
        bs2.war = dv.evt()
        sw_done = dv.evt()

        if sub == -2:
            if "swc" in dbg_out:
                dma("sync", dbg_out["swc"][:, 0:8], swcol[:], waits=[sw_done])
                dma("sync", dbg_out["swc"][:, 8:12],
                    stat4[:], waits=[sw_done])
            P.do("vector", lambda v: v.engine_nop(),
                 [sl.evt(), sg.evt(), st.evt(), ac.evt(), po.evt(),
                  pe.evt(), dv.evt()], dv)
            P.finalize()
            return nc

        esA = es.enter_context(contextlib.ExitStack())

        def sba(name, shape, dt=BF16):
            return esA.enter_context(nc.sbuf_tensor(name, shape, dt))

        w_wr = sba("w_wr", [128, 8, CH]); w_wk = sba("w_wk", [128, 8, CH])
        w_wv = sba("w_wv", [128, 8, CH]); w_wo = sba("w_wo", [128, 4, C])
        w_wrl = sba("w_wrl", [128, 8, CH]); w_wkl = sba("w_wkl", [128, 8, CH])
        w_wvl = sba("w_wvl", [128, 8, CH])
        w_maa1 = sba("w_maa1", [128, 8, 128])
        w_m2A = sba("w_m2A", [64, C])
        w_m2B = sba("w_m2B", [64, C])
        w_g1 = sba("w_g1", [128, 8, 128]); w_g2 = sba("w_g2", [128, CH])
        w_d1 = sba("w_d1", [128, 8, 64]); w_d2 = sba("w_d2", [65, CH])
        w_ra1 = sba("w_ra1", [128, 8, 48]); w_rb1 = sba("w_rb1", [128, 8, 48])
        w_ra2 = sba("w_ra2", [65, 1024]); w_rb2 = sba("w_rb2", [65, 1024])
        for dst, name in ((w_wr, "wrT"), (w_wk, "wkT"), (w_wv, "wvT"),
                          (w_wo, "woT"), (w_maa1, "maa1T"), (w_g1, "gate1T"),
                          (w_d1, "dec1T"), (w_ra1, "r16aT"), (w_rb1, "r16bT")):
            dma("gpsimd", dst[:], d[name].rearrange("(c p) o -> p c o", p=128))
        dma("gpsimd", w_m2A[:], d["maa2"][0:2].rearrange("g k c -> (g k) c"))
        dma("gpsimd", w_m2B[:], d["maa2"][2:4].rearrange("g k c -> (g k) c"))
        dma("gpsimd", w_g2[:], d["gate2"])
        dma("gpsimd", w_d2[:], d["dec2b"])
        dma("gpsimd", w_ra2[:], d["r16a2"])
        dma("gpsimd", w_rb2[:], d["r16b2"])
        esA0 = contextlib.ExitStack()
        wfp = esA0.enter_context(nc.sbuf_tensor("wfp", [128, 8, CH], F32))
        for hi_t, lo_t, name in ((w_wr, w_wrl, "wrT"), (w_wk, w_wkl, "wkT"),
                                 (w_wv, w_wvl, "wvT")):
            dma("sync", wfp[:], d[name].rearrange("(c p) o -> p c o", p=128),
                waits=[dv.evt()])
            vec(lambda v, hi_t=hi_t, lo_t=lo_t: v.tensor_tensor(
                lo_t[:], wfp[:], hi_t[:], ALU.subtract),
                waits=[sl.evt(), sg.evt()])
        esA0.close()
        wld = sg.evt()
        wlo_done = dv.evt()

        if sub == -1:
            P.do("vector", lambda v: v.engine_nop(),
                 [sl.evt(), sg.evt(), st.evt(), ac.evt(), po.evt(),
                  pe.evt(), dv.evt()], dv)
            P.finalize()
            return nc

        NTT = CK // 128
        xf = sba("xf", [128, 8, CK + 1], F32)
        xm0 = sba("xm0", [128, 8, CK], F32)
        xm1 = sba("xm1", [128, 8, CK], F32)
        xxt = sba("xxt", [128, 8, CK])
        xmh = sba("xmh", [128, 8, CK])
        xml = sba("xml", [128, 8, CK])
        loA = sba("loA", [64, CK]); loB = sba("loB", [64, CK])
        g_in = sba("g_in", [128, CK])
        dec_in = sba("dec_in", [65, CK]); ra_in = sba("ra_in", [65, CK])
        rb_in = sba("rb_in", [65, CK])
        dec_inl = sba("dec_inl", [65, CK])
        dec_f = sba("dec_f", [64, CK], F32)
        vec(lambda v: v.memset(dec_in[:], 0.0))
        vec(lambda v: v.memset(ra_in[:], 0.0))
        vec(lambda v: v.memset(rb_in[:], 0.0))
        vec(lambda v: v.memset(dec_inl[:], 0.0))
        vec(lambda v: v.memset(dec_in[64:65, :], 1.0))
        vec(lambda v: v.memset(ra_in[64:65, :], 1.0))
        vec(lambda v: v.memset(rb_in[64:65, :], 1.0))
        inner_init = dv.evt()

        def tokf(name):
            return sba(name, [128, NTT, CH], F32)

        def tokb(name):
            return sba(name, [128, NTT, CH])

        rtok = tokf("rtok"); gtok = tokb("gtok")
        ktmp = tokf("ktmp"); kkn = tokf("kkn"); kfin = tokf("kfin")
        asig = tokf("asig"); masig = tokf("masig"); mksig = tokf("mksig")
        ept = tokb("ept"); emt = tokb("emt"); epvt = tokb("epvt")
        sdec = tokf("sdec"); scr1 = tokf("scr1")
        ke = mksig
        ytok = sdec
        vtok = tokb("vtok")
        gamh = tokb("gamh"); gaml = tokb("gaml")
        atil = tokb("atil"); btil = tokb("btil")
        ktil = tokb("ktil"); qtil = tokb("qtil")
        ygt = tokb("ygt")

        s8 = lambda n: sba(n, [128, NTT, 8], F32)
        mu8, var8, rstd8, rk8 = s8("mu8"), s8("var8"), s8("rstd8"), s8("rk8")
        dlt = sba("dlt", [128, 4, NSUB], F32)

        AFt = sba("AFt", [128, 4, NTT, 128])
        BFt = sba("BFt", [128, 4, NTT, 128])
        KFt = sba("KFt", [128, 4, NTT, 128])
        QFt = sba("QFt", [128, 4, NTT, 128])

        m8 = lambda n: sba(n, [128, 8, 64])
        Tsb, Ttsb, Usb, QBsb, QKsb, Nsb, Msb = map(
            m8, ["Tsb", "Ttsb", "Usb", "QBsb", "QKsb", "Nsb", "Msb"])
        Xsb = sba("Xsb", [128, 8, 128])
        Xlo = sba("Xlo", [128, 8, 128])
        Wh = sba("Wh", [128, 4, 64]); Mc = sba("Mc", [128, 4, 64])
        Zf = esA.enter_context(nc.sbuf_tensor("Zf", [128, 4, 64], F32))
        Zhi = sba("Zhi", [128, 4, 64]); Zlo = sba("Zlo", [128, 4, 64])
        vec(lambda v: v.memset(Zf[:], 0.0))
        vec(lambda v: v.memset(Zhi[:], 0.0))
        vec(lambda v: v.memset(Zlo[:], 0.0))
        z_evt = dv.evt()

        NBAR = 5
        cc_evts = []

        for ci in range(nchunks):
            bw = [(bar, NBAR * ci)]
            if ci == 0:
                vec(lambda v: v.memset(xf[:, :, 0:1], 0.0))
                dma("gpsimd", xf[:, :, 1:CK + 1],
                    d["xT"].rearrange("(c p) t -> p c t", p=128)[:, :, 0:CK],
                    waits=[inner_init])
            else:
                dma("gpsimd", xf[:, :, 0:CK + 1],
                    d["xT"].rearrange("(c p) t -> p c t", p=128)[
                        :, :, ci * CK - 1:ci * CK + CK], waits=bw)
            xr = sg.evt()

            vec(lambda v: v.tensor_tensor(xxt[:], xf[:, :, 0:CK],
                                          xf[:, :, 1:CK + 1], ALU.subtract),
                waits=[xr, consts_done, bc_done] + bw)
            for cc2 in range(8):
                vec(lambda v, cc2=cc2: v.scalar_tensor_tensor(
                    xmh[:, cc2, :], xxt[:, cc2, :], tmx[:, cc2:cc2 + 1],
                    xf[:, cc2, 1:CK + 1], ALU.mult, ALU.add), waits=[vload])
            mix_evt = dv.evt()

            blo = grab()
            for cc2 in range(8):
                mm(blo, blo.ap[:, 0:CK], w_maa1[:, cc2, :], xmh[:, cc2, :],
                   cc2 == 0, cc2 == 7,
                   waits=[mix_evt, wld] + bw if cc2 == 0 else [])
            penop()
            act(lambda s, blo=blo: s.activation(
                out=loA[:], in_=blo.ap[0:64, 0:CK], func=AF.Tanh),
                waits=[pe.evt()] + bw)
            act(lambda s, blo=blo: s.activation(
                out=loB[:], in_=blo.ap[64:128, 0:CK], func=AF.Tanh))
            blo.war = ac.evt()
            lo_evt = ac.evt()
            lo_done_pe = pe.evt()

            def emit_lo2_xm(gi, xmbuf, extra_wait):
                lot = loA if gi < 2 else loB
                ksl = slice((gi % 2) * 32, (gi % 2) * 32 + 32)
                for cc2 in range(8):
                    bb = grab()
                    wm2 = w_m2A if gi < 2 else w_m2B
                    mm(bb, bb.ap[:, 0:CK],
                       wm2[(gi % 2) * 32:(gi % 2) * 32 + 32,
                           cc2 * 128:(cc2 + 1) * 128],
                       lot[ksl, :], True, True, waits=[lo_evt])
                    penop()
                    w0 = [pe.evt()]
                    if cc2 == 0:
                        w0.append(lo_done_pe)
                        if extra_wait is not None:
                            w0.append(extra_wait)
                    act(lambda s, bb=bb, cc2=cc2, gi=gi, xmbuf=xmbuf:
                        s.activation(
                            out=xmbuf[:, cc2, :], in_=bb.ap[:, 0:CK],
                            func=AF.Identity, bias=tmg[:, cc2, gi:gi + 1]),
                        waits=w0)
                    bb.war = ac.evt()
                lo2_evt = ac.evt()
                pool(lambda g, xmbuf=xmbuf: g.tensor_tensor(
                    xmbuf[:], xxt[:], xmbuf[:], ALU.mult), waits=[lo2_evt])
                vec(lambda v, xmbuf=xmbuf: v.tensor_tensor(
                    xmbuf[:], xf[:, :, 1:CK + 1], xmbuf[:], ALU.add),
                    waits=[po.evt()])
                xm_f = dv.evt()
                act(lambda s, xmbuf=xmbuf: s.copy(xmh[:], xmbuf[:]),
                    waits=[xm_f] + ([extra_wait] if extra_wait else []))
                vec(lambda v, xmbuf=xmbuf: v.tensor_tensor(
                    xml[:], xmbuf[:], xmh[:], ALU.subtract),
                    waits=[ac.evt()])
                return dv.evt()

            def proj512(xmbuf, wtile, dst, e_in, wlo=None):
                for tt in range(NTT):
                    bp = grab()
                    for cc2 in range(8):
                        mm(bp, bp.ap[:, 0:512],
                           xmh[:, cc2, tt * 128:(tt + 1) * 128],
                           wtile[:, cc2, :], cc2 == 0, False,
                           waits=[e_in, wlo_done] if cc2 == 0 else [])
                        mm(bp, bp.ap[:, 0:512],
                           xml[:, cc2, tt * 128:(tt + 1) * 128],
                           wtile[:, cc2, :], False, False)
                        if wlo is not None:
                            mm(bp, bp.ap[:, 0:512],
                               xmh[:, cc2, tt * 128:(tt + 1) * 128],
                               wlo[:, cc2, :], False, cc2 == 7)
                        elif cc2 == 7:
                            mm(bp, bp.ap[:, 0:512],
                               xml[:, cc2, tt * 128:(tt + 1) * 128],
                               wtile[:, cc2, :], False, True, skip=True)
                    penop()
                    act(lambda s, bp=bp, tt=tt, dst=dst: s.copy(
                        dst[:, tt, :], bp.ap[:, 0:512]), waits=[pe.evt()])
                    bp.war = ac.evt()
                return ac.evt()

            e0 = emit_lo2_xm(0, xm1, lo_done_pe)
            r_evt = proj512(xm1, w_wr, rtok, e0, w_wrl)
            bg = grab()
            for cc2 in range(8):
                mm(bg, bg.ap[:, 0:CK], w_g1[:, cc2, :], xmh[:, cc2, :],
                   cc2 == 0, cc2 == 7, waits=[e0] if cc2 == 0 else [])
            penop()
            act(lambda s, bg=bg: s.activation(out=g_in[:], in_=bg.ap[:, 0:CK],
                                              func=AF.Tanh), waits=[pe.evt()])
            bg.war = ac.evt()
            gin_evt = ac.evt()
            for tt in range(NTT):
                bgo = grab()
                mm(bgo, bgo.ap[:, 0:512], g_in[:, tt * 128:(tt + 1) * 128],
                   w_g2[:], True, True, waits=[gin_evt])
                penop()
                act(lambda s, bgo=bgo, tt=tt: s.copy(gtok[:, tt, :],
                                                     bgo.ap[:, 0:512]),
                    waits=[pe.evt()])
                bgo.war = ac.evt()
            xm1_free = pe.evt()

            e1 = emit_lo2_xm(1, xm0, xm1_free)
            bd = grab()
            for cc2 in range(8):
                mm(bd, bd.ap[0:64, 0:CK], w_d1[:, cc2, :], xmh[:, cc2, :],
                   cc2 == 0, cc2 == 7, waits=[e1] if cc2 == 0 else [])
            penop()
            act(lambda s, bd=bd: s.activation(out=dec_f[:],
                                              in_=bd.ap[0:64, 0:CK],
                                              func=AF.Tanh), waits=[pe.evt()])
            bd.war = ac.evt()
            act(lambda s: s.copy(dec_in[0:64, :], dec_f[:]))
            vec(lambda v: v.tensor_tensor(dec_inl[0:64, :], dec_f[:],
                                          dec_in[0:64, :], ALU.subtract),
                waits=[ac.evt()])
            din_evt = dv.evt()
            bra = grab()
            for cc2 in range(8):
                mm(bra, bra.ap[0:48, 0:CK], w_ra1[:, cc2, :], xmh[:, cc2, :],
                   cc2 == 0, cc2 == 7, waits=[e1] if cc2 == 0 else [])
            penop()
            act(lambda s, bra=bra: s.copy(ra_in[0:48, :], bra.ap[0:48, 0:CK]),
                waits=[pe.evt()])
            bra.war = ac.evt()
            rain_evt = ac.evt()
            for tt in range(NTT):
                bdo = grab()
                mm(bdo, bdo.ap[:, 0:512], dec_in[:, tt * 128:(tt + 1) * 128],
                   w_d2[:], True, False, waits=[din_evt])
                mm(bdo, bdo.ap[:, 0:512], dec_inl[:, tt * 128:(tt + 1) * 128],
                   w_d2[:], False, True)
                penop()
                act(lambda s, bdo=bdo, tt=tt: s.activation(
                    out=sdec[:, tt, :], in_=bdo.ap[:, 0:512],
                    func=AF.Sigmoid), waits=[pe.evt()])
                bdo.war = ac.evt()
            for tt in range(NTT):
                ba1 = grab()
                mm(ba1, ba1.ap[:, 0:512], ra_in[:, tt * 128:(tt + 1) * 128],
                   w_ra2[:, 0:512], True, True, waits=[rain_evt])
                penop()
                act(lambda s, ba1=ba1, tt=tt: s.activation(
                    out=asig[:, tt, :], in_=ba1.ap[:, 0:512],
                    func=AF.Sigmoid), waits=[pe.evt()])
                ba1.war = ac.evt()
                ba2 = grab()
                mm(ba2, ba2.ap[:, 0:512], ra_in[:, tt * 128:(tt + 1) * 128],
                   w_ra2[:, 512:1024], True, True)
                penop()
                act(lambda s, ba2=ba2, tt=tt: s.activation(
                    out=masig[:, tt, :], in_=ba2.ap[:, 0:512],
                    func=AF.Sigmoid), waits=[pe.evt()])
                ba2.war = ac.evt()
            sdec_evt = ac.evt()
            xm0_free = pe.evt()

            vec(lambda v: v.tensor_scalar(gamh[:], sdec[:], -EXPM05, None,
                                          ALU.mult), waits=[sdec_evt])
            vec(lambda v: v.scalar_tensor_tensor(
                gaml[:], sdec[:], -EXPM05, gamh[:], ALU.mult, ALU.subtract))
            gam_evt = dv.evt()

            e2 = emit_lo2_xm(2, xm1, xm0_free)
            k_evt = proj512(xm1, w_wk, ktmp, e2, w_wkl)
            brb = grab()
            for cc2 in range(8):
                mm(brb, brb.ap[0:48, 0:CK], w_rb1[:, cc2, :], xmh[:, cc2, :],
                   cc2 == 0, cc2 == 7, waits=[e2] if cc2 == 0 else [])
            penop()
            act(lambda s, brb=brb: s.activation(out=rb_in[0:16, :],
                                                in_=brb.ap[0:16, 0:CK],
                                                func=AF.Tanh),
                waits=[pe.evt()])
            act(lambda s, brb=brb: s.copy(rb_in[32:48, :],
                                          brb.ap[32:48, 0:CK]))
            brb.war = ac.evt()
            rbin_evt = ac.evt()
            for tt in range(NTT):
                bb1 = grab()
                mm(bb1, bb1.ap[:, 0:512], rb_in[:, tt * 128:(tt + 1) * 128],
                   w_rb2[:, 0:512], True, True, waits=[rbin_evt])
                penop()
                act(lambda s, bb1=bb1, tt=tt: s.copy(scr1[:, tt, :],
                                                     bb1.ap[:, 0:512]),
                    waits=[pe.evt()])
                bb1.war = ac.evt()
                bb2 = grab()
                mm(bb2, bb2.ap[:, 0:512], rb_in[:, tt * 128:(tt + 1) * 128],
                   w_rb2[:, 512:1024], True, True)
                penop()
                act(lambda s, bb2=bb2, tt=tt: s.activation(
                    out=mksig[:, tt, :], in_=bb2.ap[:, 0:512],
                    func=AF.Sigmoid), waits=[pe.evt()])
                bb2.war = ac.evt()
            kkk_evt = ac.evt()
            xm1_free = pe.evt()

            e3 = emit_lo2_xm(3, xm0, xm1_free)
            v_evt = proj512(xm0, w_wv, vtok, e3, w_wvl)
            xm0_free = pe.evt()

            vec(lambda v: v.tensor_tensor(kkn[:], ktmp[:], scr1[:], ALU.add),
                waits=[k_evt, kkk_evt])
            vec(lambda v: v.tensor_tensor(scr1[:], kkn[:], kkn[:], ALU.mult))
            vec(lambda v: v.tensor_reduce(
                rk8[:], scr1[:].rearrange("p a (h s) -> p a h s", s=64),
                AX.X, ALU.add))
            vec(lambda v: v.tensor_scalar(rk8[:], rk8[:], 1e-24, None,
                                          ALU.max))
            act(lambda s: s.activation(out=rk8[:], in_=rk8[:], func=AF.Ln),
                waits=[dv.evt()])
            act(lambda s: s.activation(out=rk8[:], in_=rk8[:], func=AF.Exp,
                                       scale=CNH))
            vec(lambda v: v.tensor_tensor(
                kkn[:].rearrange("p a (h s) -> p a h s", s=64),
                kkn[:].rearrange("p a (h s) -> p a h s", s=64),
                rk8[:].rearrange("p a (h o) -> p a h o", o=1).to_broadcast(
                    (128, NTT, 8, 64)), ALU.mult), waits=[ac.evt()])
            kkn_evt = dv.evt()
            act(lambda s: s.activation(out=sdec[:], in_=sdec[:], func=AF.Ln),
                waits=[gam_evt])
            lns_evt = ac.evt()
            vec(lambda v: v.scalar_tensor_tensor(
                scr1[:], sdec[:], -0.5, mksig[:], ALU.add, ALU.mult),
                waits=[lns_evt, kkn_evt])
            act(lambda s: s.activation(out=ke[:], in_=scr1[:], func=AF.Exp),
                waits=[dv.evt()])
            ke_evt = ac.evt()
            vec(lambda v: v.tensor_scalar(scr1[:], masig[:], -1.0, 1.0,
                                          ALU.mult, ALU.add),
                waits=[ke_evt])
            pool(lambda g: g.tensor_tensor(scr1[:], asig[:], scr1[:],
                                           ALU.mult), waits=[dv.evt()])
            vec(lambda v: v.tensor_tensor(scr1[:], masig[:], scr1[:],
                                          ALU.add), waits=[po.evt()])
            pool(lambda g: g.tensor_tensor(kfin[:], ktmp[:], scr1[:],
                                           ALU.mult), waits=[dv.evt()])
            vec(lambda v: v.tensor_tensor(kfin[:], kfin[:], ke[:], ALU.mult),
                waits=[po.evt(), ke_evt])
            kfin_evt = dv.evt()

            for tt in range(NTT):
                bG = grab()
                for hf in range(2):
                    rb0 = hf * 64
                    rsl0 = slice(rb0, rb0 + 64)
                    mm(bG, bG.ap[rsl0, 0:512], tri[rsl0, :],
                       gamh[rsl0, tt, :], True, False, tp=(rb0, rb0),
                       waits=[gam_evt] if hf == 0 else [])
                    mm(bG, bG.ap[rsl0, 0:512], tri[rsl0, :],
                       gaml[rsl0, tt, :], False, True, tp=(rb0, rb0))
                penop()
                gG = pe.evt()
                act(lambda s, bG=bG, tt=tt: s.activation(
                    out=ept[:, tt, :], in_=bG.ap[:, 0:512], func=AF.Exp),
                    waits=[gG])
                act(lambda s, bG=bG, tt=tt: s.activation(
                    out=emt[:, tt, :], in_=bG.ap[:, 0:512], func=AF.Exp,
                    scale=CN1))
                vec(lambda v, bG=bG, tt=tt: v.scalar_tensor_tensor(
                    ytok[:, tt, :], gamh[:, tt, :], -1.0, bG.ap[:, 0:512],
                    ALU.mult, ALU.add), waits=[gG])
                vec(lambda v, tt=tt: v.tensor_tensor(
                    ytok[:, tt, :], ytok[:, tt, :], gaml[:, tt, :],
                    ALU.subtract))
                act(lambda s, tt=tt: s.activation(
                    out=epvt[:, tt, :], in_=ytok[:, tt, :], func=AF.Exp),
                    waits=[dv.evt()])
                bG.war = ac.evt()
            ef_evt = ac.evt()

            bDL = grab()
            for p in range(NPAIR):
                csl = slice(p * 128, (p + 1) * 128)
                for tt in range(NTT):
                    mm(bDL, bDL.ap[:, p * NSUB:(p + 1) * NSUB],
                       gamh[:, tt, csl], sel[:, tt, :], tt == 0, False,
                       waits=[gam_evt, consts_done]
                       if (p == 0 and tt == 0) else [])
                    mm(bDL, bDL.ap[:, p * NSUB:(p + 1) * NSUB],
                       gaml[:, tt, csl], sel[:, tt, :], False, tt == NTT - 1)
            penop()
            act(lambda s, bDL=bDL: s.activation(
                out=dlt[:].rearrange("p a e -> p (a e)"),
                in_=bDL.ap[:, 0:4 * NSUB], func=AF.Exp), waits=[pe.evt()])
            bDL.war = ac.evt()
            dl_evt = ac.evt()

            vec(lambda v: v.tensor_tensor(atil[:], kkn[:], epvt[:], ALU.mult),
                waits=[ef_evt, kkn_evt])
            vec(lambda v: v.scalar_tensor_tensor(
                scr1[:], kkn[:], -1.0, asig[:], ALU.mult, ALU.mult),
                waits=[kfin_evt])
            pool(lambda g: g.tensor_tensor(btil[:], scr1[:], emt[:],
                                           ALU.mult),
                 waits=[ef_evt, dv.evt(), po.evt()])
            btil_evt = po.evt()
            vec(lambda v: v.tensor_tensor(ktil[:], kfin[:], emt[:],
                                          ALU.mult), waits=[kfin_evt])
            pool(lambda g: g.tensor_tensor(qtil[:], rtok[:], ept[:],
                                           ALU.mult), waits=[r_evt, ef_evt])
            sc_dv = dv.evt()
            sc_po = po.evt()

            for (src, dst) in ((atil, AFt), (btil, BFt), (ktil, KFt),
                               (qtil, QFt)):
                for tt in range(NTT):
                    for p in range(4):
                        dma("sync", dst[:, p, tt, :],
                            src[:, tt, p * 128:(p + 1) * 128],
                            waits=[sc_dv, sc_po], transpose=True)
            tr_evt = st.evt()

            z_evt_g = [z_evt]
            b7hi_free = [None]
            y_free = [None]

            def make_sub(s_i):
                tt = s_i // 2
                rb0 = (s_i % 2) * 64
                rsl = slice(rb0, rb0 + 64)
                bX = [banks[5], banks[6]]
                st = {"nsrc": Tsb, "msrc": Ttsb}

                def mbc(mk):
                    return mk[rsl, :].unsqueeze(1).to_broadcast((64, 8, 64))

                def pv(bb):
                    return bb.ap[rsl, 0:512].rearrange("p (a s) -> p a s",
                                                       s=64)

                def evac_X(w, lo=False):
                    vec(lambda v: v.tensor_copy(
                        Xsb[rsl, 0:4, :].rearrange("p a x -> p (a x)"),
                        bX[0].ap[rsl, 0:512]), waits=w)
                    act(lambda s: s.copy(
                        Xsb[rsl, 4:8, :].rearrange("p a x -> p (a x)"),
                        bX[1].ap[rsl, 0:512]), waits=w)
                    if lo:
                        hi_done = ac.evt()
                        vec(lambda v: v.tensor_tensor(
                            Xlo[rsl, 0:4, :].rearrange("p a x -> p (a x)"),
                            bX[0].ap[rsl, 0:512],
                            Xsb[rsl, 0:4, :].rearrange("p a x -> p (a x)"),
                            ALU.subtract))
                        vec(lambda v: v.tensor_tensor(
                            Xlo[rsl, 4:8, :].rearrange("p a x -> p (a x)"),
                            bX[1].ap[rsl, 0:512],
                            Xsb[rsl, 4:8, :].rearrange("p a x -> p (a x)"),
                            ALU.subtract), waits=[hi_done])
                    return [dv.evt(), ac.evt()]

                def stage_g(extra_w=()):
                    bT, bTt, bU, bQB, bQK = (banks[0], banks[1], banks[2],
                                             banks[3], banks[4])
                    for p in range(NPAIR):
                        for h2 in range(2):
                            i8 = p * 2 + h2
                            hsl = slice(h2 * 64, h2 * 64 + 64)
                            col = slice(i8 * 64, i8 * 64 + 64)
                            AFs = AFt[hsl, p, tt, rb0:rb0 + 64]
                            BFs = BFt[hsl, p, tt, rb0:rb0 + 64]
                            KFs = KFt[hsl, p, tt, rb0:rb0 + 64]
                            QFs = QFt[hsl, p, tt, rb0:rb0 + 64]
                            w0 = [tr_evt] + list(extra_w) if i8 == 0 else []
                            mm(bT, bT.ap[rsl, col], AFs, BFs, True, True,
                               tp=(h2 * 64, rb0), waits=w0)
                            mm(bTt, bTt.ap[rsl, col], BFs, AFs, True, True,
                               tp=(h2 * 64, rb0))
                            mm(bU, bU.ap[rsl, col], KFs, AFs, True, True,
                               tp=(h2 * 64, rb0))
                            mm(bQB, bQB.ap[rsl, col], BFs, QFs, True, True,
                               tp=(h2 * 64, rb0))
                            mm(bQK, bQK.ap[rsl, col], KFs, QFs, True, True,
                               tp=(h2 * 64, rb0))
                    penop()
                    gram_evt = pe.evt()
                    vec(lambda v: v.tensor_tensor(
                        Tsb[rsl, :, :], pv(bT), mbc(maskS), ALU.mult),
                        waits=[gram_evt])
                    vec(lambda v: v.tensor_tensor(
                        Ttsb[rsl, :, :], pv(bTt), mbc(maskSU), ALU.mult))
                    vec(lambda v: v.tensor_tensor(
                        Usb[rsl, :, :], pv(bU), mbc(maskSU), ALU.mult))
                    vec(lambda v: v.tensor_tensor(
                        QBsb[rsl, :, :], pv(bQB), mbc(maskIU), ALU.mult))
                    vec(lambda v: v.tensor_tensor(
                        QKsb[rsl, :, :], pv(bQK), mbc(maskIU), ALU.mult))
                    st["gram_sb"] = dv.evt()
                    st["pw"] = [st["gram_sb"]]

                def stage_x0():
                    for bXp in bX:
                        mm(bXp, bXp.ap[rsl, 0:512], z1[0:1, 0:64],
                           zr[0:1, :], True, True, tp=(0, rb0),
                           waits=[st["gram_sb"], v_evt])
                    for p in range(NPAIR):
                        bXp = bX[p // 2]
                        xo = (p % 2) * 256
                        for h2 in range(2):
                            i8 = p * 2 + h2
                            base = p * 128 + h2 * 64
                            mm(None, bXp.ap[rsl, xo + h2 * 128:
                                            xo + h2 * 128 + 64],
                               Usb[rsl, i8, :], vtok[rsl, tt, base:base + 64],
                               False, False, tp=(rb0, rb0), skip=True)
                            mm(None, bXp.ap[rsl, xo + h2 * 128 + 64:
                                            xo + h2 * 128 + 128],
                               ident[rsl, :], atil[rsl, tt, base:base + 64],
                               False, False, tp=(rb0, rb0), skip=True)
                    penop()
                    st["xe"] = evac_X([pe.evt()])

                def stage_lvl(lvl):
                    nsrc, msrc = st["nsrc"], st["msrc"]
                    for p in range(NPAIR):
                        bXp = bX[p // 2]
                        xo = (p % 2) * 256
                        for h2 in range(2):
                            i8 = p * 2 + h2
                            mm(None, bXp.ap[rsl, xo + h2 * 128:
                                            xo + h2 * 128 + 128],
                               msrc[rsl, i8, :], Xsb[rsl, i8, :],
                               False, False, tp=(rb0, rb0),
                               waits=st["xe"] + st["pw"]
                               if i8 == 0 and p == 0 else [],
                               skip=True)
                    penop()
                    ap_evt = pe.evt()
                    if lvl < 5:
                        bN, bM = banks[0], banks[1]
                        for i8 in range(8):
                            col = slice(i8 * 64, i8 * 64 + 64)
                            mm(bN, bN.ap[rsl, col], msrc[rsl, i8, :],
                               nsrc[rsl, i8, :], True, True, tp=(rb0, rb0),
                               waits=st["pw"] if i8 == 0 else [])
                            mm(bM, bM.ap[rsl, col], nsrc[rsl, i8, :],
                               msrc[rsl, i8, :], True, True, tp=(rb0, rb0))
                        penop()
                        sq_evt = pe.evt()
                        vec(lambda v: v.tensor_copy(Nsb[rsl, :, :], pv(bN)),
                            waits=[sq_evt])
                        act(lambda s: s.copy(Msb[rsl, :, :], pv(bM)),
                            waits=[sq_evt])
                        st["pw"] = [dv.evt(), ac.evt()]
                        st["nsrc"], st["msrc"] = Nsb, Msb
                    st["xe"] = evac_X([ap_evt], lo=(lvl == 5))

                def stage_wy():
                    b7 = banks[7]
                    for p in range(NPAIR):
                        for h2 in range(2):
                            i8 = p * 2 + h2
                            hsl = slice(h2 * 64, h2 * 64 + 64)
                            w0 = (st["xe"] + [st["gram_sb"], z_evt_g[0]]
                                  + ([b7hi_free[0]] if b7hi_free[0] else [])
                                  if i8 == 0 else [])
                            mm(b7, b7.ap[hsl, 256 + p * 64:
                                           256 + (p + 1) * 64],
                               Xsb[rsl, i8, 64:128], QBsb[rsl, i8, :],
                               True, True, tp=(rb0, h2 * 64), waits=w0)
                    penop()
                    vec(lambda v: v.tensor_tensor(
                        Wh[:], banks[7].ap[:, 256:512].rearrange(
                            "p (a s) -> p a s", s=64),
                        QFt[:, :, tt, rb0:rb0 + 64], ALU.add),
                        waits=[pe.evt()])
                    st["wh"] = dv.evt()
                    bY = banks[3]
                    mm(bY, bY.ap[rsl, 0:512], z1[0:1, 0:64], zr[0:1, :],
                       True, True, tp=(0, rb0),
                       waits=st["xe"] + ([y_free[0]] if y_free[0] else []))
                    for p in range(NPAIR):
                        for h2 in range(2):
                            i8 = p * 2 + h2
                            base = p * 128 + h2 * 64
                            ycol = slice(base, base + 64)
                            mm(None, bY.ap[rsl, ycol], QBsb[rsl, i8, :],
                               Xsb[rsl, i8, 0:64], False, False,
                               tp=(rb0, rb0), skip=True)
                            mm(None, bY.ap[rsl, ycol], QBsb[rsl, i8, :],
                               Xlo[rsl, i8, 0:64], False, False,
                               tp=(rb0, rb0), skip=True)
                            mm(None, bY.ap[rsl, ycol], QKsb[rsl, i8, :],
                               vtok[rsl, tt, base:base + 64], False, False,
                               tp=(rb0, rb0), skip=True)
                    for p in range(NPAIR):
                        for h2 in range(2):
                            i8 = p * 2 + h2
                            hsl = slice(h2 * 64, h2 * 64 + 64)
                            base = p * 128 + h2 * 64
                            mm(None, bY.ap[rsl, base:base + 64], Wh[hsl, p, :],
                               Zhi[hsl, p, :], False, False,
                               tp=(h2 * 64, rb0),
                               waits=[st["wh"], z_evt_g[0]] if i8 == 0
                               else [], skip=True)
                    penop()
                    act(lambda s: s.copy(ytok[rsl, tt, :],
                                         banks[3].ap[rsl, 0:512]),
                        waits=[pe.evt()])
                    y_free[0] = ac.evt()

                def stage_pz():
                    b7 = banks[7]
                    for p in range(NPAIR):
                        for h2 in range(2):
                            i8 = p * 2 + h2
                            hsl = slice(h2 * 64, h2 * 64 + 64)
                            mm(b7, b7.ap[hsl, 256 + p * 64:
                                           256 + (p + 1) * 64],
                               Xsb[rsl, i8, 64:128],
                               btil[rsl, tt, base_of(p, h2)], True, True,
                               tp=(rb0, h2 * 64),
                               waits=st["xe"] + [st["wh"]] if i8 == 0 else [])
                    penop()
                    vec(lambda v: v.tensor_tensor(
                        Mc[:], b7.ap[:, 256:512].rearrange(
                            "p (a s) -> p a s", s=64),
                        ident[:].unsqueeze(1).to_broadcast((128, 4, 64)),
                        ALU.add), waits=[pe.evt()])
                    mc_evt = dv.evt()
                    b7hi_free[0] = mc_evt
                    mm(b7, b7.ap[:, 0:256], z1[:], zr[0:1, 0:256],
                       True, True, waits=st["xe"] + [mc_evt, z_evt_g[0]])
                    for p in range(NPAIR):
                        for h2 in range(2):
                            i8 = p * 2 + h2
                            hsl = slice(h2 * 64, h2 * 64 + 64)
                            zcol = slice(p * 64, (p + 1) * 64)
                            base = p * 128 + h2 * 64
                            mm(None, b7.ap[hsl, zcol],
                               btil[rsl, tt, base:base + 64],
                               Xsb[rsl, i8, 0:64],
                               False, False, tp=(rb0, h2 * 64), skip=True)
                            mm(None, b7.ap[hsl, zcol],
                               btil[rsl, tt, base:base + 64],
                               Xlo[rsl, i8, 0:64],
                               False, False, tp=(rb0, h2 * 64), skip=True)
                            mm(None, b7.ap[hsl, zcol],
                               ktil[rsl, tt, base:base + 64],
                               vtok[rsl, tt, base:base + 64], False, False,
                               tp=(rb0, h2 * 64), skip=True)
                            mm(None, b7.ap[hsl, zcol], Mc[hsl, p, :],
                               Zhi[hsl, p, :], False, False,
                               tp=(h2 * 64, h2 * 64), skip=True)
                            mm(None, b7.ap[hsl, zcol], Mc[hsl, p, :],
                               Zlo[hsl, p, :], False, False,
                               tp=(h2 * 64, h2 * 64), skip=True)
                    penop()
                    zup_evt = pe.evt()
                    vec(lambda v: v.tensor_tensor(
                        Zf[:], b7.ap[:, 0:256].rearrange("p (a s) -> p a s",
                                                         s=64),
                        dlt[:, :, s_i:s_i + 1].to_broadcast((128, 4, 64)),
                        ALU.mult), waits=[zup_evt, dl_evt])
                    vec(lambda v: v.tensor_copy(Zhi[:], Zf[:]))
                    vec(lambda v: v.tensor_tensor(Zlo[:], Zf[:], Zhi[:],
                                                  ALU.subtract))
                    z_evt_g[0] = dv.evt()

                def base_of(p, h2):
                    base = p * 128 + h2 * 64
                    return slice(base, base + 64)

                st["g"], st["x0"], st["lvl"] = stage_g, stage_x0, stage_lvl
                st["wy"], st["pz"] = stage_wy, stage_pz
                return st

            for pk in range(0, NSUB, 2):
                sa = make_sub(pk)
                sb_ = make_sub(pk + 1)
                pair_w = [dv.evt(), ac.evt()] if pk else []
                sa["g"](pair_w)
                sb_["g"]()
                sa["x0"]()
                sb_["x0"]()
                for lvl in range(6):
                    sa["lvl"](lvl)
                    sb_["lvl"](lvl)
                sa["wy"]()
                sa["pz"]()
                sb_["wy"]()
                sb_["pz"]()

            z_evt = z_evt_g[0]
            y_all = ac.evt()

            y4 = ytok[:].rearrange("p a (h s) -> p a h s", s=64)

            def bch(t_):
                return t_[:].rearrange("p a (h o) -> p a h o",
                                       o=1).to_broadcast((128, NTT, 8, 64))

            vec(lambda v: v.tensor_reduce(mu8[:], y4, AX.X, ALU.add),
                waits=[y_all])
            vec(lambda v: v.tensor_scalar(mu8[:], mu8[:], 1.0 / HS, None,
                                          ALU.mult))
            pool(lambda g: g.tensor_tensor(scr1[:], ytok[:], ytok[:],
                                           ALU.mult), waits=[y_all, dv.evt()])
            vec(lambda v: v.tensor_reduce(
                var8[:], scr1[:].rearrange("p a (h s) -> p a h s", s=64),
                AX.X, ALU.add), waits=[po.evt()])
            vec(lambda v: v.tensor_tensor(rstd8[:], mu8[:], mu8[:], ALU.mult))
            vec(lambda v: v.scalar_tensor_tensor(
                var8[:], var8[:], 1.0 / HS, rstd8[:], ALU.mult, ALU.subtract))
            act(lambda s: s.activation(out=rstd8[:], in_=var8[:], func=AF.Ln,
                                       bias=CEG), waits=[dv.evt()])
            act(lambda s: s.activation(out=rstd8[:], in_=rstd8[:],
                                       func=AF.Exp, scale=CNH))
            rstd_evt = ac.evt()
            vec(lambda v: v.tensor_tensor(y4, y4, bch(mu8), ALU.subtract),
                waits=[rstd_evt])
            vec(lambda v: v.tensor_tensor(y4, y4, bch(rstd8), ALU.mult))
            vec(lambda v: v.tensor_tensor(
                ytok[:], ytok[:],
                LNW_BC.unsqueeze(1).to_broadcast((128, NTT, CH)), ALU.mult),
                waits=[bc_done])
            vec(lambda v: v.tensor_tensor(
                ytok[:], ytok[:],
                LNB_BC.unsqueeze(1).to_broadcast((128, NTT, CH)), ALU.add))
            pool(lambda g: g.tensor_tensor(
                scr1[:], kfin[:],
                FAA_BC.unsqueeze(1).to_broadcast((128, NTT, CH)), ALU.mult),
                waits=[kfin_evt, bc_done, dv.evt()])
            pool(lambda g: g.tensor_tensor(scr1[:], rtok[:], scr1[:],
                                           ALU.mult))
            vec(lambda v: v.tensor_reduce(
                rk8[:], scr1[:].rearrange("p a (h s) -> p a h s", s=64),
                AX.X, ALU.add), waits=[po.evt()])
            pool(lambda g: g.tensor_tensor(
                scr1[:].rearrange("p a (h s) -> p a h s", s=64),
                vtok[:].rearrange("p a (h s) -> p a h s", s=64),
                bch(rk8), ALU.mult), waits=[dv.evt()])
            vec(lambda v: v.tensor_tensor(ytok[:], ytok[:], scr1[:],
                                          ALU.add), waits=[po.evt()])
            vec(lambda v: v.tensor_tensor(ygt[:], ytok[:], gtok[:],
                                          ALU.mult))
            yg_evt = dv.evt()

            ygf = AFt
            for tt in range(NTT):
                for p in range(4):
                    dma("sync", ygf[:, p, tt, :],
                        ygt[:, tt, p * 128:(p + 1) * 128],
                        waits=[yg_evt], transpose=True)
            ygf_evt = st.evt()
            prev_wo_store = None
            for tt in range(NTT):
                for eh in range(2):
                    bo = grab()
                    for p in range(4):
                        mm(bo, bo.ap[:, 0:512], ygf[:, p, tt, :],
                           w_wo[:, p, eh * 512:(eh + 1) * 512], p == 0,
                           p == 3, waits=[ygf_evt] if p == 0 else [])
                    penop()
                    act(lambda s, bo=bo, tt=tt, eh=eh: s.copy(
                        scr1[:, tt, :], bo.ap[:, 0:512]),
                        waits=[pe.evt(), prev_wo_store] if eh == 1
                        else [pe.evt()])
                    bo.war = ac.evt()
                    row0 = (ci % 8) * 512 + (ci // 8) * 256 + tt * 128
                    prev_wo_store = dma(
                        "sync", opart[row0:row0 + 128,
                                      eh * 512:(eh + 1) * 512],
                        scr1[:, tt, :], waits=[ac.evt()])
            wo_evt = sl.evt()

            if nchunks == NCK and ci >= NCK - 8:
                j = ci - (NCK - 8)
                P.do("gpsimd", lambda g, j=j: g.collective_compute(
                    "ReduceScatter", ALU.add,
                    replica_groups=[[0, 1], [2, 3], [4, 5], [6, 7]],
                    ins=[opart[j * 512:(j + 1) * 512, :]],
                    outs=[osum[j * 256:(j + 1) * 256, :]]),
                    [wo_evt, sg.evt()], cc)
                cc_evts.append(cc.evt())

            sl_c, st_c, sg_c, po_c = sl.evt(), st.evt(), sg.evt(), po.evt()
            dv_c, ac_c = dv.evt(), ac.evt()
            P.do("tensor", lambda t: t.nop(), [dv_c, ac_c, pe.evt()], bar)
            P.do("vector", lambda v: v.engine_nop(), [ac_c], bar)
            P.do("scalar", lambda s: s.nop(), [dv_c], bar)
            P.do("gpsimd", lambda g: g.engine_nop(), [sg_c], bar)
            P.do("sync", lambda s: s.nop(), [sl_c, st_c, po_c], bar)

            if sub == 8:
                break

        opart_done = sl.evt()
        attn_bar = bar.evt()
        esA.close()

        if phases == "A":
            P.do("vector", lambda v: v.engine_nop(),
                 [opart_done, attn_bar, st.evt(), sg.evt(), ac.evt(),
                  po.evt(), pe.evt(), sw_done], dv)
            P.finalize()
            return nc

        esB = es.enter_context(contextlib.ExitStack())
        esB1 = es.enter_context(contextlib.ExitStack())
        esC = es.enter_context(contextlib.ExitStack())

        def sbb(name, shape, dt=F32):
            return esB.enter_context(nc.sbuf_tensor(name, shape, dt))

        def sbb1(name, shape, dt=F32):
            return esB1.enter_context(nc.sbuf_tensor(name, shape, dt))

        def sbc(name, shape, dt=F32):
            return esC.enter_context(nc.sbuf_tensor(name, shape, dt))

        if not cc_evts:
            P.do("gpsimd", lambda g: g.collective_compute(
                "ReduceScatter", ALU.add,
                replica_groups=[[0, 1], [2, 3], [4, 5], [6, 7]],
                ins=[opart[:, :]], outs=[osum[:, :]]),
                [opart_done, attn_bar, sg.evt()], cc)
            cc_evts = [cc.evt()] * 8

        sx_r = sbb("sx_r", [128, 16])
        hq_s = sbb("hq_s", [128, 16, 2])
        hmax = sbb("hmax", [128, 16])
        f_mu = sbb("f_mu", [128, 4])
        f_sg = sbb("f_sg", [128, 4])
        f_t1 = sbb("f_t1", [128, 4])
        fscr = sbb("fscr", [128, 4, C])
        xqT = sbc("xqT", [128, 8, TH], BF16)
        x1t = sbb1("x1t", [128, 4, C])
        xr1 = sbb1("xr1", [128, 4, C])
        xq = sbb1("xq", [128, 4, C], BF16)

        for fi4 in range(4):
            tsl = slice(fi4 * 512, (fi4 + 1) * 512)
            dma("sync", x1t[:], osum[tsl, :].rearrange("(a p) c -> p a c",
                                                       p=128),
                waits=[cc_evts[2 * fi4 + 1], attn_bar, dv.evt()])
            dma("sync", xr1[:], d["xtm"][tsl, :].rearrange(
                "(a p) c -> p a c", p=128))
            ld_evt = sl.evt()
            vec(lambda v: v.tensor_tensor(x1t[:], x1t[:], xr1[:], ALU.add),
                waits=[ld_evt, st.evt()])
            vec(lambda v: v.tensor_reduce(f_mu[:], x1t[:], AX.X, ALU.add))
            vec(lambda v: v.tensor_scalar(f_mu[:], f_mu[:], 1.0 / C, None,
                                          ALU.mult))
            pool(lambda g: g.tensor_tensor(fscr[:], x1t[:], x1t[:], ALU.mult),
                 waits=[dv.evt()])
            vec(lambda v: v.tensor_reduce(f_sg[:], fscr[:], AX.X, ALU.add),
                waits=[po.evt()])
            vec(lambda v: v.tensor_tensor(f_t1[:], f_mu[:], f_mu[:],
                                          ALU.mult))
            vec(lambda v: v.scalar_tensor_tensor(
                f_sg[:], f_sg[:], 1.0 / C, f_t1[:], ALU.mult, ALU.subtract))
            act(lambda s: s.activation(out=f_sg[:], in_=f_sg[:], func=AF.Ln,
                                       bias=CEL), waits=[dv.evt()])
            act(lambda s: s.activation(out=f_sg[:], in_=f_sg[:], func=AF.Exp,
                                       scale=CNH))
            for a_ in range(4):
                vec(lambda v, a_=a_: v.scalar_tensor_tensor(
                    x1t[:, a_, :], x1t[:, a_, :], f_mu[:, a_:a_ + 1],
                    f_sg[:, a_:a_ + 1].to_broadcast((128, C)),
                    ALU.subtract, ALU.mult), waits=[ac.evt()])
            x1_evt = dv.evt()
            x1store = dma("sync",
                          x1sp[tsl, :].rearrange("(a p) c -> p a c", p=128),
                          x1t[:], waits=[x1_evt])
            vec(lambda v: v.tensor_reduce(f_mu[:], x1t[:], AX.X, ALU.max,
                                          apply_absolute_value=True))
            vec(lambda v: v.tensor_scalar(f_mu[:], f_mu[:], 1e-5,
                                          1.0 / 127.0, ALU.max, ALU.mult))
            vec(lambda v, fi4=fi4: v.tensor_copy(
                sx_r[:, fi4 * 4:(fi4 + 1) * 4], f_mu[:]))
            vec(lambda v: v.reciprocal(f_sg[:], f_mu[:]))
            for a_ in range(4):
                vec(lambda v, a_=a_: v.tensor_scalar(
                    x1t[:, a_, :], x1t[:, a_, :], f_sg[:, a_:a_ + 1], MAGIC,
                    ALU.mult, ALU.add), waits=[x1store] if a_ == 0 else [])
            vec(lambda v: v.tensor_scalar(xq[:], x1t[:], MAGIC, None,
                                          ALU.subtract))
            xq_evt = dv.evt()
            for a_ in range(4):
                for cc2 in range(8):
                    dma("sync", xqT[:, cc2, fi4 * 512 + a_ * 128:
                                    fi4 * 512 + (a_ + 1) * 128],
                        xq[:, a_, cc2 * 128:(cc2 + 1) * 128],
                        waits=[xq_evt], sem=st, transpose=True)
        xqT_evt = st.evt()
        sx_done = dv.evt()
        esB1.close()

        if phases == "AB":
            P.do("vector", lambda v: v.engine_nop(),
                 [xqT_evt, sx_done, sl.evt(), sg.evt(), ac.evt(), po.evt(),
                  pe.evt(), sw_done], dv)
            P.finalize()
            return nc

        wg_f = sbc("wg_f", [128, 8, 512])
        wu_f = sbc("wu_f", [128, 8, 512])
        wg_t = sbc("wg_t", [128, 2, 8, 512], BF16)
        wu_t = sbc("wu_t", [128, 2, 8, 512], BF16)
        tscr = sbc("tscr", [128, 8, 512])
        hblk = sbc("hblk", [128, 512])
        m1c = sbc("m1c", [128, 4])
        sigt = sbc("sigt", [128, 512])
        m1t = sbc("m1t", [128, 512])
        vec(lambda v: v.memset(hmax[:], 0.0))

        def ternarize(dst, srcf, wi, w_extra):
            vec(lambda v: v.tensor_scalar(
                tscr[:], srcf[:], swcol[:, wi:wi + 1], MAGIC, ALU.mult,
                ALU.add), waits=[sw_done] + [w for w in w_extra if w])
            rd = dv.evt()
            vec(lambda v: v.tensor_scalar(
                tscr[:], tscr[:], MAGIC, 1.0, ALU.subtract, ALU.min))
            vec(lambda v: v.tensor_scalar(dst, tscr[:], -1.0, None,
                                          ALU.max))
            return rd, dv.evt()

        prev_rd_g = None
        prev_rd_u = None
        tern_done = [None, None]
        buf_pe2 = [None, None]
        prev_hb_store = None
        for si in range(NSLAB + 1):
            if si < NSLAB:
                sb = si % 2
                isl = slice(si * 512, (si + 1) * 512)
                dma("sync", wg_f[:],
                    d["wgT"].rearrange("(c p) i -> p c i", p=128)[:, :, isl],
                    waits=[xqT_evt] if si == 0 else [prev_rd_g])
                lg = sl.evt()
                prev_rd_g, _ = ternarize(wg_t[:, sb, :, :], wg_f, 0,
                                         [lg, buf_pe2[sb]])
                dma("sync", wu_f[:],
                    d["wuT"].rearrange("(c p) i -> p c i", p=128)[:, :, isl],
                    waits=[prev_rd_u] if prev_rd_u else [])
                lu = sl.evt()
                prev_rd_u, tern_done[sb] = ternarize(wu_t[:, sb, :, :], wu_f,
                                                     1, [lu])
            if si == 0:
                continue
            sj = si - 1
            sb2 = sj % 2
            isl2 = slice(sj * 512, (sj + 1) * 512)
            for fi in range(FTT):
                bgp = grab(); bup = grab()
                for cc2 in range(8):
                    mm(bgp, bgp.ap[:, 0:512],
                       xqT[:, cc2, fi * 128:(fi + 1) * 128],
                       wg_t[:, sb2, cc2, :], cc2 == 0, cc2 == 7,
                       waits=[tern_done[sb2], xqT_evt] if cc2 == 0 else [])
                for cc2 in range(8):
                    mm(bup, bup.ap[:, 0:512],
                       xqT[:, cc2, fi * 128:(fi + 1) * 128],
                       wu_t[:, sb2, cc2, :], cc2 == 0, cc2 == 7,
                       waits=[tern_done[sb2]] if cc2 == 0 else [])
                penop()
                gu_evt = pe.evt()
                vec(lambda v, fi=fi: v.tensor_tensor(
                    m1c[:, 0:1], sx_r[:, fi:fi + 1], swcol[:, 4:5], ALU.mult),
                    waits=[gu_evt, sx_done, sw_done])
                vec(lambda v, fi=fi: v.tensor_tensor(
                    m1c[:, 1:2], sx_r[:, fi:fi + 1], swcol[:, 5:6], ALU.mult))
                vec(lambda v: v.tensor_tensor(m1c[:, 2:3], m1c[:, 0:1],
                                              m1c[:, 1:2], ALU.mult))
                sc_evt = dv.evt()
                act(lambda s, bgp=bgp: s.activation(
                    out=sigt[:], in_=bgp.ap[:, 0:512], func=AF.Sigmoid,
                    scale=m1c[:, 0:1]), waits=[sc_evt])
                sig_evt = ac.evt()
                vec(lambda v, bgp=bgp: v.scalar_tensor_tensor(
                    m1t[:], bgp.ap[:, 0:512], m1c[:, 2:3], sigt[:], ALU.mult,
                    ALU.mult), waits=[sig_evt])
                bgp.war = dv.evt()
                vec(lambda v, bup=bup: v.tensor_tensor(
                    hblk[:], m1t[:], bup.ap[:, 0:512], ALU.mult),
                    waits=[prev_hb_store] if prev_hb_store else [])
                bup.war = dv.evt()
                vec(lambda v: v.tensor_reduce(m1c[:, 3:4], hblk[:], AX.X,
                                              ALU.max,
                                              apply_absolute_value=True))
                vec(lambda v, fi=fi: v.tensor_tensor(
                    hmax[:, fi:fi + 1], hmax[:, fi:fi + 1], m1c[:, 3:4],
                    ALU.max))
                hb_evt = dv.evt()
                prev_hb_store = dma("sync",
                                    hsp[fi * 128:(fi + 1) * 128, isl2],
                                    hblk[:], waits=[hb_evt])
            buf_pe2[sb2] = pe.evt()
        prev_slab_pe = pe.evt()
        h_spill = sl.evt()
        vec(lambda v: v.tensor_scalar(hmax[:], hmax[:], 1e-5, 1.0 / 127.0,
                                      ALU.max, ALU.mult))
        vec(lambda v: v.tensor_copy(hq_s[:, :, 1], hmax[:]))
        vec(lambda v: v.reciprocal(hmax[:], hmax[:]))
        vec(lambda v: v.tensor_copy(hq_s[:, :, 0], hmax[:]))
        s2_done = dv.evt()
        esC.close()

        if phases == "ABC":
            P.do("vector", lambda v: v.engine_nop(),
                 [s2_done, h_spill, sl.evt(), sg.evt(), ac.evt(), po.evt(),
                  pe.evt()], dv)
            P.finalize()
            return nc

        esD = es.enter_context(contextlib.ExitStack())

        def sbd(name, shape, dt=F32):
            return esD.enter_context(nc.sbuf_tensor(name, shape, dt))

        wd_t = sbd("wd_t", [128, 32, 1024], BF16)
        wdl = sbd("wdl", [128, 4096])
        tscrD = sbd("tscrD", [128, 4096])
        wdq = sbd("wdq", [128, 2, 4096], BF16)
        hqt = sbd("hqt", [128, 2, 32, 128], BF16)
        o2 = sbd("o2", [128, 4, C])
        x1d = sbd("x1d", [128, 4, C])

        for sj in range(8):
            dma("sync", wdl[:].rearrange("p (i c) -> p i c", c=1024),
                d["wdT"].rearrange("(i p) c -> p i c", p=128)[
                    :, sj * 4:(sj + 1) * 4, :],
                waits=[h_spill, s2_done, prev_slab_pe] if sj == 0
                else [dv.evt()])
            vec(lambda v: v.tensor_scalar(
                tscrD[:], wdl[:], swcol[:, 2:3], MAGIC, ALU.mult, ALU.add),
                waits=[sl.evt(), sw_done])
            vec(lambda v: v.tensor_scalar(tscrD[:], tscrD[:], MAGIC, 1.0,
                                          ALU.subtract, ALU.min))
            vec(lambda v, sj=sj: v.tensor_scalar(
                wd_t[:, sj * 4:(sj + 1) * 4, :].rearrange("p a q -> p (a q)"),
                tscrD[:], -1.0, None, ALU.max))
        wd_done = dv.evt()

        prev_hqT = [None, None]
        buf_pe = [None, None]
        st_out = None
        lx_evt = None
        ld_evt = None
        for fi in range(16):
            q4, a_ = divmod(fi, 4)
            pb = fi % 2
            if fi == 0:
                dma("gpsimd", wdl[:], hsp[0:128, :],
                    waits=[h_spill, wd_done])
                ld_evt = sg.evt()
                dma("gpsimd", x1d[:],
                    x1sp[0:512, :].rearrange("(a p) c -> p a c", p=128),
                    waits=[h_spill])
                lx_evt = sg.evt()
            vec(lambda v, fi=fi: v.tensor_scalar(
                tscrD[:], wdl[:], hq_s[:, fi, 0:1], MAGIC, ALU.mult,
                ALU.add), waits=[ld_evt, s2_done]
                + ([wd_done] if fi == 0 else []))
            rd_evt = dv.evt()
            vec(lambda v, pb=pb: v.tensor_scalar(
                wdq[:, pb, :], tscrD[:], MAGIC, None, ALU.subtract),
                waits=[prev_hqT[pb]] if prev_hqT[pb] else [])
            hqq = dv.evt()
            if fi + 1 < 16:
                dma("gpsimd", wdl[:], hsp[(fi + 1) * 128:(fi + 2) * 128, :],
                    waits=[rd_evt])
                ld_evt = sg.evt()
            for ic in range(32):
                dma("sync", hqt[:, pb, ic, :],
                    wdq[:, pb, ic * 128:(ic + 1) * 128],
                    waits=[hqq] + ([buf_pe[pb]] if buf_pe[pb] else []),
                    sem=st, transpose=True)
            hqT_evt = st.evt()
            prev_hqT[pb] = hqT_evt
            for eh in range(2):
                bo = grab()
                for ic in range(32):
                    mm(bo, bo.ap[:, 0:512], hqt[:, pb, ic, :],
                       wd_t[:, ic, eh * 512:(eh + 1) * 512],
                       ic == 0, ic == 31,
                       waits=[hqT_evt, wd_done] if ic == 0 else [])
                penop()
                aw = [pe.evt()]
                if a_ == 0 and eh == 0 and st_out:
                    aw.append(st_out)
                act(lambda s, bo=bo, a_=a_, eh=eh: s.copy(
                    o2[:, a_, eh * 512:(eh + 1) * 512], bo.ap[:, 0:512]),
                    waits=aw)
                bo.war = ac.evt()
            buf_pe[pb] = pe.evt()
            if a_ < 3:
                continue
            tsl = slice(q4 * 512, (q4 + 1) * 512)
            down_evt = ac.evt()
            for a_ in range(4):
                fi = q4 * 4 + a_
                vec(lambda v, fi=fi: v.tensor_tensor(
                    f_t1[:, 0:1], hq_s[:, fi, 1:2], swcol[:, 6:7], ALU.mult),
                    waits=[down_evt, lx_evt])
                vec(lambda v, a_=a_: v.scalar_tensor_tensor(
                    o2[:, a_, :], o2[:, a_, :], f_t1[:, 0:1], x1d[:, a_, :],
                    ALU.mult, ALU.add))
            vec(lambda v: v.tensor_reduce(f_mu[:], o2[:], AX.X, ALU.add))
            vec(lambda v: v.tensor_scalar(f_mu[:], f_mu[:], 1.0 / C, None,
                                          ALU.mult))
            pool(lambda g: g.tensor_tensor(fscr[:], o2[:], o2[:], ALU.mult),
                 waits=[dv.evt()])
            vec(lambda v: v.tensor_reduce(f_sg[:], fscr[:], AX.X, ALU.add),
                waits=[po.evt()])
            vec(lambda v: v.tensor_tensor(f_t1[:], f_mu[:], f_mu[:],
                                          ALU.mult))
            vec(lambda v: v.scalar_tensor_tensor(
                f_sg[:], f_sg[:], 1.0 / C, f_t1[:], ALU.mult, ALU.subtract))
            act(lambda s: s.activation(out=f_sg[:], in_=f_sg[:], func=AF.Ln,
                                       bias=CEL), waits=[dv.evt()])
            act(lambda s: s.activation(out=f_sg[:], in_=f_sg[:], func=AF.Exp,
                                       scale=CNH))
            for a_ in range(4):
                vec(lambda v, a_=a_: v.scalar_tensor_tensor(
                    o2[:, a_, :], o2[:, a_, :], f_mu[:, a_:a_ + 1],
                    f_sg[:, a_:a_ + 1].to_broadcast((128, C)),
                    ALU.subtract, ALU.mult), waits=[ac.evt()])
            oe = dv.evt()
            dma("gpsimd", out[tsl, :].rearrange("(a p) c -> p a c", p=128),
                o2[:], waits=[oe])
            st_out = sg.evt()
            if q4 + 1 < 4:
                dma("gpsimd", x1d[:],
                    x1sp[(q4 + 1) * 512:(q4 + 2) * 512, :].rearrange(
                        "(a p) c -> p a c", p=128), waits=[dv.evt()])
                lx_evt = sg.evt()

        if "x1" in dbg_out:
            dma("gpsimd", dbg_out["x1"][:, :], x1sp[:, :],
                waits=[sl.evt()])
        if "h" in dbg_out:
            dma("gpsimd", dbg_out["h"][:, :], hsp[:, :], waits=[sl.evt()])
        final_store = sl.evt()
        P.do("vector", lambda v: v.engine_nop(),
             [final_store, sg.evt()], dv)
        esD.close()
        esB.close()

        P.finalize()

    return nc


def kernel(**inputs):
    maps = host_prep(inputs)
    nc = build_program()
    res = run_bass_kernel_spmd(nc, maps, core_ids=list(range(8)))
    outa = np.empty((B, T, C), np.float32)
    for core in range(8):
        b, hg = divmod(core, 2)
        outa[b, hg * TH:(hg + 1) * TH, :] = res.results[core]["out"]
    return outa


# version 19
# speedup vs baseline: 1.1172x; 1.1172x over previous
import contextlib
import numpy as np
import concourse.bass as bass
import concourse.mybir as mybir
from concourse.bass_utils import run_bass_kernel_spmd

F32 = mybir.dt.float32
BF16 = mybir.dt.bfloat16
AF = mybir.ActivationFunctionType
ALU = mybir.AluOpType
AX = mybir.AxisListType

B, T, C, H, HS = 4, 4096, 1024, 16, 64
CH, NH, NPAIR = 512, 8, 4
CK, NCK = 256, 16
L = 64
NSUB = CK // L
NTTG = CK // 128
TH = 2048
INNER = 4096
NSLAB, FTT = 8, 16
MAGIC = float(np.float32(1.5 * 2 ** 23))
MG4 = float(np.float32(1.5 * 2 ** 19))
EPS_GN = 0.00064
EPS_LN = 1e-5
EXPM05 = float(np.exp(np.float32(-0.5)))

ENGINES = ("tensor", "vector", "scalar", "gpsimd", "sync")


class Sem:
    __slots__ = ("h", "n", "name")

    def __init__(self, h, name):
        self.h, self.n, self.name = h, 0, name

    def evt(self):
        return (self, self.n)


class Pipe:

    def __init__(self, nc, es):
        self.nc, self.es = nc, es
        self.block = None
        self.self_sem = {}
        self.seen = {e: {} for e in ENGINES}
        self._pend = None

    def sem(self, name):
        return Sem(self.es.enter_context(self.nc.semaphore(name)), name)

    def begin(self, block):
        self.block = block

    def _emit(self, eng, fn, w, inc, amt):
        seen = self.seen[eng]

        def body(engine):
            for (s_, c_) in w:
                if seen.get(id(s_), -1) < c_:
                    engine.wait_ge(s_.h, c_)
                    seen[id(s_)] = c_
            instr = fn(engine)
            if inc is not None:
                instr.then_inc(inc.h, amt)
        getattr(self.block, eng)(body)

    def _flush(self):
        if self._pend is not None:
            fn, w, inc, amt = self._pend
            self._pend = None
            self._emit("tensor", fn, w, inc, amt)

    def do(self, eng, fn, waits=(), inc=None, amt=1):
        w = []
        for x in waits:
            if x is None:
                continue
            if isinstance(x, list):
                for (s_, c_) in x:
                    if c_ > 0:
                        w.append((s_, c_))
            else:
                (s_, c_) = x
                if c_ > 0:
                    w.append((s_, c_))
        ss = self.self_sem.get(eng)
        if ss is not None and ss.n > 0:
            w.append((ss, ss.n))
        if eng == "tensor":
            self._flush()
            self._pend = (fn, w, inc, amt)
            if inc is not None:
                inc.n += amt
                return (inc, inc.n)
            return None
        self._emit(eng, fn, w, inc, amt)
        if inc is not None:
            inc.n += amt
            return (inc, inc.n)
        return None

    def attach_pe(self, pe):
        assert self._pend is not None
        fn, w, inc, amt = self._pend
        assert inc is None
        self._pend = (fn, w, pe, 1)
        pe.n += 1
        return (pe, pe.n)

    def finalize(self, block=None):
        self._flush()


class QRing:

    def __init__(self, pipe, name, n=4):
        self.sems = [pipe.sem(f"{name}{i}") for i in range(n)]
        self.i = 0

    def next(self):
        s = self.sems[self.i % len(self.sems)]
        self.i += 1
        return s

    def evt(self):
        return [(s, s.n) for s in self.sems]


class Bank:
    __slots__ = ("ap", "war")

    def __init__(self, t):
        self.ap, self.war = t, None


def host_prep(ins):
    x = np.asarray(ins["L0_attn_x"], np.float32)
    maps = []
    wgT = np.ascontiguousarray(np.asarray(ins["L0_ffn_w_gate"], np.float32).T)
    wuT = np.ascontiguousarray(np.asarray(ins["L0_ffn_w_up"], np.float32).T)
    wdT = np.ascontiguousarray(np.asarray(ins["L0_ffn_w_down"], np.float32).T)
    for core in range(8):
        b, hg = divmod(core, 2)
        ch = slice(hg * CH, (hg + 1) * CH)
        th = slice(hg * TH, (hg + 1) * TH)
        g = lambda k: np.asarray(ins["L0_attn_" + k], np.float32)
        r16a2 = np.zeros((65, 1024), np.float32)
        r16a2[0:16, 0:512] = g("aaa_w2")[:, ch]
        r16a2[32:48, 512:1024] = g("ma_w2")[:, ch]
        r16a2[64, 0:512] = g("time_aaaaa")[ch]
        r16a2[64, 512:1024] = g("time_misc_a")[ch]
        r16b2 = np.zeros((65, 1024), np.float32)
        r16b2[0:16, 0:512] = g("kkk_w2")[:, ch]
        r16b2[32:48, 512:1024] = g("mk_w2")[:, ch]
        r16b2[64, 512:1024] = g("time_misc_k")[ch]
        r16aT = np.zeros((C, 48), np.float32)
        r16aT[:, 0:16] = g("aaa_w1").T
        r16aT[:, 32:48] = g("ma_w1").T
        r16bT = np.zeros((C, 48), np.float32)
        r16bT[:, 0:16] = g("kkk_w1").T
        r16bT[:, 32:48] = g("mk_w1").T
        dec2b = np.concatenate([g("decay_w2")[:, ch], g("time_decay")[None, ch]], 0)
        sc = slice(core * 512, (core + 1) * 512)
        wstat = np.stack([wgT[:, sc].reshape(128, 4096),
                          wuT[:, sc].reshape(128, 4096),
                          wdT[sc, :].reshape(128, 4096)], axis=1)
        m = {
            "xT": np.ascontiguousarray(x[b].T),
            "xtm": np.ascontiguousarray(x[b, th]),
            "tmx_c": np.ascontiguousarray(g("time_maa_x").reshape(8, 128).T),
            "tmg_c": np.ascontiguousarray(
                g("time_maa").reshape(4, 8, 128).transpose(2, 1, 0)),
            "maa1T": np.ascontiguousarray(g("maa_w1").T),
            "maa2": np.ascontiguousarray(g("maa_w2")),
            "wrT": np.ascontiguousarray(g("Wr").T[:, ch]),
            "wkT": np.ascontiguousarray(g("Wk").T[:, ch]),
            "wvT": np.ascontiguousarray(g("Wv").T[:, ch]),
            "woT": np.ascontiguousarray(g("Wo").T[ch, :]),
            "gate1T": np.ascontiguousarray(g("gate_w1").T),
            "gate2": np.ascontiguousarray(g("gate_w2")[:, ch]),
            "dec1T": np.ascontiguousarray(g("decay_w1").T),
            "dec2b": np.ascontiguousarray(dec2b),
            "r16aT": r16aT, "r16bT": r16bT, "r16a2": r16a2, "r16b2": r16b2,
            "faa": np.ascontiguousarray(g("time_faaaa")[None, ch]),
            "lnw": np.ascontiguousarray(g("ln_w")[None, ch]),
            "lnb": np.ascontiguousarray(g("ln_b")[None, ch]),
            "wstat": np.ascontiguousarray(wstat),
            "wgT": wgT, "wuT": wuT, "wdT": wdT,
        }
        maps.append(m)
    return maps


def build_program(dbg=(), nchunks=NCK, phases="ABCD", sub=99):
    nc = bass.Bass("TRN2", target_bir_lowering=False)
    es = contextlib.ExitStack()
    d = {}

    def din(name, shape):
        d[name] = nc.dram_tensor(name, shape, F32, kind="ExternalInput").ap()

    din("xT", [C, T]); din("xtm", [TH, C])
    din("tmx_c", [128, 8]); din("tmg_c", [128, 8, 4])
    din("maa1T", [C, 128]); din("maa2", [4, 32, C])
    din("wrT", [C, CH]); din("wkT", [C, CH]); din("wvT", [C, CH])
    din("woT", [CH, C])
    din("gate1T", [C, 128]); din("gate2", [128, CH])
    din("dec1T", [C, 64]); din("dec2b", [65, CH])
    din("r16aT", [C, 48]); din("r16bT", [C, 48])
    din("r16a2", [65, 1024]); din("r16b2", [65, 1024])
    din("faa", [1, CH]); din("lnw", [1, CH]); din("lnb", [1, CH])
    din("wstat", [128, 3, 4096])
    din("wgT", [C, INNER]); din("wuT", [C, INNER]); din("wdT", [INNER, C])
    out = nc.dram_tensor("out", [TH, C], F32, kind="ExternalOutput").ap()

    opart = nc.dram_tensor("opart", [T, C], F32).ap()
    osum = nc.dram_tensor("osum", [TH, C], F32).ap()
    x1sp = nc.dram_tensor("x1sp", [TH, C], F32).ap()
    hsp = nc.dram_tensor("hsp", [TH, INNER], F32).ap()
    stat_dram = nc.dram_tensor("stat_dram", [1, 8], F32).ap()
    stin = nc.dram_tensor("stin", [8, 8], F32).ap()
    stout = nc.dram_tensor("stout", [1, 8], F32).ap()

    dbg_out = {}
    for name, shape in dbg:
        dbg_out[name] = nc.dram_tensor("dbg_" + name, shape, F32,
                                       kind="ExternalOutput").ap()

    with es:
        P = Pipe(nc, es)

        def sb_outer(name, shape, dt=F32):
            return es.enter_context(nc.sbuf_tensor(name, shape, dt))

        sl = QRing(P, "sl"); sg = QRing(P, "sg"); st = QRing(P, "st")
        pe = P.sem("pe"); dv = P.sem("dv"); ac = P.sem("ac")
        po = P.sem("po"); cc = P.sem("cc"); bar = P.sem("bar")
        P.self_sem = {"vector": dv, "scalar": ac, "gpsimd": po}
        qrings = {"sync": sl, "gpsimd": sg, "st": st}
        P.begin(es.enter_context(nc.Block(no_gpsimd_drain=False)))

        banks = [Bank(es.enter_context(nc.psum_tensor(f"pb{i}", [128, 512],
                                                      F32)))
                 for i in range(8)]
        rr = [0]

        def grab():
            b = banks[rr[0] % 8]
            rr[0] += 1
            return b

        def mm(bank, out_ap, lhsT, rhs, start, stop, tp=None, waits=(),
               inc=None, skip=False):
            w = list(waits)
            if bank is not None and start and bank.war is not None:
                w.append(bank.war)
            def fn(t, o=out_ap, l=lhsT, r=rhs, s1=start, s2=stop, tp=tp,
                   sk=skip):
                return t.matmul(o, l, r, start=s1, stop=s2, tile_position=tp,
                                skip_group_check=sk)
            return P.do("tensor", fn, w, inc)

        def penop(waits=()):
            return P.attach_pe(pe)

        def dma(queue, out_ap, in_ap, waits=(), sem=None, transpose=False):
            ring = st if transpose else qrings[queue]
            s_ = ring.next()
            w = list(waits) + [(s_, s_.n)]
            def fn(e, o=out_ap, i=in_ap, tr=transpose):
                if tr:
                    return e.dma_start(out=o, in_=i, transpose=True)
                return e.dma_start(out=o, in_=i)
            return P.do(queue, fn, w, s_, 16)

        def act(fn, waits=()):
            return P.do("scalar", fn, waits, ac)

        def vec(fn, waits=()):
            return P.do("vector", fn, waits, dv)

        def pool(fn, waits=()):
            return P.do("gpsimd", fn, waits, po)

        tmx = sb_outer("tmx", [128, 8])
        tmg = sb_outer("tmg", [128, 8, 4])
        dma("sync", tmx[:], d["tmx_c"])
        dma("sync", tmg[:], d["tmg_c"])
        rows_bf = sb_outer("rows_bf", [1, 3, CH], BF16)
        dma("gpsimd", rows_bf[:, 0, :], d["faa"])
        dma("gpsimd", rows_bf[:, 1, :], d["lnw"])
        dma("gpsimd", rows_bf[:, 2, :], d["lnb"])
        vload = sl.evt()
        rows_ld = sg.evt()

        ident = sb_outer("ident", [128, 64], BF16)
        tri = sb_outer("tri", [128, 64], BF16)
        maskS = sb_outer("maskS", [128, 64])
        maskSU = sb_outer("maskSU", [128, 64])
        maskIU = sb_outer("maskIU", [128, 64])
        sel = sb_outer("sel", [128, NTTG, NSUB], BF16)
        iscr = sb_outer("iscr", [128, 64])
        consts = sb_outer("consts", [128, 8])
        ones1 = sb_outer("ones1", [1, 128], BF16)
        z1 = sb_outer("z1", [1, 128], BF16)
        zr = sb_outer("zr", [1, 512], BF16)
        onesc = sb_outer("onesc", [128, 4], BF16)
        rows_bc = sb_outer("rows_bc", [128, 3, CH], BF16)
        swcol = sb_outer("swcol", [128, 8])

        pool(lambda g: g.iota(iscr[0:64, :], [[-1, 64]], base=0,
                              channel_multiplier=1,
                              allow_small_or_imprecise_dtypes=True))
        pool(lambda g: g.iota(iscr[64:128, :], [[-1, 64]], base=0,
                              channel_multiplier=1,
                              allow_small_or_imprecise_dtypes=True))
        io1 = po.evt()
        vec(lambda v: v.tensor_scalar(ident[:], iscr[:], 0.0, None,
                                      ALU.is_equal), waits=[io1])
        vec(lambda v: v.tensor_scalar(tri[:], iscr[:], 0.0, None, ALU.is_le))
        vec(lambda v: v.tensor_scalar(maskS[:], iscr[:], 0.0, None, ALU.is_gt))
        vec(lambda v: v.tensor_scalar(maskSU[:], iscr[:], 0.0, None,
                                      ALU.is_lt))
        vec(lambda v: v.tensor_scalar(maskIU[:], iscr[:], 0.0, None,
                                      ALU.is_le))
        vec(lambda v: v.memset(ones1[:], 1.0))
        vec(lambda v: v.memset(z1[:], 0.0))
        vec(lambda v: v.memset(zr[:], 0.0))
        vec(lambda v: v.memset(onesc[:], 0.0))
        vec(lambda v: v.memset(onesc[:, 0:1], 1.0))
        for j, val in enumerate([-1.0, -0.5, EPS_GN, EPS_LN, 1e-24,
                                 1.0 / HS, 127.0, 1e-5]):
            vec(lambda v, j=j, val=val: v.memset(consts[:, j:j + 1], val))
        CN1 = consts[:, 0:1]; CNH = consts[:, 1:2]; CEG = consts[:, 2:3]
        CEL = consts[:, 3:4]
        pool(lambda g: g.iota(iscr[:, 0:NSUB], [[1, NSUB]], base=0,
                              channel_multiplier=0,
                              allow_small_or_imprecise_dtypes=True),
             waits=[dv.evt()])
        pool(lambda g: g.iota(iscr[:, 8:9], [[0, 1]], base=0,
                              channel_multiplier=1,
                              allow_small_or_imprecise_dtypes=True))
        io2 = po.evt()
        vec(lambda v: v.tensor_scalar(iscr[:, 9:10], iscr[:, 8:9], 63.5, None,
                                      ALU.is_gt), waits=[io2])
        for tt in range(NTTG):
            vec(lambda v, tt=tt: v.tensor_scalar(
                iscr[:, 10:11], iscr[:, 9:10], 1.0, float(2 * tt),
                ALU.mult, ALU.add))
            vec(lambda v, tt=tt: v.scalar_tensor_tensor(
                iscr[:, 11:11 + NSUB],
                iscr[:, 10:11].to_broadcast((128, NSUB)), -1.0,
                iscr[:, 0:NSUB], ALU.mult, ALU.add))
            vec(lambda v, tt=tt: v.tensor_scalar(
                sel[:, tt, :], iscr[:, 11:11 + NSUB], 0.0, None,
                ALU.is_equal))
        consts_done = dv.evt()

        if sub == -4:
            P.do("vector", lambda v: v.engine_nop(),
                 [sl.evt(), sg.evt(), st.evt(), ac.evt(), po.evt(),
                  pe.evt(), dv.evt()], dv)
            P.finalize()
            return nc

        rbf = rows_ld
        for j in range(3):
            bb = grab()
            mm(bb, bb.ap[:, 0:512], ones1[:], rows_bf[:, j, :], True, True,
               waits=[rbf, consts_done])
            penop()
            vec(lambda v, bb=bb, j=j: v.tensor_copy(rows_bc[:, j, :],
                                                    bb.ap[:, 0:512]),
                waits=[pe.evt()])
            bb.war = dv.evt()

        if sub == -3:
            P.do("vector", lambda v: v.engine_nop(),
                 [sl.evt(), sg.evt(), st.evt(), ac.evt(), po.evt(),
                  pe.evt(), dv.evt()], dv)
            P.finalize()
            return nc
        bc_done = dv.evt()
        FAA_BC = rows_bc[:, 0, :]
        LNW_BC = rows_bc[:, 1, :]
        LNB_BC = rows_bc[:, 2, :]

        wst = sb_outer("wst", [128, 512])
        stat4 = sb_outer("stat4", [128, 4])
        stat_c = sb_outer("stat_c", [128, 1])
        stat_bf = sb_outer("stat_bf", [128, 12], BF16)
        statr = sb_outer("statr", [128, 4])
        strow = sb_outer("strow", [4, 2])
        stlo = sb_outer("stlo", [4, 1])
        swsum = sb_outer("swsum", [1, 8])
        vec(lambda v: v.memset(stat4[:], 0.0))
        for wi in range(3):
            for pc in range(8):
                dma("sync", wst[:],
                    d["wstat"][:, wi, pc * 512:(pc + 1) * 512],
                    waits=[dv.evt()])
                vec(lambda v: v.tensor_reduce(
                    stat_c[:], wst[:], AX.X, ALU.add,
                    apply_absolute_value=True), waits=[sl.evt()])
                vec(lambda v, wi=wi: v.tensor_tensor(
                    stat4[:, wi:wi + 1], stat4[:, wi:wi + 1], stat_c[:],
                    ALU.add))
        vec(lambda v: v.tensor_copy(stat_bf[:, 0:4], stat4[:]))
        vec(lambda v: v.tensor_tensor(statr[:], stat4[:],
                                      stat_bf[:, 0:4], ALU.subtract))
        vec(lambda v: v.tensor_copy(stat_bf[:, 4:8], statr[:]))
        vec(lambda v: v.tensor_tensor(stat_bf[:, 8:12], statr[:],
                                      stat_bf[:, 4:8], ALU.subtract))
        bs = grab()
        mm(bs, bs.ap[0:4, 0:1], stat_bf[:, 0:4], onesc[:, 0:1], True, True,
           waits=[dv.evt(), consts_done])
        mm(bs, bs.ap[0:4, 1:2], stat_bf[:, 4:8], onesc[:, 0:1], True, True)
        mm(bs, bs.ap[0:4, 2:3], stat_bf[:, 8:12], onesc[:, 0:1], True, True)
        penop()
        vec(lambda v, bs=bs: v.tensor_scalar(
            strow[:, 0:1], bs.ap[0:4, 0:1], MG4, MG4, ALU.add, ALU.subtract),
            waits=[pe.evt()])
        vec(lambda v, bs=bs: v.tensor_tensor(
            stlo[:], bs.ap[0:4, 0:1], strow[:, 0:1], ALU.subtract))
        vec(lambda v, bs=bs: v.tensor_tensor(
            stlo[:], stlo[:], bs.ap[0:4, 1:2], ALU.add))
        vec(lambda v, bs=bs: v.tensor_tensor(
            strow[:, 1:2], stlo[:], bs.ap[0:4, 2:3], ALU.add))
        bs.war = dv.evt()
        dma("gpsimd", stat_dram[0, 0:4], strow[:, 0], waits=[dv.evt()])
        dma("gpsimd", stat_dram[0, 4:8], strow[:, 1], waits=[dv.evt()])
        dma("gpsimd", stin[:, :], stat_dram.to_broadcast((8, 8)),
            waits=[sg.evt()])
        P.do("gpsimd", lambda g: g.collective_compute(
            "ReduceScatter", ALU.add, replica_groups=[list(range(8))],
            ins=[stin[:, :]], outs=[stout[:, :]]), [sg.evt()], cc)
        dma("gpsimd", swsum[:], stout[:, :], waits=[cc.evt()])
        swr = sb_outer("swr", [1, 24], BF16)
        swf = sb_outer("swf", [1, 8])
        swfr = sb_outer("swfr", [1, 8])
        vec(lambda v: v.tensor_tensor(
            swsum[:, 0:4], swsum[:, 0:4], swsum[:, 4:8], ALU.add),
            waits=[sg.evt()])
        vec(lambda v: v.tensor_scalar(
            swsum[:, 0:4], swsum[:, 0:4], 1.0 / (C * INNER), 1e-5, ALU.mult,
            ALU.max))
        vec(lambda v: v.tensor_copy(swf[:, 4:8], swsum[:, 0:4]))
        vec(lambda v: v.reciprocal(swsum[:, 0:4], swsum[:, 0:4]))
        vec(lambda v: v.tensor_copy(swf[:, 0:4], swsum[:, 0:4]))
        vec(lambda v: v.tensor_copy(swr[:, 0:8], swf[:]))
        vec(lambda v: v.tensor_tensor(swfr[:], swf[:], swr[:, 0:8],
                                      ALU.subtract))
        vec(lambda v: v.tensor_copy(swr[:, 8:16], swfr[:]))
        vec(lambda v: v.tensor_tensor(swr[:, 16:24], swfr[:], swr[:, 8:16],
                                      ALU.subtract))
        bs2 = grab()
        mm(bs2, bs2.ap[:, 0:24], ones1[:], swr[:], True, True,
           waits=[dv.evt()])
        penop()
        vec(lambda v, bs2=bs2: v.tensor_copy(swcol[:], bs2.ap[:, 0:8]),
            waits=[pe.evt()])
        vec(lambda v, bs2=bs2: v.tensor_tensor(
            swcol[:], swcol[:], bs2.ap[:, 8:16], ALU.add))
        vec(lambda v, bs2=bs2: v.tensor_tensor(
            swcol[:], swcol[:], bs2.ap[:, 16:24], ALU.add))
        bs2.war = dv.evt()
        sw_done = dv.evt()

        if sub == -2:
            if "swc" in dbg_out:
                dma("sync", dbg_out["swc"][:, 0:8], swcol[:], waits=[sw_done])
                dma("sync", dbg_out["swc"][:, 8:12],
                    stat4[:], waits=[sw_done])
            P.do("vector", lambda v: v.engine_nop(),
                 [sl.evt(), sg.evt(), st.evt(), ac.evt(), po.evt(),
                  pe.evt(), dv.evt()], dv)
            P.finalize()
            return nc

        esA = es.enter_context(contextlib.ExitStack())

        def sba(name, shape, dt=BF16):
            return esA.enter_context(nc.sbuf_tensor(name, shape, dt))

        w_wr = sba("w_wr", [128, 8, CH]); w_wk = sba("w_wk", [128, 8, CH])
        w_wv = sba("w_wv", [128, 8, CH]); w_wo = sba("w_wo", [128, 4, C])
        w_wrl = sba("w_wrl", [128, 8, CH]); w_wkl = sba("w_wkl", [128, 8, CH])
        w_wvl = sba("w_wvl", [128, 8, CH])
        w_maa1 = sba("w_maa1", [128, 8, 128])
        w_m2A = sba("w_m2A", [64, C])
        w_m2B = sba("w_m2B", [64, C])
        w_g1 = sba("w_g1", [128, 8, 128]); w_g2 = sba("w_g2", [128, CH])
        w_d1 = sba("w_d1", [128, 8, 64]); w_d2 = sba("w_d2", [65, CH])
        w_ra1 = sba("w_ra1", [128, 8, 48]); w_rb1 = sba("w_rb1", [128, 8, 48])
        w_ra2 = sba("w_ra2", [65, 1024]); w_rb2 = sba("w_rb2", [65, 1024])
        for dst, name in ((w_wr, "wrT"), (w_wk, "wkT"), (w_wv, "wvT"),
                          (w_wo, "woT"), (w_maa1, "maa1T"), (w_g1, "gate1T"),
                          (w_d1, "dec1T"), (w_ra1, "r16aT"), (w_rb1, "r16bT")):
            dma("gpsimd", dst[:], d[name].rearrange("(c p) o -> p c o", p=128))
        dma("gpsimd", w_m2A[:], d["maa2"][0:2].rearrange("g k c -> (g k) c"))
        dma("gpsimd", w_m2B[:], d["maa2"][2:4].rearrange("g k c -> (g k) c"))
        dma("gpsimd", w_g2[:], d["gate2"])
        dma("gpsimd", w_d2[:], d["dec2b"])
        dma("gpsimd", w_ra2[:], d["r16a2"])
        dma("gpsimd", w_rb2[:], d["r16b2"])
        esA0 = contextlib.ExitStack()
        wfp = esA0.enter_context(nc.sbuf_tensor("wfp", [128, 8, CH], F32))
        for hi_t, lo_t, name in ((w_wr, w_wrl, "wrT"), (w_wk, w_wkl, "wkT"),
                                 (w_wv, w_wvl, "wvT")):
            dma("sync", wfp[:], d[name].rearrange("(c p) o -> p c o", p=128),
                waits=[dv.evt()])
            vec(lambda v, hi_t=hi_t, lo_t=lo_t: v.tensor_tensor(
                lo_t[:], wfp[:], hi_t[:], ALU.subtract),
                waits=[sl.evt(), sg.evt()])
        esA0.close()
        wld = sg.evt()
        wlo_done = dv.evt()

        if sub == -1:
            P.do("vector", lambda v: v.engine_nop(),
                 [sl.evt(), sg.evt(), st.evt(), ac.evt(), po.evt(),
                  pe.evt(), dv.evt()], dv)
            P.finalize()
            return nc

        NTT = CK // 128
        xf = sba("xf", [128, 8, CK + 1], F32)
        xm0 = sba("xm0", [128, 8, CK], F32)
        xm1 = sba("xm1", [128, 8, CK], F32)
        xxt = sba("xxt", [128, 8, CK])
        xmh = sba("xmh", [128, 8, CK])
        xml = sba("xml", [128, 8, CK])
        loA = sba("loA", [64, CK]); loB = sba("loB", [64, CK])
        g_in = sba("g_in", [128, CK])
        dec_in = sba("dec_in", [65, CK]); ra_in = sba("ra_in", [65, CK])
        rb_in = sba("rb_in", [65, CK])
        dec_inl = sba("dec_inl", [65, CK])
        dec_f = sba("dec_f", [64, CK], F32)
        vec(lambda v: v.memset(dec_in[:], 0.0))
        vec(lambda v: v.memset(ra_in[:], 0.0))
        vec(lambda v: v.memset(rb_in[:], 0.0))
        vec(lambda v: v.memset(dec_inl[:], 0.0))
        vec(lambda v: v.memset(dec_in[64:65, :], 1.0))
        vec(lambda v: v.memset(ra_in[64:65, :], 1.0))
        vec(lambda v: v.memset(rb_in[64:65, :], 1.0))
        inner_init = dv.evt()

        def tokf(name):
            return sba(name, [128, NTT, CH], F32)

        def tokb(name):
            return sba(name, [128, NTT, CH])

        rtok = tokf("rtok"); gtok = tokb("gtok")
        ktmp = tokf("ktmp"); kkn = tokf("kkn"); kfin = tokf("kfin")
        asig = tokf("asig"); masig = tokf("masig"); mksig = tokf("mksig")
        ept = tokb("ept"); emt = tokb("emt"); epvt = tokb("epvt")
        sdec = tokf("sdec"); scr1 = tokf("scr1")
        ke = mksig
        ytok = sdec
        vtok = tokb("vtok")
        gamh = tokb("gamh"); gaml = tokb("gaml")
        atil = tokb("atil"); btil = tokb("btil")
        ktil = tokb("ktil"); qtil = tokb("qtil")
        ygt = tokb("ygt")

        s8 = lambda n: sba(n, [128, NTT, 8], F32)
        mu8, var8, rstd8, rk8 = s8("mu8"), s8("var8"), s8("rstd8"), s8("rk8")
        dlt = sba("dlt", [128, 4, NSUB], F32)

        AFt = sba("AFt", [128, 4, NTT, 128])
        BFt = sba("BFt", [128, 4, NTT, 128])
        KFt = sba("KFt", [128, 4, NTT, 128])
        QFt = sba("QFt", [128, 4, NTT, 128])

        m8 = lambda n: sba(n, [128, 8, 64])
        Tsb, Ttsb, Usb, QBsb, QKsb, Nsb, Msb = map(
            m8, ["Tsb", "Ttsb", "Usb", "QBsb", "QKsb", "Nsb", "Msb"])
        Xsb = sba("Xsb", [128, 8, 128])
        Xlo = sba("Xlo", [128, 8, 128])
        Wh = sba("Wh", [128, 4, 64]); Mc = sba("Mc", [128, 4, 64])
        Zf = esA.enter_context(nc.sbuf_tensor("Zf", [128, 4, 64], F32))
        Zhi = sba("Zhi", [128, 4, 64]); Zlo = sba("Zlo", [128, 4, 64])
        vec(lambda v: v.memset(Zf[:], 0.0))
        vec(lambda v: v.memset(Zhi[:], 0.0))
        vec(lambda v: v.memset(Zlo[:], 0.0))
        z_evt = dv.evt()

        NBAR = 5
        cc_evts = []

        for ci in range(nchunks):
            bw = [(bar, NBAR * ci)]
            if ci == 0:
                vec(lambda v: v.memset(xf[:, :, 0:1], 0.0))
                dma("gpsimd", xf[:, :, 1:CK + 1],
                    d["xT"].rearrange("(c p) t -> p c t", p=128)[:, :, 0:CK],
                    waits=[inner_init])
            else:
                dma("gpsimd", xf[:, :, 0:CK + 1],
                    d["xT"].rearrange("(c p) t -> p c t", p=128)[
                        :, :, ci * CK - 1:ci * CK + CK], waits=bw)
            xr = sg.evt()

            vec(lambda v: v.tensor_tensor(xxt[:], xf[:, :, 0:CK],
                                          xf[:, :, 1:CK + 1], ALU.subtract),
                waits=[xr, consts_done, bc_done] + bw)
            for cc2 in range(8):
                vec(lambda v, cc2=cc2: v.scalar_tensor_tensor(
                    xmh[:, cc2, :], xxt[:, cc2, :], tmx[:, cc2:cc2 + 1],
                    xf[:, cc2, 1:CK + 1], ALU.mult, ALU.add), waits=[vload])
            mix_evt = dv.evt()

            blo = grab()
            for cc2 in range(8):
                mm(blo, blo.ap[:, 0:CK], w_maa1[:, cc2, :], xmh[:, cc2, :],
                   cc2 == 0, cc2 == 7,
                   waits=[mix_evt, wld] + bw if cc2 == 0 else [])
            penop()
            act(lambda s, blo=blo: s.activation(
                out=loA[:], in_=blo.ap[0:64, 0:CK], func=AF.Tanh),
                waits=[pe.evt()] + bw)
            act(lambda s, blo=blo: s.activation(
                out=loB[:], in_=blo.ap[64:128, 0:CK], func=AF.Tanh))
            blo.war = ac.evt()
            lo_evt = ac.evt()
            lo_done_pe = pe.evt()

            def emit_lo2_xm(gi, xmbuf, extra_wait):
                lot = loA if gi < 2 else loB
                ksl = slice((gi % 2) * 32, (gi % 2) * 32 + 32)
                for cc2 in range(8):
                    bb = grab()
                    wm2 = w_m2A if gi < 2 else w_m2B
                    mm(bb, bb.ap[:, 0:CK],
                       wm2[(gi % 2) * 32:(gi % 2) * 32 + 32,
                           cc2 * 128:(cc2 + 1) * 128],
                       lot[ksl, :], True, True, waits=[lo_evt])
                    penop()
                    w0 = [pe.evt()]
                    if cc2 == 0:
                        w0.append(lo_done_pe)
                        if extra_wait is not None:
                            w0.append(extra_wait)
                    act(lambda s, bb=bb, cc2=cc2, gi=gi, xmbuf=xmbuf:
                        s.activation(
                            out=xmbuf[:, cc2, :], in_=bb.ap[:, 0:CK],
                            func=AF.Identity, bias=tmg[:, cc2, gi:gi + 1]),
                        waits=w0)
                    bb.war = ac.evt()
                lo2_evt = ac.evt()
                pool(lambda g, xmbuf=xmbuf: g.tensor_tensor(
                    xmbuf[:], xxt[:], xmbuf[:], ALU.mult), waits=[lo2_evt])
                vec(lambda v, xmbuf=xmbuf: v.tensor_tensor(
                    xmbuf[:], xf[:, :, 1:CK + 1], xmbuf[:], ALU.add),
                    waits=[po.evt()])
                xm_f = dv.evt()
                act(lambda s, xmbuf=xmbuf: s.copy(xmh[:], xmbuf[:]),
                    waits=[xm_f] + ([extra_wait] if extra_wait else []))
                vec(lambda v, xmbuf=xmbuf: v.tensor_tensor(
                    xml[:], xmbuf[:], xmh[:], ALU.subtract),
                    waits=[ac.evt()])
                return dv.evt()

            def proj512(xmbuf, wtile, dst, e_in, wlo=None):
                for tt in range(NTT):
                    bp = grab()
                    for cc2 in range(8):
                        mm(bp, bp.ap[:, 0:512],
                           xmh[:, cc2, tt * 128:(tt + 1) * 128],
                           wtile[:, cc2, :], cc2 == 0, False,
                           waits=[e_in, wlo_done] if cc2 == 0 else [])
                        mm(bp, bp.ap[:, 0:512],
                           xml[:, cc2, tt * 128:(tt + 1) * 128],
                           wtile[:, cc2, :], False, False)
                        if wlo is not None:
                            mm(bp, bp.ap[:, 0:512],
                               xmh[:, cc2, tt * 128:(tt + 1) * 128],
                               wlo[:, cc2, :], False, cc2 == 7)
                        elif cc2 == 7:
                            mm(bp, bp.ap[:, 0:512],
                               xml[:, cc2, tt * 128:(tt + 1) * 128],
                               wtile[:, cc2, :], False, True, skip=True)
                    penop()
                    act(lambda s, bp=bp, tt=tt, dst=dst: s.copy(
                        dst[:, tt, :], bp.ap[:, 0:512]), waits=[pe.evt()])
                    bp.war = ac.evt()
                return ac.evt()

            e0 = emit_lo2_xm(0, xm1, lo_done_pe)
            r_evt = proj512(xm1, w_wr, rtok, e0, w_wrl)
            bg = grab()
            for cc2 in range(8):
                mm(bg, bg.ap[:, 0:CK], w_g1[:, cc2, :], xmh[:, cc2, :],
                   cc2 == 0, cc2 == 7, waits=[e0] if cc2 == 0 else [])
            penop()
            act(lambda s, bg=bg: s.activation(out=g_in[:], in_=bg.ap[:, 0:CK],
                                              func=AF.Tanh), waits=[pe.evt()])
            bg.war = ac.evt()
            gin_evt = ac.evt()
            for tt in range(NTT):
                bgo = grab()
                mm(bgo, bgo.ap[:, 0:512], g_in[:, tt * 128:(tt + 1) * 128],
                   w_g2[:], True, True, waits=[gin_evt])
                penop()
                act(lambda s, bgo=bgo, tt=tt: s.copy(gtok[:, tt, :],
                                                     bgo.ap[:, 0:512]),
                    waits=[pe.evt()])
                bgo.war = ac.evt()
            xm1_free = pe.evt()

            e1 = emit_lo2_xm(1, xm0, xm1_free)
            bd = grab()
            for cc2 in range(8):
                mm(bd, bd.ap[0:64, 0:CK], w_d1[:, cc2, :], xmh[:, cc2, :],
                   cc2 == 0, cc2 == 7, waits=[e1] if cc2 == 0 else [])
            penop()
            act(lambda s, bd=bd: s.activation(out=dec_f[:],
                                              in_=bd.ap[0:64, 0:CK],
                                              func=AF.Tanh), waits=[pe.evt()])
            bd.war = ac.evt()
            act(lambda s: s.copy(dec_in[0:64, :], dec_f[:]))
            vec(lambda v: v.tensor_tensor(dec_inl[0:64, :], dec_f[:],
                                          dec_in[0:64, :], ALU.subtract),
                waits=[ac.evt()])
            din_evt = dv.evt()
            bra = grab()
            for cc2 in range(8):
                mm(bra, bra.ap[0:48, 0:CK], w_ra1[:, cc2, :], xmh[:, cc2, :],
                   cc2 == 0, cc2 == 7, waits=[e1] if cc2 == 0 else [])
            penop()
            act(lambda s, bra=bra: s.copy(ra_in[0:48, :], bra.ap[0:48, 0:CK]),
                waits=[pe.evt()])
            bra.war = ac.evt()
            rain_evt = ac.evt()
            for tt in range(NTT):
                bdo = grab()
                mm(bdo, bdo.ap[:, 0:512], dec_in[:, tt * 128:(tt + 1) * 128],
                   w_d2[:], True, False, waits=[din_evt])
                mm(bdo, bdo.ap[:, 0:512], dec_inl[:, tt * 128:(tt + 1) * 128],
                   w_d2[:], False, True)
                penop()
                act(lambda s, bdo=bdo, tt=tt: s.activation(
                    out=sdec[:, tt, :], in_=bdo.ap[:, 0:512],
                    func=AF.Sigmoid), waits=[pe.evt()])
                bdo.war = ac.evt()
            for tt in range(NTT):
                ba1 = grab()
                mm(ba1, ba1.ap[:, 0:512], ra_in[:, tt * 128:(tt + 1) * 128],
                   w_ra2[:, 0:512], True, True, waits=[rain_evt])
                penop()
                act(lambda s, ba1=ba1, tt=tt: s.activation(
                    out=asig[:, tt, :], in_=ba1.ap[:, 0:512],
                    func=AF.Sigmoid), waits=[pe.evt()])
                ba1.war = ac.evt()
                ba2 = grab()
                mm(ba2, ba2.ap[:, 0:512], ra_in[:, tt * 128:(tt + 1) * 128],
                   w_ra2[:, 512:1024], True, True)
                penop()
                act(lambda s, ba2=ba2, tt=tt: s.activation(
                    out=masig[:, tt, :], in_=ba2.ap[:, 0:512],
                    func=AF.Sigmoid), waits=[pe.evt()])
                ba2.war = ac.evt()
            sdec_evt = ac.evt()
            xm0_free = pe.evt()

            vec(lambda v: v.tensor_scalar(gamh[:], sdec[:], -EXPM05, None,
                                          ALU.mult), waits=[sdec_evt])
            vec(lambda v: v.scalar_tensor_tensor(
                gaml[:], sdec[:], -EXPM05, gamh[:], ALU.mult, ALU.subtract))
            gam_evt = dv.evt()

            e2 = emit_lo2_xm(2, xm1, xm0_free)
            k_evt = proj512(xm1, w_wk, ktmp, e2, w_wkl)
            brb = grab()
            for cc2 in range(8):
                mm(brb, brb.ap[0:48, 0:CK], w_rb1[:, cc2, :], xmh[:, cc2, :],
                   cc2 == 0, cc2 == 7, waits=[e2] if cc2 == 0 else [])
            penop()
            act(lambda s, brb=brb: s.activation(out=rb_in[0:16, :],
                                                in_=brb.ap[0:16, 0:CK],
                                                func=AF.Tanh),
                waits=[pe.evt()])
            act(lambda s, brb=brb: s.copy(rb_in[32:48, :],
                                          brb.ap[32:48, 0:CK]))
            brb.war = ac.evt()
            rbin_evt = ac.evt()
            for tt in range(NTT):
                bb1 = grab()
                mm(bb1, bb1.ap[:, 0:512], rb_in[:, tt * 128:(tt + 1) * 128],
                   w_rb2[:, 0:512], True, True, waits=[rbin_evt])
                penop()
                act(lambda s, bb1=bb1, tt=tt: s.copy(scr1[:, tt, :],
                                                     bb1.ap[:, 0:512]),
                    waits=[pe.evt()])
                bb1.war = ac.evt()
                bb2 = grab()
                mm(bb2, bb2.ap[:, 0:512], rb_in[:, tt * 128:(tt + 1) * 128],
                   w_rb2[:, 512:1024], True, True)
                penop()
                act(lambda s, bb2=bb2, tt=tt: s.activation(
                    out=mksig[:, tt, :], in_=bb2.ap[:, 0:512],
                    func=AF.Sigmoid), waits=[pe.evt()])
                bb2.war = ac.evt()
            kkk_evt = ac.evt()
            xm1_free = pe.evt()

            e3 = emit_lo2_xm(3, xm0, xm1_free)
            v_evt = proj512(xm0, w_wv, vtok, e3, w_wvl)
            xm0_free = pe.evt()

            vec(lambda v: v.tensor_tensor(kkn[:], ktmp[:], scr1[:], ALU.add),
                waits=[k_evt, kkk_evt])
            vec(lambda v: v.tensor_tensor(scr1[:], kkn[:], kkn[:], ALU.mult))
            vec(lambda v: v.tensor_reduce(
                rk8[:], scr1[:].rearrange("p a (h s) -> p a h s", s=64),
                AX.X, ALU.add))
            vec(lambda v: v.tensor_scalar(rk8[:], rk8[:], 1e-24, None,
                                          ALU.max))
            act(lambda s: s.activation(out=rk8[:], in_=rk8[:], func=AF.Ln),
                waits=[dv.evt()])
            act(lambda s: s.activation(out=rk8[:], in_=rk8[:], func=AF.Exp,
                                       scale=CNH))
            vec(lambda v: v.tensor_tensor(
                kkn[:].rearrange("p a (h s) -> p a h s", s=64),
                kkn[:].rearrange("p a (h s) -> p a h s", s=64),
                rk8[:].rearrange("p a (h o) -> p a h o", o=1).to_broadcast(
                    (128, NTT, 8, 64)), ALU.mult), waits=[ac.evt()])
            kkn_evt = dv.evt()
            act(lambda s: s.activation(out=sdec[:], in_=sdec[:], func=AF.Ln),
                waits=[gam_evt])
            lns_evt = ac.evt()
            vec(lambda v: v.scalar_tensor_tensor(
                scr1[:], sdec[:], -0.5, mksig[:], ALU.add, ALU.mult),
                waits=[lns_evt, kkn_evt])
            act(lambda s: s.activation(out=ke[:], in_=scr1[:], func=AF.Exp),
                waits=[dv.evt()])
            ke_evt = ac.evt()
            vec(lambda v: v.tensor_scalar(scr1[:], masig[:], -1.0, 1.0,
                                          ALU.mult, ALU.add),
                waits=[ke_evt])
            pool(lambda g: g.tensor_tensor(scr1[:], asig[:], scr1[:],
                                           ALU.mult), waits=[dv.evt()])
            vec(lambda v: v.tensor_tensor(scr1[:], masig[:], scr1[:],
                                          ALU.add), waits=[po.evt()])
            pool(lambda g: g.tensor_tensor(kfin[:], ktmp[:], scr1[:],
                                           ALU.mult), waits=[dv.evt()])
            vec(lambda v: v.tensor_tensor(kfin[:], kfin[:], ke[:], ALU.mult),
                waits=[po.evt(), ke_evt])
            kfin_evt = dv.evt()

            for tt in range(NTT):
                bG = grab()
                for hf in range(2):
                    rb0 = hf * 64
                    rsl0 = slice(rb0, rb0 + 64)
                    mm(bG, bG.ap[rsl0, 0:512], tri[rsl0, :],
                       gamh[rsl0, tt, :], True, False, tp=(rb0, rb0),
                       waits=[gam_evt] if hf == 0 else [])
                    mm(bG, bG.ap[rsl0, 0:512], tri[rsl0, :],
                       gaml[rsl0, tt, :], False, True, tp=(rb0, rb0))
                penop()
                gG = pe.evt()
                act(lambda s, bG=bG, tt=tt: s.activation(
                    out=ept[:, tt, :], in_=bG.ap[:, 0:512], func=AF.Exp),
                    waits=[gG])
                act(lambda s, bG=bG, tt=tt: s.activation(
                    out=emt[:, tt, :], in_=bG.ap[:, 0:512], func=AF.Exp,
                    scale=CN1))
                vec(lambda v, bG=bG, tt=tt: v.scalar_tensor_tensor(
                    ytok[:, tt, :], gamh[:, tt, :], -1.0, bG.ap[:, 0:512],
                    ALU.mult, ALU.add), waits=[gG])
                vec(lambda v, tt=tt: v.tensor_tensor(
                    ytok[:, tt, :], ytok[:, tt, :], gaml[:, tt, :],
                    ALU.subtract))
                act(lambda s, tt=tt: s.activation(
                    out=epvt[:, tt, :], in_=ytok[:, tt, :], func=AF.Exp),
                    waits=[dv.evt()])
                bG.war = ac.evt()
            ef_evt = ac.evt()

            bDL = grab()
            for p in range(NPAIR):
                csl = slice(p * 128, (p + 1) * 128)
                for tt in range(NTT):
                    mm(bDL, bDL.ap[:, p * NSUB:(p + 1) * NSUB],
                       gamh[:, tt, csl], sel[:, tt, :], tt == 0, False,
                       waits=[gam_evt, consts_done]
                       if (p == 0 and tt == 0) else [])
                    mm(bDL, bDL.ap[:, p * NSUB:(p + 1) * NSUB],
                       gaml[:, tt, csl], sel[:, tt, :], False, tt == NTT - 1)
            penop()
            act(lambda s, bDL=bDL: s.activation(
                out=dlt[:].rearrange("p a e -> p (a e)"),
                in_=bDL.ap[:, 0:4 * NSUB], func=AF.Exp), waits=[pe.evt()])
            bDL.war = ac.evt()
            dl_evt = ac.evt()

            vec(lambda v: v.tensor_tensor(atil[:], kkn[:], epvt[:], ALU.mult),
                waits=[ef_evt, kkn_evt])
            vec(lambda v: v.scalar_tensor_tensor(
                scr1[:], kkn[:], -1.0, asig[:], ALU.mult, ALU.mult),
                waits=[kfin_evt])
            pool(lambda g: g.tensor_tensor(btil[:], scr1[:], emt[:],
                                           ALU.mult),
                 waits=[ef_evt, dv.evt(), po.evt()])
            btil_evt = po.evt()
            vec(lambda v: v.tensor_tensor(ktil[:], kfin[:], emt[:],
                                          ALU.mult), waits=[kfin_evt])
            pool(lambda g: g.tensor_tensor(qtil[:], rtok[:], ept[:],
                                           ALU.mult), waits=[r_evt, ef_evt])
            sc_dv = dv.evt()
            sc_po = po.evt()

            for (src_, dst) in ((atil, AFt), (btil, BFt), (ktil, KFt),
                                (qtil, QFt)):
                for tt in range(NTT):
                    dma("sync", dst[:, :, tt, :],
                        src_[:, tt, :].rearrange("p (t x) -> p t x", x=128),
                        waits=[sc_dv, sc_po], transpose=True)
            tr_evt = st.evt()

            z_evt_g = [z_evt]
            b7hi_free = [None]
            y_free = [None]

            def make_sub(s_i):
                tt = s_i // 2
                rb0 = (s_i % 2) * 64
                rsl = slice(rb0, rb0 + 64)
                bX = [banks[5], banks[6]]
                st = {"nsrc": Tsb, "msrc": Ttsb}

                def mbc(mk):
                    return mk[rsl, :].unsqueeze(1).to_broadcast((64, 8, 64))

                def pv(bb):
                    return bb.ap[rsl, 0:512].rearrange("p (a s) -> p a s",
                                                       s=64)

                def evac_X(w, lo=False):
                    vec(lambda v: v.tensor_copy(
                        Xsb[rsl, 0:4, :].rearrange("p a x -> p (a x)"),
                        bX[0].ap[rsl, 0:512]), waits=w)
                    act(lambda s: s.copy(
                        Xsb[rsl, 4:8, :].rearrange("p a x -> p (a x)"),
                        bX[1].ap[rsl, 0:512]), waits=w)
                    if lo:
                        hi_done = ac.evt()
                        vec(lambda v: v.tensor_tensor(
                            Xlo[rsl, 0:4, :].rearrange("p a x -> p (a x)"),
                            bX[0].ap[rsl, 0:512],
                            Xsb[rsl, 0:4, :].rearrange("p a x -> p (a x)"),
                            ALU.subtract))
                        vec(lambda v: v.tensor_tensor(
                            Xlo[rsl, 4:8, :].rearrange("p a x -> p (a x)"),
                            bX[1].ap[rsl, 0:512],
                            Xsb[rsl, 4:8, :].rearrange("p a x -> p (a x)"),
                            ALU.subtract), waits=[hi_done])
                    return [dv.evt(), ac.evt()]

                def stage_g(extra_w=()):
                    bT, bTt, bU, bQB, bQK = (banks[0], banks[1], banks[2],
                                             banks[3], banks[4])
                    for p in range(NPAIR):
                        for h2 in range(2):
                            i8 = p * 2 + h2
                            hsl = slice(h2 * 64, h2 * 64 + 64)
                            col = slice(i8 * 64, i8 * 64 + 64)
                            AFs = AFt[hsl, p, tt, rb0:rb0 + 64]
                            BFs = BFt[hsl, p, tt, rb0:rb0 + 64]
                            KFs = KFt[hsl, p, tt, rb0:rb0 + 64]
                            QFs = QFt[hsl, p, tt, rb0:rb0 + 64]
                            w0 = [tr_evt] + list(extra_w) if i8 == 0 else []
                            mm(bT, bT.ap[rsl, col], AFs, BFs, True, True,
                               tp=(h2 * 64, rb0), waits=w0)
                            mm(bTt, bTt.ap[rsl, col], BFs, AFs, True, True,
                               tp=(h2 * 64, rb0))
                            mm(bU, bU.ap[rsl, col], KFs, AFs, True, True,
                               tp=(h2 * 64, rb0))
                            mm(bQB, bQB.ap[rsl, col], BFs, QFs, True, True,
                               tp=(h2 * 64, rb0))
                            mm(bQK, bQK.ap[rsl, col], KFs, QFs, True, True,
                               tp=(h2 * 64, rb0))
                    penop()
                    gram_evt = pe.evt()
                    vec(lambda v: v.tensor_tensor(
                        Tsb[rsl, :, :], pv(bT), mbc(maskS), ALU.mult),
                        waits=[gram_evt])
                    vec(lambda v: v.tensor_tensor(
                        Ttsb[rsl, :, :], pv(bTt), mbc(maskSU), ALU.mult))
                    vec(lambda v: v.tensor_tensor(
                        Usb[rsl, :, :], pv(bU), mbc(maskSU), ALU.mult))
                    vec(lambda v: v.tensor_tensor(
                        QBsb[rsl, :, :], pv(bQB), mbc(maskIU), ALU.mult))
                    vec(lambda v: v.tensor_tensor(
                        QKsb[rsl, :, :], pv(bQK), mbc(maskIU), ALU.mult))
                    st["gram_sb"] = dv.evt()
                    st["pw"] = [st["gram_sb"]]

                def stage_x0():
                    for bXp in bX:
                        mm(bXp, bXp.ap[rsl, 0:512], z1[0:1, 0:64],
                           zr[0:1, :], True, True, tp=(0, rb0),
                           waits=[st["gram_sb"], v_evt])
                    for p in range(NPAIR):
                        bXp = bX[p // 2]
                        xo = (p % 2) * 256
                        for h2 in range(2):
                            i8 = p * 2 + h2
                            base = p * 128 + h2 * 64
                            mm(None, bXp.ap[rsl, xo + h2 * 128:
                                            xo + h2 * 128 + 64],
                               Usb[rsl, i8, :], vtok[rsl, tt, base:base + 64],
                               False, False, tp=(rb0, rb0), skip=True)
                            mm(None, bXp.ap[rsl, xo + h2 * 128 + 64:
                                            xo + h2 * 128 + 128],
                               ident[rsl, :], atil[rsl, tt, base:base + 64],
                               False, False, tp=(rb0, rb0), skip=True)
                    penop()
                    st["xe"] = evac_X([pe.evt()])

                def stage_lvl(lvl):
                    nsrc, msrc = st["nsrc"], st["msrc"]
                    for p in range(NPAIR):
                        bXp = bX[p // 2]
                        xo = (p % 2) * 256
                        for h2 in range(2):
                            i8 = p * 2 + h2
                            mm(None, bXp.ap[rsl, xo + h2 * 128:
                                            xo + h2 * 128 + 128],
                               msrc[rsl, i8, :], Xsb[rsl, i8, :],
                               False, False, tp=(rb0, rb0),
                               waits=st["xe"] + st["pw"]
                               if i8 == 0 and p == 0 else [],
                               skip=True)
                    penop()
                    ap_evt = pe.evt()
                    if lvl < 5:
                        bN, bM = banks[0], banks[1]
                        for i8 in range(8):
                            col = slice(i8 * 64, i8 * 64 + 64)
                            mm(bN, bN.ap[rsl, col], msrc[rsl, i8, :],
                               nsrc[rsl, i8, :], True, True, tp=(rb0, rb0),
                               waits=st["pw"] if i8 == 0 else [])
                            mm(bM, bM.ap[rsl, col], nsrc[rsl, i8, :],
                               msrc[rsl, i8, :], True, True, tp=(rb0, rb0))
                        penop()
                        sq_evt = pe.evt()
                        vec(lambda v: v.tensor_copy(Nsb[rsl, :, :], pv(bN)),
                            waits=[sq_evt])
                        act(lambda s: s.copy(Msb[rsl, :, :], pv(bM)),
                            waits=[sq_evt])
                        st["pw"] = [dv.evt(), ac.evt()]
                        st["nsrc"], st["msrc"] = Nsb, Msb
                    st["xe"] = evac_X([ap_evt], lo=(lvl == 5))

                def stage_wy():
                    b7 = banks[7]
                    for p in range(NPAIR):
                        for h2 in range(2):
                            i8 = p * 2 + h2
                            hsl = slice(h2 * 64, h2 * 64 + 64)
                            w0 = (st["xe"] + [st["gram_sb"], z_evt_g[0]]
                                  + ([b7hi_free[0]] if b7hi_free[0] else [])
                                  if i8 == 0 else [])
                            mm(b7, b7.ap[hsl, 256 + p * 64:
                                           256 + (p + 1) * 64],
                               Xsb[rsl, i8, 64:128], QBsb[rsl, i8, :],
                               True, True, tp=(rb0, h2 * 64), waits=w0)
                    penop()
                    vec(lambda v: v.tensor_tensor(
                        Wh[:], banks[7].ap[:, 256:512].rearrange(
                            "p (a s) -> p a s", s=64),
                        QFt[:, :, tt, rb0:rb0 + 64], ALU.add),
                        waits=[pe.evt()])
                    st["wh"] = dv.evt()
                    bY = banks[3]
                    mm(bY, bY.ap[rsl, 0:512], z1[0:1, 0:64], zr[0:1, :],
                       True, True, tp=(0, rb0),
                       waits=st["xe"] + ([y_free[0]] if y_free[0] else []))
                    for p in range(NPAIR):
                        for h2 in range(2):
                            i8 = p * 2 + h2
                            base = p * 128 + h2 * 64
                            ycol = slice(base, base + 64)
                            mm(None, bY.ap[rsl, ycol], QBsb[rsl, i8, :],
                               Xsb[rsl, i8, 0:64], False, False,
                               tp=(rb0, rb0), skip=True)
                            mm(None, bY.ap[rsl, ycol], QBsb[rsl, i8, :],
                               Xlo[rsl, i8, 0:64], False, False,
                               tp=(rb0, rb0), skip=True)
                            mm(None, bY.ap[rsl, ycol], QKsb[rsl, i8, :],
                               vtok[rsl, tt, base:base + 64], False, False,
                               tp=(rb0, rb0), skip=True)
                    for p in range(NPAIR):
                        for h2 in range(2):
                            i8 = p * 2 + h2
                            hsl = slice(h2 * 64, h2 * 64 + 64)
                            base = p * 128 + h2 * 64
                            mm(None, bY.ap[rsl, base:base + 64], Wh[hsl, p, :],
                               Zhi[hsl, p, :], False, False,
                               tp=(h2 * 64, rb0),
                               waits=[st["wh"], z_evt_g[0]] if i8 == 0
                               else [], skip=True)
                    penop()
                    act(lambda s: s.copy(ytok[rsl, tt, :],
                                         banks[3].ap[rsl, 0:512]),
                        waits=[pe.evt()])
                    y_free[0] = ac.evt()

                def stage_pz():
                    b7 = banks[7]
                    for p in range(NPAIR):
                        for h2 in range(2):
                            i8 = p * 2 + h2
                            hsl = slice(h2 * 64, h2 * 64 + 64)
                            mm(b7, b7.ap[hsl, 256 + p * 64:
                                           256 + (p + 1) * 64],
                               Xsb[rsl, i8, 64:128],
                               btil[rsl, tt, base_of(p, h2)], True, True,
                               tp=(rb0, h2 * 64),
                               waits=st["xe"] + [st["wh"]] if i8 == 0 else [])
                    penop()
                    vec(lambda v: v.tensor_tensor(
                        Mc[:], b7.ap[:, 256:512].rearrange(
                            "p (a s) -> p a s", s=64),
                        ident[:].unsqueeze(1).to_broadcast((128, 4, 64)),
                        ALU.add), waits=[pe.evt()])
                    mc_evt = dv.evt()
                    b7hi_free[0] = mc_evt
                    mm(b7, b7.ap[:, 0:256], z1[:], zr[0:1, 0:256],
                       True, True, waits=st["xe"] + [mc_evt, z_evt_g[0]])
                    for p in range(NPAIR):
                        for h2 in range(2):
                            i8 = p * 2 + h2
                            hsl = slice(h2 * 64, h2 * 64 + 64)
                            zcol = slice(p * 64, (p + 1) * 64)
                            base = p * 128 + h2 * 64
                            mm(None, b7.ap[hsl, zcol],
                               btil[rsl, tt, base:base + 64],
                               Xsb[rsl, i8, 0:64],
                               False, False, tp=(rb0, h2 * 64), skip=True)
                            mm(None, b7.ap[hsl, zcol],
                               btil[rsl, tt, base:base + 64],
                               Xlo[rsl, i8, 0:64],
                               False, False, tp=(rb0, h2 * 64), skip=True)
                            mm(None, b7.ap[hsl, zcol],
                               ktil[rsl, tt, base:base + 64],
                               vtok[rsl, tt, base:base + 64], False, False,
                               tp=(rb0, h2 * 64), skip=True)
                            mm(None, b7.ap[hsl, zcol], Mc[hsl, p, :],
                               Zhi[hsl, p, :], False, False,
                               tp=(h2 * 64, h2 * 64), skip=True)
                            mm(None, b7.ap[hsl, zcol], Mc[hsl, p, :],
                               Zlo[hsl, p, :], False, False,
                               tp=(h2 * 64, h2 * 64), skip=True)
                    penop()
                    zup_evt = pe.evt()
                    vec(lambda v: v.tensor_tensor(
                        Zf[:], b7.ap[:, 0:256].rearrange("p (a s) -> p a s",
                                                         s=64),
                        dlt[:, :, s_i:s_i + 1].to_broadcast((128, 4, 64)),
                        ALU.mult), waits=[zup_evt, dl_evt])
                    vec(lambda v: v.tensor_copy(Zhi[:], Zf[:]))
                    vec(lambda v: v.tensor_tensor(Zlo[:], Zf[:], Zhi[:],
                                                  ALU.subtract))
                    z_evt_g[0] = dv.evt()

                def base_of(p, h2):
                    base = p * 128 + h2 * 64
                    return slice(base, base + 64)

                st["g"], st["x0"], st["lvl"] = stage_g, stage_x0, stage_lvl
                st["wy"], st["pz"] = stage_wy, stage_pz
                return st

            for pk in range(0, NSUB, 2):
                sa = make_sub(pk)
                sb_ = make_sub(pk + 1)
                pair_w = [dv.evt(), ac.evt()] if pk else []
                sa["g"](pair_w)
                sb_["g"]()
                sa["x0"]()
                sb_["x0"]()
                for lvl in range(6):
                    sa["lvl"](lvl)
                    sb_["lvl"](lvl)
                sa["wy"]()
                sa["pz"]()
                sb_["wy"]()
                sb_["pz"]()

            z_evt = z_evt_g[0]
            y_all = ac.evt()

            y4 = ytok[:].rearrange("p a (h s) -> p a h s", s=64)

            def bch(t_):
                return t_[:].rearrange("p a (h o) -> p a h o",
                                       o=1).to_broadcast((128, NTT, 8, 64))

            vec(lambda v: v.tensor_reduce(mu8[:], y4, AX.X, ALU.add),
                waits=[y_all])
            vec(lambda v: v.tensor_scalar(mu8[:], mu8[:], 1.0 / HS, None,
                                          ALU.mult))
            pool(lambda g: g.tensor_tensor(scr1[:], ytok[:], ytok[:],
                                           ALU.mult), waits=[y_all, dv.evt()])
            vec(lambda v: v.tensor_reduce(
                var8[:], scr1[:].rearrange("p a (h s) -> p a h s", s=64),
                AX.X, ALU.add), waits=[po.evt()])
            vec(lambda v: v.tensor_tensor(rstd8[:], mu8[:], mu8[:], ALU.mult))
            vec(lambda v: v.scalar_tensor_tensor(
                var8[:], var8[:], 1.0 / HS, rstd8[:], ALU.mult, ALU.subtract))
            act(lambda s: s.activation(out=rstd8[:], in_=var8[:], func=AF.Ln,
                                       bias=CEG), waits=[dv.evt()])
            act(lambda s: s.activation(out=rstd8[:], in_=rstd8[:],
                                       func=AF.Exp, scale=CNH))
            rstd_evt = ac.evt()
            vec(lambda v: v.tensor_tensor(y4, y4, bch(mu8), ALU.subtract),
                waits=[rstd_evt])
            vec(lambda v: v.tensor_tensor(y4, y4, bch(rstd8), ALU.mult))
            vec(lambda v: v.tensor_tensor(
                ytok[:], ytok[:],
                LNW_BC.unsqueeze(1).to_broadcast((128, NTT, CH)), ALU.mult),
                waits=[bc_done])
            vec(lambda v: v.tensor_tensor(
                ytok[:], ytok[:],
                LNB_BC.unsqueeze(1).to_broadcast((128, NTT, CH)), ALU.add))
            pool(lambda g: g.tensor_tensor(
                scr1[:], kfin[:],
                FAA_BC.unsqueeze(1).to_broadcast((128, NTT, CH)), ALU.mult),
                waits=[kfin_evt, bc_done, dv.evt()])
            pool(lambda g: g.tensor_tensor(scr1[:], rtok[:], scr1[:],
                                           ALU.mult))
            vec(lambda v: v.tensor_reduce(
                rk8[:], scr1[:].rearrange("p a (h s) -> p a h s", s=64),
                AX.X, ALU.add), waits=[po.evt()])
            pool(lambda g: g.tensor_tensor(
                scr1[:].rearrange("p a (h s) -> p a h s", s=64),
                vtok[:].rearrange("p a (h s) -> p a h s", s=64),
                bch(rk8), ALU.mult), waits=[dv.evt()])
            vec(lambda v: v.tensor_tensor(ytok[:], ytok[:], scr1[:],
                                          ALU.add), waits=[po.evt()])
            vec(lambda v: v.tensor_tensor(ygt[:], ytok[:], gtok[:],
                                          ALU.mult))
            yg_evt = dv.evt()

            ygf = AFt
            for tt in range(NTT):
                dma("sync", ygf[:, :, tt, :],
                    ygt[:, tt, :].rearrange("p (t x) -> p t x", x=128),
                    waits=[yg_evt], transpose=True)
            ygf_evt = st.evt()
            prev_wo_store = None
            for tt in range(NTT):
                for eh in range(2):
                    bo = grab()
                    for p in range(4):
                        mm(bo, bo.ap[:, 0:512], ygf[:, p, tt, :],
                           w_wo[:, p, eh * 512:(eh + 1) * 512], p == 0,
                           p == 3, waits=[ygf_evt] if p == 0 else [])
                    penop()
                    act(lambda s, bo=bo, tt=tt, eh=eh: s.copy(
                        scr1[:, tt, :], bo.ap[:, 0:512]),
                        waits=[pe.evt(), prev_wo_store] if eh == 1
                        else [pe.evt()])
                    bo.war = ac.evt()
                    row0 = (ci % 8) * 512 + (ci // 8) * 256 + tt * 128
                    prev_wo_store = dma(
                        "sync", opart[row0:row0 + 128,
                                      eh * 512:(eh + 1) * 512],
                        scr1[:, tt, :], waits=[ac.evt()])
            wo_evt = sl.evt()

            if nchunks == NCK and ci >= NCK - 8:
                j = ci - (NCK - 8)
                P.do("gpsimd", lambda g, j=j: g.collective_compute(
                    "ReduceScatter", ALU.add,
                    replica_groups=[[0, 1], [2, 3], [4, 5], [6, 7]],
                    ins=[opart[j * 512:(j + 1) * 512, :]],
                    outs=[osum[j * 256:(j + 1) * 256, :]]),
                    [wo_evt, sg.evt()], cc)
                cc_evts.append(cc.evt())

            sl_c, st_c, sg_c, po_c = sl.evt(), st.evt(), sg.evt(), po.evt()
            dv_c, ac_c = dv.evt(), ac.evt()
            P.do("tensor", lambda t: t.nop(), [dv_c, ac_c, pe.evt()], bar)
            P.do("vector", lambda v: v.engine_nop(), [ac_c], bar)
            P.do("scalar", lambda s: s.nop(), [dv_c], bar)
            P.do("gpsimd", lambda g: g.engine_nop(), [sg_c], bar)
            P.do("sync", lambda s: s.nop(), [sl_c, st_c, po_c], bar)

            if sub == 8:
                break

        opart_done = sl.evt()
        attn_bar = bar.evt()
        esA.close()

        if phases == "A":
            P.do("vector", lambda v: v.engine_nop(),
                 [opart_done, attn_bar, st.evt(), sg.evt(), ac.evt(),
                  po.evt(), pe.evt(), sw_done], dv)
            P.finalize()
            return nc

        esB = es.enter_context(contextlib.ExitStack())
        esB1 = es.enter_context(contextlib.ExitStack())
        esC = es.enter_context(contextlib.ExitStack())

        def sbb(name, shape, dt=F32):
            return esB.enter_context(nc.sbuf_tensor(name, shape, dt))

        def sbb1(name, shape, dt=F32):
            return esB1.enter_context(nc.sbuf_tensor(name, shape, dt))

        def sbc(name, shape, dt=F32):
            return esC.enter_context(nc.sbuf_tensor(name, shape, dt))

        if not cc_evts:
            P.do("gpsimd", lambda g: g.collective_compute(
                "ReduceScatter", ALU.add,
                replica_groups=[[0, 1], [2, 3], [4, 5], [6, 7]],
                ins=[opart[:, :]], outs=[osum[:, :]]),
                [opart_done, attn_bar, sg.evt()], cc)
            cc_evts = [cc.evt()] * 8

        sx_r = sbb("sx_r", [128, 16])
        hq_s = sbb("hq_s", [128, 16, 2])
        hmax = sbb("hmax", [128, 16])
        f_mu = sbb("f_mu", [128, 4])
        f_sg = sbb("f_sg", [128, 4])
        f_t1 = sbb("f_t1", [128, 4])
        fscr = sbb("fscr", [128, 4, C])
        xqT = sbc("xqT", [128, 8, TH], BF16)
        x1t = sbb1("x1t", [128, 4, C])
        xr1 = sbb1("xr1", [128, 4, C])
        xq = sbb1("xq", [128, 4, C], BF16)

        for fi4 in range(4):
            tsl = slice(fi4 * 512, (fi4 + 1) * 512)
            dma("sync", x1t[:], osum[tsl, :].rearrange("(a p) c -> p a c",
                                                       p=128),
                waits=[cc_evts[2 * fi4 + 1], attn_bar, dv.evt()])
            dma("sync", xr1[:], d["xtm"][tsl, :].rearrange(
                "(a p) c -> p a c", p=128))
            ld_evt = sl.evt()
            vec(lambda v: v.tensor_tensor(x1t[:], x1t[:], xr1[:], ALU.add),
                waits=[ld_evt, st.evt()])
            vec(lambda v: v.tensor_reduce(f_mu[:], x1t[:], AX.X, ALU.add))
            vec(lambda v: v.tensor_scalar(f_mu[:], f_mu[:], 1.0 / C, None,
                                          ALU.mult))
            pool(lambda g: g.tensor_tensor(fscr[:], x1t[:], x1t[:], ALU.mult),
                 waits=[dv.evt()])
            vec(lambda v: v.tensor_reduce(f_sg[:], fscr[:], AX.X, ALU.add),
                waits=[po.evt()])
            vec(lambda v: v.tensor_tensor(f_t1[:], f_mu[:], f_mu[:],
                                          ALU.mult))
            vec(lambda v: v.scalar_tensor_tensor(
                f_sg[:], f_sg[:], 1.0 / C, f_t1[:], ALU.mult, ALU.subtract))
            act(lambda s: s.activation(out=f_sg[:], in_=f_sg[:], func=AF.Ln,
                                       bias=CEL), waits=[dv.evt()])
            act(lambda s: s.activation(out=f_sg[:], in_=f_sg[:], func=AF.Exp,
                                       scale=CNH))
            for a_ in range(4):
                vec(lambda v, a_=a_: v.scalar_tensor_tensor(
                    x1t[:, a_, :], x1t[:, a_, :], f_mu[:, a_:a_ + 1],
                    f_sg[:, a_:a_ + 1].to_broadcast((128, C)),
                    ALU.subtract, ALU.mult), waits=[ac.evt()])
            x1_evt = dv.evt()
            x1store = dma("sync",
                          x1sp[tsl, :].rearrange("(a p) c -> p a c", p=128),
                          x1t[:], waits=[x1_evt])
            vec(lambda v: v.tensor_reduce(f_mu[:], x1t[:], AX.X, ALU.max,
                                          apply_absolute_value=True))
            vec(lambda v: v.tensor_scalar(f_mu[:], f_mu[:], 1e-5,
                                          1.0 / 127.0, ALU.max, ALU.mult))
            vec(lambda v, fi4=fi4: v.tensor_copy(
                sx_r[:, fi4 * 4:(fi4 + 1) * 4], f_mu[:]))
            vec(lambda v: v.reciprocal(f_sg[:], f_mu[:]))
            for a_ in range(4):
                vec(lambda v, a_=a_: v.tensor_scalar(
                    x1t[:, a_, :], x1t[:, a_, :], f_sg[:, a_:a_ + 1], MAGIC,
                    ALU.mult, ALU.add), waits=[x1store] if a_ == 0 else [])
            vec(lambda v: v.tensor_scalar(xq[:], x1t[:], MAGIC, None,
                                          ALU.subtract))
            xq_evt = dv.evt()
            for a_ in range(4):
                dma("sync",
                    xqT[:, :, fi4 * 512 + a_ * 128:
                        fi4 * 512 + (a_ + 1) * 128],
                    xq[:, a_, :].rearrange("p (t x) -> p t x", x=128),
                    waits=[xq_evt], sem=st, transpose=True)
        xqT_evt = st.evt()
        sx_done = dv.evt()
        esB1.close()

        if phases == "AB":
            P.do("vector", lambda v: v.engine_nop(),
                 [xqT_evt, sx_done, sl.evt(), sg.evt(), ac.evt(), po.evt(),
                  pe.evt(), sw_done], dv)
            P.finalize()
            return nc

        wg_f = sbc("wg_f", [128, 8, 512])
        wu_f = sbc("wu_f", [128, 8, 512])
        wg_t = sbc("wg_t", [128, 2, 8, 512], BF16)
        wu_t = sbc("wu_t", [128, 2, 8, 512], BF16)
        tscr = sbc("tscr", [128, 8, 512])
        hblk = sbc("hblk", [128, 512])
        m1c = sbc("m1c", [128, 4])
        sigt = sbc("sigt", [128, 512])
        m1t = sbc("m1t", [128, 512])
        vec(lambda v: v.memset(hmax[:], 0.0))

        def ternarize(dst, srcf, wi, w_extra):
            vec(lambda v: v.tensor_scalar(
                tscr[:], srcf[:], swcol[:, wi:wi + 1], MAGIC, ALU.mult,
                ALU.add), waits=[sw_done] + [w for w in w_extra if w])
            rd = dv.evt()
            vec(lambda v: v.tensor_scalar(
                tscr[:], tscr[:], MAGIC, 1.0, ALU.subtract, ALU.min))
            vec(lambda v: v.tensor_scalar(dst, tscr[:], -1.0, None,
                                          ALU.max))
            return rd, dv.evt()

        prev_rd_g = None
        prev_rd_u = None
        tern_done = [None, None]
        buf_pe2 = [None, None]
        prev_hb_store = None
        for si in range(NSLAB + 1):
            if si < NSLAB:
                sb = si % 2
                isl = slice(si * 512, (si + 1) * 512)
                dma("sync", wg_f[:],
                    d["wgT"].rearrange("(c p) i -> p c i", p=128)[:, :, isl],
                    waits=[xqT_evt] if si == 0 else [prev_rd_g])
                lg = sl.evt()
                prev_rd_g, _ = ternarize(wg_t[:, sb, :, :], wg_f, 0,
                                         [lg, buf_pe2[sb]])
                dma("sync", wu_f[:],
                    d["wuT"].rearrange("(c p) i -> p c i", p=128)[:, :, isl],
                    waits=[prev_rd_u] if prev_rd_u else [])
                lu = sl.evt()
                prev_rd_u, tern_done[sb] = ternarize(wu_t[:, sb, :, :], wu_f,
                                                     1, [lu])
            if si == 0:
                continue
            sj = si - 1
            sb2 = sj % 2
            isl2 = slice(sj * 512, (sj + 1) * 512)
            for fi in range(FTT):
                bgp = grab(); bup = grab()
                for cc2 in range(8):
                    mm(bgp, bgp.ap[:, 0:512],
                       xqT[:, cc2, fi * 128:(fi + 1) * 128],
                       wg_t[:, sb2, cc2, :], cc2 == 0, cc2 == 7,
                       waits=[tern_done[sb2], xqT_evt] if cc2 == 0 else [])
                for cc2 in range(8):
                    mm(bup, bup.ap[:, 0:512],
                       xqT[:, cc2, fi * 128:(fi + 1) * 128],
                       wu_t[:, sb2, cc2, :], cc2 == 0, cc2 == 7,
                       waits=[tern_done[sb2]] if cc2 == 0 else [])
                penop()
                gu_evt = pe.evt()
                vec(lambda v, fi=fi: v.tensor_tensor(
                    m1c[:, 0:1], sx_r[:, fi:fi + 1], swcol[:, 4:5], ALU.mult),
                    waits=[gu_evt, sx_done, sw_done])
                vec(lambda v, fi=fi: v.tensor_tensor(
                    m1c[:, 1:2], sx_r[:, fi:fi + 1], swcol[:, 5:6], ALU.mult))
                vec(lambda v: v.tensor_tensor(m1c[:, 2:3], m1c[:, 0:1],
                                              m1c[:, 1:2], ALU.mult))
                sc_evt = dv.evt()
                act(lambda s, bgp=bgp: s.activation(
                    out=sigt[:], in_=bgp.ap[:, 0:512], func=AF.Sigmoid,
                    scale=m1c[:, 0:1]), waits=[sc_evt])
                sig_evt = ac.evt()
                vec(lambda v, bgp=bgp: v.scalar_tensor_tensor(
                    m1t[:], bgp.ap[:, 0:512], m1c[:, 2:3], sigt[:], ALU.mult,
                    ALU.mult), waits=[sig_evt])
                bgp.war = dv.evt()
                vec(lambda v, bup=bup: v.tensor_tensor(
                    hblk[:], m1t[:], bup.ap[:, 0:512], ALU.mult),
                    waits=[prev_hb_store] if prev_hb_store else [])
                bup.war = dv.evt()
                vec(lambda v: v.tensor_reduce(m1c[:, 3:4], hblk[:], AX.X,
                                              ALU.max,
                                              apply_absolute_value=True))
                vec(lambda v, fi=fi: v.tensor_tensor(
                    hmax[:, fi:fi + 1], hmax[:, fi:fi + 1], m1c[:, 3:4],
                    ALU.max))
                hb_evt = dv.evt()
                prev_hb_store = dma("sync",
                                    hsp[fi * 128:(fi + 1) * 128, isl2],
                                    hblk[:], waits=[hb_evt])
            buf_pe2[sb2] = pe.evt()
        prev_slab_pe = pe.evt()
        h_spill = sl.evt()
        vec(lambda v: v.tensor_scalar(hmax[:], hmax[:], 1e-5, 1.0 / 127.0,
                                      ALU.max, ALU.mult))
        vec(lambda v: v.tensor_copy(hq_s[:, :, 1], hmax[:]))
        vec(lambda v: v.reciprocal(hmax[:], hmax[:]))
        vec(lambda v: v.tensor_copy(hq_s[:, :, 0], hmax[:]))
        s2_done = dv.evt()
        esC.close()

        if phases == "ABC":
            P.do("vector", lambda v: v.engine_nop(),
                 [s2_done, h_spill, sl.evt(), sg.evt(), ac.evt(), po.evt(),
                  pe.evt()], dv)
            P.finalize()
            return nc

        esD = es.enter_context(contextlib.ExitStack())

        def sbd(name, shape, dt=F32):
            return esD.enter_context(nc.sbuf_tensor(name, shape, dt))

        wd_t = sbd("wd_t", [128, 32, 1024], BF16)
        wdl = sbd("wdl", [128, 4096])
        tscrD = sbd("tscrD", [128, 4096])
        wdq = sbd("wdq", [128, 2, 4096], BF16)
        hqt = sbd("hqt", [128, 2, 32, 128], BF16)
        o2 = sbd("o2", [128, 4, C])
        x1d = sbd("x1d", [128, 4, C])

        for sj in range(8):
            dma("sync", wdl[:].rearrange("p (i c) -> p i c", c=1024),
                d["wdT"].rearrange("(i p) c -> p i c", p=128)[
                    :, sj * 4:(sj + 1) * 4, :],
                waits=[h_spill, s2_done, prev_slab_pe] if sj == 0
                else [dv.evt()])
            vec(lambda v: v.tensor_scalar(
                tscrD[:], wdl[:], swcol[:, 2:3], MAGIC, ALU.mult, ALU.add),
                waits=[sl.evt(), sw_done])
            vec(lambda v: v.tensor_scalar(tscrD[:], tscrD[:], MAGIC, 1.0,
                                          ALU.subtract, ALU.min))
            vec(lambda v, sj=sj: v.tensor_scalar(
                wd_t[:, sj * 4:(sj + 1) * 4, :].rearrange("p a q -> p (a q)"),
                tscrD[:], -1.0, None, ALU.max))
        wd_done = dv.evt()

        prev_hqT = [None, None]
        buf_pe = [None, None]
        st_out = None
        lx_evt = None
        ld_evt = None
        for fi in range(16):
            q4, a_ = divmod(fi, 4)
            pb = fi % 2
            if fi == 0:
                dma("gpsimd", wdl[:], hsp[0:128, :],
                    waits=[h_spill, wd_done])
                ld_evt = sg.evt()
                dma("gpsimd", x1d[:],
                    x1sp[0:512, :].rearrange("(a p) c -> p a c", p=128),
                    waits=[h_spill])
                lx_evt = sg.evt()
            vec(lambda v, fi=fi: v.tensor_scalar(
                tscrD[:], wdl[:], hq_s[:, fi, 0:1], MAGIC, ALU.mult,
                ALU.add), waits=[ld_evt, s2_done]
                + ([wd_done] if fi == 0 else []))
            rd_evt = dv.evt()
            vec(lambda v, pb=pb: v.tensor_scalar(
                wdq[:, pb, :], tscrD[:], MAGIC, None, ALU.subtract),
                waits=[prev_hqT[pb]] if prev_hqT[pb] else [])
            hqq = dv.evt()
            if fi + 1 < 16:
                dma("gpsimd", wdl[:], hsp[(fi + 1) * 128:(fi + 2) * 128, :],
                    waits=[rd_evt])
                ld_evt = sg.evt()
            dma("sync", hqt[:, pb, :, :],
                wdq[:, pb, :].rearrange("p (t x) -> p t x", x=128),
                waits=[hqq] + ([buf_pe[pb]] if buf_pe[pb] else []),
                sem=st, transpose=True)
            hqT_evt = st.evt()
            prev_hqT[pb] = hqT_evt
            for eh in range(2):
                bo = grab()
                for ic in range(32):
                    mm(bo, bo.ap[:, 0:512], hqt[:, pb, ic, :],
                       wd_t[:, ic, eh * 512:(eh + 1) * 512],
                       ic == 0, ic == 31,
                       waits=[hqT_evt, wd_done] if ic == 0 else [])
                penop()
                aw = [pe.evt()]
                if a_ == 0 and eh == 0 and st_out:
                    aw.append(st_out)
                act(lambda s, bo=bo, a_=a_, eh=eh: s.copy(
                    o2[:, a_, eh * 512:(eh + 1) * 512], bo.ap[:, 0:512]),
                    waits=aw)
                bo.war = ac.evt()
            buf_pe[pb] = pe.evt()
            if a_ < 3:
                continue
            tsl = slice(q4 * 512, (q4 + 1) * 512)
            down_evt = ac.evt()
            for a_ in range(4):
                fi = q4 * 4 + a_
                vec(lambda v, fi=fi: v.tensor_tensor(
                    f_t1[:, 0:1], hq_s[:, fi, 1:2], swcol[:, 6:7], ALU.mult),
                    waits=[down_evt, lx_evt])
                vec(lambda v, a_=a_: v.scalar_tensor_tensor(
                    o2[:, a_, :], o2[:, a_, :], f_t1[:, 0:1], x1d[:, a_, :],
                    ALU.mult, ALU.add))
            vec(lambda v: v.tensor_reduce(f_mu[:], o2[:], AX.X, ALU.add))
            vec(lambda v: v.tensor_scalar(f_mu[:], f_mu[:], 1.0 / C, None,
                                          ALU.mult))
            pool(lambda g: g.tensor_tensor(fscr[:], o2[:], o2[:], ALU.mult),
                 waits=[dv.evt()])
            vec(lambda v: v.tensor_reduce(f_sg[:], fscr[:], AX.X, ALU.add),
                waits=[po.evt()])
            vec(lambda v: v.tensor_tensor(f_t1[:], f_mu[:], f_mu[:],
                                          ALU.mult))
            vec(lambda v: v.scalar_tensor_tensor(
                f_sg[:], f_sg[:], 1.0 / C, f_t1[:], ALU.mult, ALU.subtract))
            act(lambda s: s.activation(out=f_sg[:], in_=f_sg[:], func=AF.Ln,
                                       bias=CEL), waits=[dv.evt()])
            act(lambda s: s.activation(out=f_sg[:], in_=f_sg[:], func=AF.Exp,
                                       scale=CNH))
            for a_ in range(4):
                vec(lambda v, a_=a_: v.scalar_tensor_tensor(
                    o2[:, a_, :], o2[:, a_, :], f_mu[:, a_:a_ + 1],
                    f_sg[:, a_:a_ + 1].to_broadcast((128, C)),
                    ALU.subtract, ALU.mult), waits=[ac.evt()])
            oe = dv.evt()
            dma("gpsimd", out[tsl, :].rearrange("(a p) c -> p a c", p=128),
                o2[:], waits=[oe])
            st_out = sg.evt()
            if q4 + 1 < 4:
                dma("gpsimd", x1d[:],
                    x1sp[(q4 + 1) * 512:(q4 + 2) * 512, :].rearrange(
                        "(a p) c -> p a c", p=128), waits=[dv.evt()])
                lx_evt = sg.evt()

        if "x1" in dbg_out:
            dma("gpsimd", dbg_out["x1"][:, :], x1sp[:, :],
                waits=[sl.evt()])
        if "h" in dbg_out:
            dma("gpsimd", dbg_out["h"][:, :], hsp[:, :], waits=[sl.evt()])
        final_store = sl.evt()
        P.do("vector", lambda v: v.engine_nop(),
             [final_store, sg.evt()], dv)
        esD.close()
        esB.close()

        P.finalize()

    return nc


def kernel(**inputs):
    maps = host_prep(inputs)
    nc = build_program()
    res = run_bass_kernel_spmd(nc, maps, core_ids=list(range(8)))
    outa = np.empty((B, T, C), np.float32)
    for core in range(8):
        b, hg = divmod(core, 2)
        outa[b, hg * TH:(hg + 1) * TH, :] = res.results[core]["out"]
    return outa


# version 20
# speedup vs baseline: 1.1208x; 1.0032x over previous
import contextlib
import numpy as np
import concourse.bass as bass
import concourse.mybir as mybir
from concourse.bass_utils import run_bass_kernel_spmd

F32 = mybir.dt.float32
BF16 = mybir.dt.bfloat16
AF = mybir.ActivationFunctionType
ALU = mybir.AluOpType
AX = mybir.AxisListType

B, T, C, H, HS = 4, 4096, 1024, 16, 64
CH, NH, NPAIR = 512, 8, 4
CK, NCK = 256, 16
L = 64
NSUB = CK // L
NTTG = CK // 128
TH = 2048
INNER = 4096
NSLAB, FTT = 8, 16
MAGIC = float(np.float32(1.5 * 2 ** 23))
MG4 = float(np.float32(1.5 * 2 ** 19))
EPS_GN = 0.00064
EPS_LN = 1e-5
EXPM05 = float(np.exp(np.float32(-0.5)))

ENGINES = ("tensor", "vector", "scalar", "gpsimd", "sync")


class Sem:
    __slots__ = ("h", "n", "name")

    def __init__(self, h, name):
        self.h, self.n, self.name = h, 0, name

    def evt(self):
        return (self, self.n)


class Pipe:

    def __init__(self, nc, es):
        self.nc, self.es = nc, es
        self.block = None
        self.self_sem = {}
        self.seen = {e: {} for e in ENGINES}
        self._pend = None

    def sem(self, name):
        return Sem(self.es.enter_context(self.nc.semaphore(name)), name)

    def begin(self, block):
        self.block = block

    def _emit(self, eng, fn, w, inc, amt):
        seen = self.seen[eng]

        def body(engine):
            for (s_, c_) in w:
                if seen.get(id(s_), -1) < c_:
                    engine.wait_ge(s_.h, c_)
                    seen[id(s_)] = c_
            instr = fn(engine)
            if inc is not None:
                instr.then_inc(inc.h, amt)
        getattr(self.block, eng)(body)

    def _flush(self):
        if self._pend is not None:
            fn, w, inc, amt = self._pend
            self._pend = None
            self._emit("tensor", fn, w, inc, amt)

    def do(self, eng, fn, waits=(), inc=None, amt=1):
        w = []
        for x in waits:
            if x is None:
                continue
            if isinstance(x, list):
                for (s_, c_) in x:
                    if c_ > 0:
                        w.append((s_, c_))
            else:
                (s_, c_) = x
                if c_ > 0:
                    w.append((s_, c_))
        ss = self.self_sem.get(eng)
        if ss is not None and ss.n > 0:
            w.append((ss, ss.n))
        if eng == "tensor":
            self._flush()
            self._pend = (fn, w, inc, amt)
            if inc is not None:
                inc.n += amt
                return (inc, inc.n)
            return None
        self._emit(eng, fn, w, inc, amt)
        if inc is not None:
            inc.n += amt
            return (inc, inc.n)
        return None

    def attach_pe(self, pe):
        assert self._pend is not None
        fn, w, inc, amt = self._pend
        assert inc is None
        self._pend = (fn, w, pe, 1)
        pe.n += 1
        return (pe, pe.n)

    def finalize(self, block=None):
        self._flush()


class QRing:

    def __init__(self, pipe, name, n=4):
        self.sems = [pipe.sem(f"{name}{i}") for i in range(n)]
        self.i = 0

    def next(self):
        s = self.sems[self.i % len(self.sems)]
        self.i += 1
        return s

    def evt(self):
        return [(s, s.n) for s in self.sems]


class Bank:
    __slots__ = ("ap", "war")

    def __init__(self, t):
        self.ap, self.war = t, None


def host_prep(ins):
    x = np.asarray(ins["L0_attn_x"], np.float32)
    maps = []
    wgT = np.ascontiguousarray(np.asarray(ins["L0_ffn_w_gate"], np.float32).T)
    wuT = np.ascontiguousarray(np.asarray(ins["L0_ffn_w_up"], np.float32).T)
    wdT = np.ascontiguousarray(np.asarray(ins["L0_ffn_w_down"], np.float32).T)
    for core in range(8):
        b, hg = divmod(core, 2)
        ch = slice(hg * CH, (hg + 1) * CH)
        th = slice(hg * TH, (hg + 1) * TH)
        g = lambda k: np.asarray(ins["L0_attn_" + k], np.float32)
        r16a2 = np.zeros((65, 1024), np.float32)
        r16a2[0:16, 0:512] = g("aaa_w2")[:, ch]
        r16a2[32:48, 512:1024] = g("ma_w2")[:, ch]
        r16a2[64, 0:512] = g("time_aaaaa")[ch]
        r16a2[64, 512:1024] = g("time_misc_a")[ch]
        r16b2 = np.zeros((65, 1024), np.float32)
        r16b2[0:16, 0:512] = g("kkk_w2")[:, ch]
        r16b2[32:48, 512:1024] = g("mk_w2")[:, ch]
        r16b2[64, 512:1024] = g("time_misc_k")[ch]
        r16aT = np.zeros((C, 48), np.float32)
        r16aT[:, 0:16] = g("aaa_w1").T
        r16aT[:, 32:48] = g("ma_w1").T
        r16bT = np.zeros((C, 48), np.float32)
        r16bT[:, 0:16] = g("kkk_w1").T
        r16bT[:, 32:48] = g("mk_w1").T
        dec2b = np.concatenate([g("decay_w2")[:, ch], g("time_decay")[None, ch]], 0)
        sc = slice(core * 512, (core + 1) * 512)
        wstat = np.stack([wgT[:, sc].reshape(128, 4096),
                          wuT[:, sc].reshape(128, 4096),
                          wdT[sc, :].reshape(128, 4096)], axis=1)
        m = {
            "xT": np.ascontiguousarray(x[b].T),
            "xtm": np.ascontiguousarray(x[b, th]),
            "tmx_c": np.ascontiguousarray(g("time_maa_x").reshape(8, 128).T),
            "tmg_c": np.ascontiguousarray(
                g("time_maa").reshape(4, 8, 128).transpose(2, 1, 0)),
            "maa1T": np.ascontiguousarray(g("maa_w1").T),
            "maa2": np.ascontiguousarray(g("maa_w2")),
            "wrT": np.ascontiguousarray(g("Wr").T[:, ch]),
            "wkT": np.ascontiguousarray(g("Wk").T[:, ch]),
            "wvT": np.ascontiguousarray(g("Wv").T[:, ch]),
            "woT": np.ascontiguousarray(g("Wo").T[ch, :]),
            "gate1T": np.ascontiguousarray(g("gate_w1").T),
            "gate2": np.ascontiguousarray(g("gate_w2")[:, ch]),
            "dec1T": np.ascontiguousarray(g("decay_w1").T),
            "dec2b": np.ascontiguousarray(dec2b),
            "r16aT": r16aT, "r16bT": r16bT, "r16a2": r16a2, "r16b2": r16b2,
            "faa": np.ascontiguousarray(g("time_faaaa")[None, ch]),
            "lnw": np.ascontiguousarray(g("ln_w")[None, ch]),
            "lnb": np.ascontiguousarray(g("ln_b")[None, ch]),
            "wstat": np.ascontiguousarray(wstat),
            "wgT": wgT, "wuT": wuT, "wdT": wdT,
        }
        maps.append(m)
    return maps


def build_program(dbg=(), nchunks=NCK, phases="ABCD", sub=99):
    nc = bass.Bass("TRN2", target_bir_lowering=False)
    es = contextlib.ExitStack()
    d = {}

    def din(name, shape):
        d[name] = nc.dram_tensor(name, shape, F32, kind="ExternalInput").ap()

    din("xT", [C, T]); din("xtm", [TH, C])
    din("tmx_c", [128, 8]); din("tmg_c", [128, 8, 4])
    din("maa1T", [C, 128]); din("maa2", [4, 32, C])
    din("wrT", [C, CH]); din("wkT", [C, CH]); din("wvT", [C, CH])
    din("woT", [CH, C])
    din("gate1T", [C, 128]); din("gate2", [128, CH])
    din("dec1T", [C, 64]); din("dec2b", [65, CH])
    din("r16aT", [C, 48]); din("r16bT", [C, 48])
    din("r16a2", [65, 1024]); din("r16b2", [65, 1024])
    din("faa", [1, CH]); din("lnw", [1, CH]); din("lnb", [1, CH])
    din("wstat", [128, 3, 4096])
    din("wgT", [C, INNER]); din("wuT", [C, INNER]); din("wdT", [INNER, C])
    out = nc.dram_tensor("out", [TH, C], F32, kind="ExternalOutput").ap()

    opart = nc.dram_tensor("opart", [T, C], F32).ap()
    osum = nc.dram_tensor("osum", [TH, C], F32).ap()
    x1sp = nc.dram_tensor("x1sp", [TH, C], F32).ap()
    hsp = nc.dram_tensor("hsp", [TH, INNER], F32).ap()
    stat_dram = nc.dram_tensor("stat_dram", [1, 8], F32).ap()
    stin = nc.dram_tensor("stin", [8, 8], F32).ap()
    stout = nc.dram_tensor("stout", [1, 8], F32).ap()

    dbg_out = {}
    for name, shape in dbg:
        dbg_out[name] = nc.dram_tensor("dbg_" + name, shape, F32,
                                       kind="ExternalOutput").ap()

    with es:
        P = Pipe(nc, es)

        def sb_outer(name, shape, dt=F32):
            return es.enter_context(nc.sbuf_tensor(name, shape, dt))

        sl = QRing(P, "sl"); sg = QRing(P, "sg"); st = QRing(P, "st")
        pe = P.sem("pe"); dv = P.sem("dv"); ac = P.sem("ac")
        po = P.sem("po"); cc = P.sem("cc"); bar = P.sem("bar")
        P.self_sem = {"vector": dv, "scalar": ac, "gpsimd": po}
        qrings = {"sync": sl, "gpsimd": sg, "st": st}
        P.begin(es.enter_context(nc.Block(no_gpsimd_drain=False)))

        banks = [Bank(es.enter_context(nc.psum_tensor(f"pb{i}", [128, 512],
                                                      F32)))
                 for i in range(8)]
        rr = [0]

        def grab():
            b = banks[rr[0] % 8]
            rr[0] += 1
            return b

        def mm(bank, out_ap, lhsT, rhs, start, stop, tp=None, waits=(),
               inc=None, skip=False):
            w = list(waits)
            if bank is not None and start and bank.war is not None:
                w.append(bank.war)
            def fn(t, o=out_ap, l=lhsT, r=rhs, s1=start, s2=stop, tp=tp,
                   sk=skip):
                return t.matmul(o, l, r, start=s1, stop=s2, tile_position=tp,
                                skip_group_check=sk)
            return P.do("tensor", fn, w, inc)

        def penop(waits=()):
            return P.attach_pe(pe)

        def dma(queue, out_ap, in_ap, waits=(), sem=None, transpose=False):
            ring = st if transpose else qrings[queue]
            s_ = ring.next()
            w = list(waits) + [(s_, s_.n)]
            def fn(e, o=out_ap, i=in_ap, tr=transpose):
                if tr:
                    return e.dma_start(out=o, in_=i, transpose=True)
                return e.dma_start(out=o, in_=i)
            return P.do(queue, fn, w, s_, 16)

        def act(fn, waits=()):
            return P.do("scalar", fn, waits, ac)

        def vec(fn, waits=()):
            return P.do("vector", fn, waits, dv)

        def pool(fn, waits=()):
            return P.do("gpsimd", fn, waits, po)

        tmx = sb_outer("tmx", [128, 8])
        tmg = sb_outer("tmg", [128, 8, 4])
        dma("sync", tmx[:], d["tmx_c"])
        dma("sync", tmg[:], d["tmg_c"])
        rows_bf = sb_outer("rows_bf", [1, 3, CH], BF16)
        dma("gpsimd", rows_bf[:, 0, :], d["faa"])
        dma("gpsimd", rows_bf[:, 1, :], d["lnw"])
        dma("gpsimd", rows_bf[:, 2, :], d["lnb"])
        vload = sl.evt()
        rows_ld = sg.evt()

        ident = sb_outer("ident", [128, 64], BF16)
        tri = sb_outer("tri", [128, 64], BF16)
        maskS = sb_outer("maskS", [128, 64])
        maskSU = sb_outer("maskSU", [128, 64])
        maskIU = sb_outer("maskIU", [128, 64])
        sel = sb_outer("sel", [128, NTTG, NSUB], BF16)
        iscr = sb_outer("iscr", [128, 64])
        consts = sb_outer("consts", [128, 8])
        ones1 = sb_outer("ones1", [1, 128], BF16)
        z1 = sb_outer("z1", [1, 128], BF16)
        zr = sb_outer("zr", [1, 512], BF16)
        onesc = sb_outer("onesc", [128, 4], BF16)
        rows_bc = sb_outer("rows_bc", [128, 3, CH], BF16)
        swcol = sb_outer("swcol", [128, 8])

        pool(lambda g: g.iota(iscr[0:64, :], [[-1, 64]], base=0,
                              channel_multiplier=1,
                              allow_small_or_imprecise_dtypes=True))
        pool(lambda g: g.iota(iscr[64:128, :], [[-1, 64]], base=0,
                              channel_multiplier=1,
                              allow_small_or_imprecise_dtypes=True))
        io1 = po.evt()
        vec(lambda v: v.tensor_scalar(ident[:], iscr[:], 0.0, None,
                                      ALU.is_equal), waits=[io1])
        vec(lambda v: v.tensor_scalar(tri[:], iscr[:], 0.0, None, ALU.is_le))
        vec(lambda v: v.tensor_scalar(maskS[:], iscr[:], 0.0, None, ALU.is_gt))
        vec(lambda v: v.tensor_scalar(maskSU[:], iscr[:], 0.0, None,
                                      ALU.is_lt))
        vec(lambda v: v.tensor_scalar(maskIU[:], iscr[:], 0.0, None,
                                      ALU.is_le))
        vec(lambda v: v.memset(ones1[:], 1.0))
        vec(lambda v: v.memset(z1[:], 0.0))
        vec(lambda v: v.memset(zr[:], 0.0))
        vec(lambda v: v.memset(onesc[:], 0.0))
        vec(lambda v: v.memset(onesc[:, 0:1], 1.0))
        for j, val in enumerate([-1.0, -0.5, EPS_GN, EPS_LN, 1e-24,
                                 1.0 / HS, 127.0, 1e-5]):
            vec(lambda v, j=j, val=val: v.memset(consts[:, j:j + 1], val))
        CN1 = consts[:, 0:1]; CNH = consts[:, 1:2]; CEG = consts[:, 2:3]
        CEL = consts[:, 3:4]
        pool(lambda g: g.iota(iscr[:, 0:NSUB], [[1, NSUB]], base=0,
                              channel_multiplier=0,
                              allow_small_or_imprecise_dtypes=True),
             waits=[dv.evt()])
        pool(lambda g: g.iota(iscr[:, 8:9], [[0, 1]], base=0,
                              channel_multiplier=1,
                              allow_small_or_imprecise_dtypes=True))
        io2 = po.evt()
        vec(lambda v: v.tensor_scalar(iscr[:, 9:10], iscr[:, 8:9], 63.5, None,
                                      ALU.is_gt), waits=[io2])
        for tt in range(NTTG):
            vec(lambda v, tt=tt: v.tensor_scalar(
                iscr[:, 10:11], iscr[:, 9:10], 1.0, float(2 * tt),
                ALU.mult, ALU.add))
            vec(lambda v, tt=tt: v.scalar_tensor_tensor(
                iscr[:, 11:11 + NSUB],
                iscr[:, 10:11].to_broadcast((128, NSUB)), -1.0,
                iscr[:, 0:NSUB], ALU.mult, ALU.add))
            vec(lambda v, tt=tt: v.tensor_scalar(
                sel[:, tt, :], iscr[:, 11:11 + NSUB], 0.0, None,
                ALU.is_equal))
        consts_done = dv.evt()

        if sub == -4:
            P.do("vector", lambda v: v.engine_nop(),
                 [sl.evt(), sg.evt(), st.evt(), ac.evt(), po.evt(),
                  pe.evt(), dv.evt()], dv)
            P.finalize()
            return nc

        rbf = rows_ld
        for j in range(3):
            bb = grab()
            mm(bb, bb.ap[:, 0:512], ones1[:], rows_bf[:, j, :], True, True,
               waits=[rbf, consts_done])
            penop()
            vec(lambda v, bb=bb, j=j: v.tensor_copy(rows_bc[:, j, :],
                                                    bb.ap[:, 0:512]),
                waits=[pe.evt()])
            bb.war = dv.evt()

        if sub == -3:
            P.do("vector", lambda v: v.engine_nop(),
                 [sl.evt(), sg.evt(), st.evt(), ac.evt(), po.evt(),
                  pe.evt(), dv.evt()], dv)
            P.finalize()
            return nc
        bc_done = dv.evt()
        FAA_BC = rows_bc[:, 0, :]
        LNW_BC = rows_bc[:, 1, :]
        LNB_BC = rows_bc[:, 2, :]

        wst = sb_outer("wst", [128, 512])
        stat4 = sb_outer("stat4", [128, 4])
        stat_c = sb_outer("stat_c", [128, 1])
        stat_bf = sb_outer("stat_bf", [128, 12], BF16)
        statr = sb_outer("statr", [128, 4])
        strow = sb_outer("strow", [4, 2])
        stlo = sb_outer("stlo", [4, 1])
        swsum = sb_outer("swsum", [1, 8])
        vec(lambda v: v.memset(stat4[:], 0.0))
        for wi in range(3):
            for pc in range(8):
                dma("sync", wst[:],
                    d["wstat"][:, wi, pc * 512:(pc + 1) * 512],
                    waits=[dv.evt()])
                vec(lambda v: v.tensor_reduce(
                    stat_c[:], wst[:], AX.X, ALU.add,
                    apply_absolute_value=True), waits=[sl.evt()])
                vec(lambda v, wi=wi: v.tensor_tensor(
                    stat4[:, wi:wi + 1], stat4[:, wi:wi + 1], stat_c[:],
                    ALU.add))
        vec(lambda v: v.tensor_copy(stat_bf[:, 0:4], stat4[:]))
        vec(lambda v: v.tensor_tensor(statr[:], stat4[:],
                                      stat_bf[:, 0:4], ALU.subtract))
        vec(lambda v: v.tensor_copy(stat_bf[:, 4:8], statr[:]))
        vec(lambda v: v.tensor_tensor(stat_bf[:, 8:12], statr[:],
                                      stat_bf[:, 4:8], ALU.subtract))
        bs = grab()
        mm(bs, bs.ap[0:4, 0:1], stat_bf[:, 0:4], onesc[:, 0:1], True, True,
           waits=[dv.evt(), consts_done])
        mm(bs, bs.ap[0:4, 1:2], stat_bf[:, 4:8], onesc[:, 0:1], True, True)
        mm(bs, bs.ap[0:4, 2:3], stat_bf[:, 8:12], onesc[:, 0:1], True, True)
        penop()
        vec(lambda v, bs=bs: v.tensor_scalar(
            strow[:, 0:1], bs.ap[0:4, 0:1], MG4, MG4, ALU.add, ALU.subtract),
            waits=[pe.evt()])
        vec(lambda v, bs=bs: v.tensor_tensor(
            stlo[:], bs.ap[0:4, 0:1], strow[:, 0:1], ALU.subtract))
        vec(lambda v, bs=bs: v.tensor_tensor(
            stlo[:], stlo[:], bs.ap[0:4, 1:2], ALU.add))
        vec(lambda v, bs=bs: v.tensor_tensor(
            strow[:, 1:2], stlo[:], bs.ap[0:4, 2:3], ALU.add))
        bs.war = dv.evt()
        dma("gpsimd", stat_dram[0, 0:4], strow[:, 0], waits=[dv.evt()])
        dma("gpsimd", stat_dram[0, 4:8], strow[:, 1], waits=[dv.evt()])
        dma("gpsimd", stin[:, :], stat_dram.to_broadcast((8, 8)),
            waits=[sg.evt()])
        P.do("gpsimd", lambda g: g.collective_compute(
            "ReduceScatter", ALU.add, replica_groups=[list(range(8))],
            ins=[stin[:, :]], outs=[stout[:, :]]), [sg.evt()], cc)
        dma("gpsimd", swsum[:], stout[:, :], waits=[cc.evt()])
        swr = sb_outer("swr", [1, 24], BF16)
        swf = sb_outer("swf", [1, 8])
        swfr = sb_outer("swfr", [1, 8])
        vec(lambda v: v.tensor_tensor(
            swsum[:, 0:4], swsum[:, 0:4], swsum[:, 4:8], ALU.add),
            waits=[sg.evt()])
        vec(lambda v: v.tensor_scalar(
            swsum[:, 0:4], swsum[:, 0:4], 1.0 / (C * INNER), 1e-5, ALU.mult,
            ALU.max))
        vec(lambda v: v.tensor_copy(swf[:, 4:8], swsum[:, 0:4]))
        vec(lambda v: v.reciprocal(swsum[:, 0:4], swsum[:, 0:4]))
        vec(lambda v: v.tensor_copy(swf[:, 0:4], swsum[:, 0:4]))
        vec(lambda v: v.tensor_copy(swr[:, 0:8], swf[:]))
        vec(lambda v: v.tensor_tensor(swfr[:], swf[:], swr[:, 0:8],
                                      ALU.subtract))
        vec(lambda v: v.tensor_copy(swr[:, 8:16], swfr[:]))
        vec(lambda v: v.tensor_tensor(swr[:, 16:24], swfr[:], swr[:, 8:16],
                                      ALU.subtract))
        bs2 = grab()
        mm(bs2, bs2.ap[:, 0:24], ones1[:], swr[:], True, True,
           waits=[dv.evt()])
        penop()
        vec(lambda v, bs2=bs2: v.tensor_copy(swcol[:], bs2.ap[:, 0:8]),
            waits=[pe.evt()])
        vec(lambda v, bs2=bs2: v.tensor_tensor(
            swcol[:], swcol[:], bs2.ap[:, 8:16], ALU.add))
        vec(lambda v, bs2=bs2: v.tensor_tensor(
            swcol[:], swcol[:], bs2.ap[:, 16:24], ALU.add))
        bs2.war = dv.evt()
        sw_done = dv.evt()

        if sub == -2:
            if "swc" in dbg_out:
                dma("sync", dbg_out["swc"][:, 0:8], swcol[:], waits=[sw_done])
                dma("sync", dbg_out["swc"][:, 8:12],
                    stat4[:], waits=[sw_done])
            P.do("vector", lambda v: v.engine_nop(),
                 [sl.evt(), sg.evt(), st.evt(), ac.evt(), po.evt(),
                  pe.evt(), dv.evt()], dv)
            P.finalize()
            return nc

        esA = es.enter_context(contextlib.ExitStack())

        def sba(name, shape, dt=BF16):
            return esA.enter_context(nc.sbuf_tensor(name, shape, dt))

        w_wr = sba("w_wr", [128, 8, CH]); w_wk = sba("w_wk", [128, 8, CH])
        w_wv = sba("w_wv", [128, 8, CH]); w_wo = sba("w_wo", [128, 4, C])
        w_wrl = sba("w_wrl", [128, 8, CH]); w_wkl = sba("w_wkl", [128, 8, CH])
        w_wvl = sba("w_wvl", [128, 8, CH])
        w_maa1 = sba("w_maa1", [128, 8, 128])
        w_m2A = sba("w_m2A", [64, C])
        w_m2B = sba("w_m2B", [64, C])
        w_g1 = sba("w_g1", [128, 8, 128]); w_g2 = sba("w_g2", [128, CH])
        w_d1 = sba("w_d1", [128, 8, 64]); w_d2 = sba("w_d2", [65, CH])
        w_ra1 = sba("w_ra1", [128, 8, 48]); w_rb1 = sba("w_rb1", [128, 8, 48])
        w_ra2 = sba("w_ra2", [65, 1024]); w_rb2 = sba("w_rb2", [65, 1024])
        for dst, name in ((w_wr, "wrT"), (w_wk, "wkT"), (w_wv, "wvT"),
                          (w_wo, "woT"), (w_maa1, "maa1T"), (w_g1, "gate1T"),
                          (w_d1, "dec1T"), (w_ra1, "r16aT"), (w_rb1, "r16bT")):
            dma("gpsimd", dst[:], d[name].rearrange("(c p) o -> p c o", p=128))
        dma("gpsimd", w_m2A[:], d["maa2"][0:2].rearrange("g k c -> (g k) c"))
        dma("gpsimd", w_m2B[:], d["maa2"][2:4].rearrange("g k c -> (g k) c"))
        dma("gpsimd", w_g2[:], d["gate2"])
        dma("gpsimd", w_d2[:], d["dec2b"])
        dma("gpsimd", w_ra2[:], d["r16a2"])
        dma("gpsimd", w_rb2[:], d["r16b2"])
        esA0 = contextlib.ExitStack()
        wfp = esA0.enter_context(nc.sbuf_tensor("wfp", [128, 8, CH], F32))
        for hi_t, lo_t, name in ((w_wr, w_wrl, "wrT"), (w_wk, w_wkl, "wkT"),
                                 (w_wv, w_wvl, "wvT")):
            dma("sync", wfp[:], d[name].rearrange("(c p) o -> p c o", p=128),
                waits=[dv.evt()])
            vec(lambda v, hi_t=hi_t, lo_t=lo_t: v.tensor_tensor(
                lo_t[:], wfp[:], hi_t[:], ALU.subtract),
                waits=[sl.evt(), sg.evt()])
        esA0.close()
        wld = sg.evt()
        wlo_done = dv.evt()

        if sub == -1:
            P.do("vector", lambda v: v.engine_nop(),
                 [sl.evt(), sg.evt(), st.evt(), ac.evt(), po.evt(),
                  pe.evt(), dv.evt()], dv)
            P.finalize()
            return nc

        NTT = CK // 128
        xf = sba("xf", [128, 8, CK + 1], F32)
        xm0 = sba("xm0", [128, 8, CK], F32)
        xm1 = sba("xm1", [128, 8, CK], F32)
        xxt = sba("xxt", [128, 8, CK])
        xmh = sba("xmh", [128, 8, CK])
        xml = sba("xml", [128, 8, CK])
        loA = sba("loA", [64, CK]); loB = sba("loB", [64, CK])
        g_in = sba("g_in", [128, CK])
        dec_in = sba("dec_in", [65, CK]); ra_in = sba("ra_in", [65, CK])
        rb_in = sba("rb_in", [65, CK])
        dec_inl = sba("dec_inl", [65, CK])
        dec_f = sba("dec_f", [64, CK], F32)
        vec(lambda v: v.memset(dec_in[:], 0.0))
        vec(lambda v: v.memset(ra_in[:], 0.0))
        vec(lambda v: v.memset(rb_in[:], 0.0))
        vec(lambda v: v.memset(dec_inl[:], 0.0))
        vec(lambda v: v.memset(dec_in[64:65, :], 1.0))
        vec(lambda v: v.memset(ra_in[64:65, :], 1.0))
        vec(lambda v: v.memset(rb_in[64:65, :], 1.0))
        inner_init = dv.evt()

        def tokf(name):
            return sba(name, [128, NTT, CH], F32)

        def tokb(name):
            return sba(name, [128, NTT, CH])

        rtok = tokf("rtok"); gtok = tokb("gtok")
        ktmp = tokf("ktmp"); kkn = tokf("kkn"); kfin = tokf("kfin")
        asig = tokf("asig"); masig = tokf("masig"); mksig = tokf("mksig")
        ept = tokb("ept"); emt = tokb("emt"); epvt = tokb("epvt")
        sdec = tokf("sdec"); scr1 = tokf("scr1")
        ke = mksig
        ytok = sdec
        vtok = tokb("vtok")
        gamh = tokb("gamh"); gaml = tokb("gaml")
        atil = tokb("atil"); btil = tokb("btil")
        ktil = tokb("ktil"); qtil = tokb("qtil")
        ygt = tokb("ygt")

        s8 = lambda n: sba(n, [128, NTT, 8], F32)
        mu8, var8, rstd8, rk8 = s8("mu8"), s8("var8"), s8("rstd8"), s8("rk8")
        dlt = sba("dlt", [128, 4, NSUB], F32)

        AFt = sba("AFt", [128, 4, NTT, 128])
        BFt = sba("BFt", [128, 4, NTT, 128])
        KFt = sba("KFt", [128, 4, NTT, 128])
        QFt = sba("QFt", [128, 4, NTT, 128])

        m8 = lambda n: sba(n, [128, 8, 64])
        Tsb, Ttsb, Usb, QBsb, QKsb, Nsb, Msb = map(
            m8, ["Tsb", "Ttsb", "Usb", "QBsb", "QKsb", "Nsb", "Msb"])
        Xsb = sba("Xsb", [128, 8, 128])
        Xlo = sba("Xlo", [128, 8, 128])
        Wh = sba("Wh", [128, 4, 64]); Mc = sba("Mc", [128, 4, 64])
        Zf = esA.enter_context(nc.sbuf_tensor("Zf", [128, 4, 64], F32))
        Zhi = sba("Zhi", [128, 4, 64]); Zlo = sba("Zlo", [128, 4, 64])
        vec(lambda v: v.memset(Zf[:], 0.0))
        vec(lambda v: v.memset(Zhi[:], 0.0))
        vec(lambda v: v.memset(Zlo[:], 0.0))
        z_evt = dv.evt()

        NBAR = 5
        cc_evts = []
        prev_h = None

        for ci in range(nchunks):
            if prev_h is None:
                bw_load, bw_mix, bw_act = [], [], []
            else:
                bw_load = [prev_h["pdv"]]
                bw_mix = [prev_h["ppe"], prev_h["ppo"]]
                bw_act = [prev_h["pdv"]]
                pool(lambda g: g.engine_nop(), waits=[prev_h["pdv"]])
            if ci == 0:
                vec(lambda v: v.memset(xf[:, :, 0:1], 0.0))
                dma("gpsimd", xf[:, :, 1:CK + 1],
                    d["xT"].rearrange("(c p) t -> p c t", p=128)[:, :, 0:CK],
                    waits=[inner_init])
            else:
                dma("gpsimd", xf[:, :, 0:CK + 1],
                    d["xT"].rearrange("(c p) t -> p c t", p=128)[
                        :, :, ci * CK - 1:ci * CK + CK], waits=bw_load)
            xr = sg.evt()

            vec(lambda v: v.tensor_tensor(xxt[:], xf[:, :, 0:CK],
                                          xf[:, :, 1:CK + 1], ALU.subtract),
                waits=[xr, consts_done, bc_done] + bw_mix)
            for cc2 in range(8):
                vec(lambda v, cc2=cc2: v.scalar_tensor_tensor(
                    xmh[:, cc2, :], xxt[:, cc2, :], tmx[:, cc2:cc2 + 1],
                    xf[:, cc2, 1:CK + 1], ALU.mult, ALU.add), waits=[vload])
            mix_evt = dv.evt()

            blo = grab()
            for cc2 in range(8):
                mm(blo, blo.ap[:, 0:CK], w_maa1[:, cc2, :], xmh[:, cc2, :],
                   cc2 == 0, cc2 == 7,
                   waits=[mix_evt, wld] if cc2 == 0 else [])
            penop()
            act(lambda s, blo=blo: s.activation(
                out=loA[:], in_=blo.ap[0:64, 0:CK], func=AF.Tanh),
                waits=[pe.evt()] + bw_act)
            act(lambda s, blo=blo: s.activation(
                out=loB[:], in_=blo.ap[64:128, 0:CK], func=AF.Tanh))
            blo.war = ac.evt()
            lo_evt = ac.evt()
            lo_done_pe = pe.evt()

            def emit_lo2_xm(gi, xmbuf, extra_wait):
                lot = loA if gi < 2 else loB
                ksl = slice((gi % 2) * 32, (gi % 2) * 32 + 32)
                for cc2 in range(8):
                    bb = grab()
                    wm2 = w_m2A if gi < 2 else w_m2B
                    mm(bb, bb.ap[:, 0:CK],
                       wm2[(gi % 2) * 32:(gi % 2) * 32 + 32,
                           cc2 * 128:(cc2 + 1) * 128],
                       lot[ksl, :], True, True, waits=[lo_evt])
                    penop()
                    w0 = [pe.evt()]
                    if cc2 == 0:
                        w0.append(lo_done_pe)
                        if extra_wait is not None:
                            w0.append(extra_wait)
                    act(lambda s, bb=bb, cc2=cc2, gi=gi, xmbuf=xmbuf:
                        s.activation(
                            out=xmbuf[:, cc2, :], in_=bb.ap[:, 0:CK],
                            func=AF.Identity, bias=tmg[:, cc2, gi:gi + 1]),
                        waits=w0)
                    bb.war = ac.evt()
                lo2_evt = ac.evt()
                pool(lambda g, xmbuf=xmbuf: g.tensor_tensor(
                    xmbuf[:], xxt[:], xmbuf[:], ALU.mult), waits=[lo2_evt])
                vec(lambda v, xmbuf=xmbuf: v.tensor_tensor(
                    xmbuf[:], xf[:, :, 1:CK + 1], xmbuf[:], ALU.add),
                    waits=[po.evt()])
                xm_f = dv.evt()
                act(lambda s, xmbuf=xmbuf: s.copy(xmh[:], xmbuf[:]),
                    waits=[xm_f] + ([extra_wait] if extra_wait else []))
                vec(lambda v, xmbuf=xmbuf: v.tensor_tensor(
                    xml[:], xmbuf[:], xmh[:], ALU.subtract),
                    waits=[ac.evt()])
                return dv.evt()

            def proj512(xmbuf, wtile, dst, e_in, wlo=None):
                for tt in range(NTT):
                    bp = grab()
                    for cc2 in range(8):
                        mm(bp, bp.ap[:, 0:512],
                           xmh[:, cc2, tt * 128:(tt + 1) * 128],
                           wtile[:, cc2, :], cc2 == 0, False,
                           waits=[e_in, wlo_done] if cc2 == 0 else [])
                        mm(bp, bp.ap[:, 0:512],
                           xml[:, cc2, tt * 128:(tt + 1) * 128],
                           wtile[:, cc2, :], False, False)
                        if wlo is not None:
                            mm(bp, bp.ap[:, 0:512],
                               xmh[:, cc2, tt * 128:(tt + 1) * 128],
                               wlo[:, cc2, :], False, cc2 == 7)
                        elif cc2 == 7:
                            mm(bp, bp.ap[:, 0:512],
                               xml[:, cc2, tt * 128:(tt + 1) * 128],
                               wtile[:, cc2, :], False, True, skip=True)
                    penop()
                    act(lambda s, bp=bp, tt=tt, dst=dst: s.copy(
                        dst[:, tt, :], bp.ap[:, 0:512]), waits=[pe.evt()])
                    bp.war = ac.evt()
                return ac.evt()

            e0 = emit_lo2_xm(0, xm1, lo_done_pe)
            r_evt = proj512(xm1, w_wr, rtok, e0, w_wrl)
            bg = grab()
            for cc2 in range(8):
                mm(bg, bg.ap[:, 0:CK], w_g1[:, cc2, :], xmh[:, cc2, :],
                   cc2 == 0, cc2 == 7, waits=[e0] if cc2 == 0 else [])
            penop()
            act(lambda s, bg=bg: s.activation(out=g_in[:], in_=bg.ap[:, 0:CK],
                                              func=AF.Tanh), waits=[pe.evt()])
            bg.war = ac.evt()
            gin_evt = ac.evt()
            for tt in range(NTT):
                bgo = grab()
                mm(bgo, bgo.ap[:, 0:512], g_in[:, tt * 128:(tt + 1) * 128],
                   w_g2[:], True, True, waits=[gin_evt])
                penop()
                act(lambda s, bgo=bgo, tt=tt: s.copy(gtok[:, tt, :],
                                                     bgo.ap[:, 0:512]),
                    waits=[pe.evt()])
                bgo.war = ac.evt()
            xm1_free = pe.evt()

            e1 = emit_lo2_xm(1, xm0, xm1_free)
            bd = grab()
            for cc2 in range(8):
                mm(bd, bd.ap[0:64, 0:CK], w_d1[:, cc2, :], xmh[:, cc2, :],
                   cc2 == 0, cc2 == 7, waits=[e1] if cc2 == 0 else [])
            penop()
            act(lambda s, bd=bd: s.activation(out=dec_f[:],
                                              in_=bd.ap[0:64, 0:CK],
                                              func=AF.Tanh), waits=[pe.evt()])
            bd.war = ac.evt()
            act(lambda s: s.copy(dec_in[0:64, :], dec_f[:]))
            vec(lambda v: v.tensor_tensor(dec_inl[0:64, :], dec_f[:],
                                          dec_in[0:64, :], ALU.subtract),
                waits=[ac.evt()])
            din_evt = dv.evt()
            bra = grab()
            for cc2 in range(8):
                mm(bra, bra.ap[0:48, 0:CK], w_ra1[:, cc2, :], xmh[:, cc2, :],
                   cc2 == 0, cc2 == 7, waits=[e1] if cc2 == 0 else [])
            penop()
            act(lambda s, bra=bra: s.copy(ra_in[0:48, :], bra.ap[0:48, 0:CK]),
                waits=[pe.evt()])
            bra.war = ac.evt()
            rain_evt = ac.evt()
            for tt in range(NTT):
                bdo = grab()
                mm(bdo, bdo.ap[:, 0:512], dec_in[:, tt * 128:(tt + 1) * 128],
                   w_d2[:], True, False, waits=[din_evt])
                mm(bdo, bdo.ap[:, 0:512], dec_inl[:, tt * 128:(tt + 1) * 128],
                   w_d2[:], False, True)
                penop()
                act(lambda s, bdo=bdo, tt=tt: s.activation(
                    out=sdec[:, tt, :], in_=bdo.ap[:, 0:512],
                    func=AF.Sigmoid), waits=[pe.evt()])
                bdo.war = ac.evt()
            for tt in range(NTT):
                ba1 = grab()
                mm(ba1, ba1.ap[:, 0:512], ra_in[:, tt * 128:(tt + 1) * 128],
                   w_ra2[:, 0:512], True, True, waits=[rain_evt])
                penop()
                act(lambda s, ba1=ba1, tt=tt: s.activation(
                    out=asig[:, tt, :], in_=ba1.ap[:, 0:512],
                    func=AF.Sigmoid), waits=[pe.evt()])
                ba1.war = ac.evt()
                ba2 = grab()
                mm(ba2, ba2.ap[:, 0:512], ra_in[:, tt * 128:(tt + 1) * 128],
                   w_ra2[:, 512:1024], True, True)
                penop()
                act(lambda s, ba2=ba2, tt=tt: s.activation(
                    out=masig[:, tt, :], in_=ba2.ap[:, 0:512],
                    func=AF.Sigmoid), waits=[pe.evt()])
                ba2.war = ac.evt()
            sdec_evt = ac.evt()
            xm0_free = pe.evt()

            vec(lambda v: v.tensor_scalar(gamh[:], sdec[:], -EXPM05, None,
                                          ALU.mult), waits=[sdec_evt])
            vec(lambda v: v.scalar_tensor_tensor(
                gaml[:], sdec[:], -EXPM05, gamh[:], ALU.mult, ALU.subtract))
            gam_evt = dv.evt()

            e2 = emit_lo2_xm(2, xm1, xm0_free)
            k_evt = proj512(xm1, w_wk, ktmp, e2, w_wkl)
            brb = grab()
            for cc2 in range(8):
                mm(brb, brb.ap[0:48, 0:CK], w_rb1[:, cc2, :], xmh[:, cc2, :],
                   cc2 == 0, cc2 == 7, waits=[e2] if cc2 == 0 else [])
            penop()
            act(lambda s, brb=brb: s.activation(out=rb_in[0:16, :],
                                                in_=brb.ap[0:16, 0:CK],
                                                func=AF.Tanh),
                waits=[pe.evt()])
            act(lambda s, brb=brb: s.copy(rb_in[32:48, :],
                                          brb.ap[32:48, 0:CK]))
            brb.war = ac.evt()
            rbin_evt = ac.evt()
            for tt in range(NTT):
                bb1 = grab()
                mm(bb1, bb1.ap[:, 0:512], rb_in[:, tt * 128:(tt + 1) * 128],
                   w_rb2[:, 0:512], True, True, waits=[rbin_evt])
                penop()
                act(lambda s, bb1=bb1, tt=tt: s.copy(scr1[:, tt, :],
                                                     bb1.ap[:, 0:512]),
                    waits=[pe.evt()])
                bb1.war = ac.evt()
                bb2 = grab()
                mm(bb2, bb2.ap[:, 0:512], rb_in[:, tt * 128:(tt + 1) * 128],
                   w_rb2[:, 512:1024], True, True)
                penop()
                act(lambda s, bb2=bb2, tt=tt: s.activation(
                    out=mksig[:, tt, :], in_=bb2.ap[:, 0:512],
                    func=AF.Sigmoid), waits=[pe.evt()])
                bb2.war = ac.evt()
            kkk_evt = ac.evt()
            xm1_free = pe.evt()

            e3 = emit_lo2_xm(3, xm0, xm1_free)
            v_evt = proj512(xm0, w_wv, vtok, e3, w_wvl)
            xm0_free = pe.evt()

            vec(lambda v: v.tensor_tensor(kkn[:], ktmp[:], scr1[:], ALU.add),
                waits=[k_evt, kkk_evt]
                + ([prev_h["po"], prev_h["yg"]] if prev_h else []))
            vec(lambda v: v.tensor_tensor(scr1[:], kkn[:], kkn[:], ALU.mult))
            vec(lambda v: v.tensor_reduce(
                rk8[:], scr1[:].rearrange("p a (h s) -> p a h s", s=64),
                AX.X, ALU.add))
            vec(lambda v: v.tensor_scalar(rk8[:], rk8[:], 1e-24, None,
                                          ALU.max))
            act(lambda s: s.activation(out=rk8[:], in_=rk8[:], func=AF.Ln),
                waits=[dv.evt()])
            act(lambda s: s.activation(out=rk8[:], in_=rk8[:], func=AF.Exp,
                                       scale=CNH))
            vec(lambda v: v.tensor_tensor(
                kkn[:].rearrange("p a (h s) -> p a h s", s=64),
                kkn[:].rearrange("p a (h s) -> p a h s", s=64),
                rk8[:].rearrange("p a (h o) -> p a h o", o=1).to_broadcast(
                    (128, NTT, 8, 64)), ALU.mult), waits=[ac.evt()])
            kkn_evt = dv.evt()
            act(lambda s: s.activation(out=sdec[:], in_=sdec[:], func=AF.Ln),
                waits=[gam_evt])
            lns_evt = ac.evt()
            vec(lambda v: v.scalar_tensor_tensor(
                scr1[:], sdec[:], -0.5, mksig[:], ALU.add, ALU.mult),
                waits=[lns_evt, kkn_evt])
            act(lambda s: s.activation(out=ke[:], in_=scr1[:], func=AF.Exp),
                waits=[dv.evt()])
            ke_evt = ac.evt()
            vec(lambda v: v.tensor_scalar(scr1[:], masig[:], -1.0, 1.0,
                                          ALU.mult, ALU.add),
                waits=[ke_evt])
            pool(lambda g: g.tensor_tensor(scr1[:], asig[:], scr1[:],
                                           ALU.mult), waits=[dv.evt()])
            vec(lambda v: v.tensor_tensor(scr1[:], masig[:], scr1[:],
                                          ALU.add), waits=[po.evt()])
            pool(lambda g: g.tensor_tensor(kfin[:], ktmp[:], scr1[:],
                                           ALU.mult), waits=[dv.evt()])
            vec(lambda v: v.tensor_tensor(kfin[:], kfin[:], ke[:], ALU.mult),
                waits=[po.evt(), ke_evt])
            kfin_evt = dv.evt()

            for tt in range(NTT):
                bG = grab()
                for hf in range(2):
                    rb0 = hf * 64
                    rsl0 = slice(rb0, rb0 + 64)
                    mm(bG, bG.ap[rsl0, 0:512], tri[rsl0, :],
                       gamh[rsl0, tt, :], True, False, tp=(rb0, rb0),
                       waits=[gam_evt] if hf == 0 else [])
                    mm(bG, bG.ap[rsl0, 0:512], tri[rsl0, :],
                       gaml[rsl0, tt, :], False, True, tp=(rb0, rb0))
                penop()
                gG = pe.evt()
                act(lambda s, bG=bG, tt=tt: s.activation(
                    out=ept[:, tt, :], in_=bG.ap[:, 0:512], func=AF.Exp),
                    waits=[gG])
                act(lambda s, bG=bG, tt=tt: s.activation(
                    out=emt[:, tt, :], in_=bG.ap[:, 0:512], func=AF.Exp,
                    scale=CN1))
                vec(lambda v, bG=bG, tt=tt: v.scalar_tensor_tensor(
                    ytok[:, tt, :], gamh[:, tt, :], -1.0, bG.ap[:, 0:512],
                    ALU.mult, ALU.add), waits=[gG])
                vec(lambda v, tt=tt: v.tensor_tensor(
                    ytok[:, tt, :], ytok[:, tt, :], gaml[:, tt, :],
                    ALU.subtract))
                act(lambda s, tt=tt: s.activation(
                    out=epvt[:, tt, :], in_=ytok[:, tt, :], func=AF.Exp),
                    waits=[dv.evt()])
                bG.war = ac.evt()
            ef_evt = ac.evt()

            bDL = grab()
            for p in range(NPAIR):
                csl = slice(p * 128, (p + 1) * 128)
                for tt in range(NTT):
                    mm(bDL, bDL.ap[:, p * NSUB:(p + 1) * NSUB],
                       gamh[:, tt, csl], sel[:, tt, :], tt == 0, False,
                       waits=[gam_evt, consts_done]
                       if (p == 0 and tt == 0) else [])
                    mm(bDL, bDL.ap[:, p * NSUB:(p + 1) * NSUB],
                       gaml[:, tt, csl], sel[:, tt, :], False, tt == NTT - 1)
            penop()
            act(lambda s, bDL=bDL: s.activation(
                out=dlt[:].rearrange("p a e -> p (a e)"),
                in_=bDL.ap[:, 0:4 * NSUB], func=AF.Exp), waits=[pe.evt()])
            bDL.war = ac.evt()
            dl_evt = ac.evt()

            vec(lambda v: v.tensor_tensor(atil[:], kkn[:], epvt[:], ALU.mult),
                waits=[ef_evt, kkn_evt])
            vec(lambda v: v.scalar_tensor_tensor(
                scr1[:], kkn[:], -1.0, asig[:], ALU.mult, ALU.mult),
                waits=[kfin_evt])
            pool(lambda g: g.tensor_tensor(btil[:], scr1[:], emt[:],
                                           ALU.mult),
                 waits=[ef_evt, dv.evt(), po.evt()])
            btil_evt = po.evt()
            vec(lambda v: v.tensor_tensor(ktil[:], kfin[:], emt[:],
                                          ALU.mult), waits=[kfin_evt])
            pool(lambda g: g.tensor_tensor(qtil[:], rtok[:], ept[:],
                                           ALU.mult), waits=[r_evt, ef_evt])
            sc_dv = dv.evt()
            sc_po = po.evt()

            for (src_, dst) in ((atil, AFt), (btil, BFt), (ktil, KFt),
                                (qtil, QFt)):
                for tt in range(NTT):
                    dma("sync", dst[:, :, tt, :],
                        src_[:, tt, :].rearrange("p (t x) -> p t x", x=128),
                        waits=[sc_dv, sc_po]
                        + ([prev_h["wo_pe"]] if prev_h else []),
                        transpose=True)
            tr_evt = st.evt()

            prescan_cap = {"pdv": dv.evt(), "ppe": pe.evt(),
                           "ppo": po.evt()}

            z_evt_g = [z_evt]
            b7hi_free = [None]
            y_free = [prev_h["yg"] if prev_h else None]

            def make_sub(s_i):
                tt = s_i // 2
                rb0 = (s_i % 2) * 64
                rsl = slice(rb0, rb0 + 64)
                bX = [banks[5], banks[6]]
                st = {"nsrc": Tsb, "msrc": Ttsb}

                def mbc(mk):
                    return mk[rsl, :].unsqueeze(1).to_broadcast((64, 8, 64))

                def pv(bb):
                    return bb.ap[rsl, 0:512].rearrange("p (a s) -> p a s",
                                                       s=64)

                def evac_X(w, lo=False):
                    vec(lambda v: v.tensor_copy(
                        Xsb[rsl, 0:4, :].rearrange("p a x -> p (a x)"),
                        bX[0].ap[rsl, 0:512]), waits=w)
                    act(lambda s: s.copy(
                        Xsb[rsl, 4:8, :].rearrange("p a x -> p (a x)"),
                        bX[1].ap[rsl, 0:512]), waits=w)
                    if lo:
                        hi_done = ac.evt()
                        vec(lambda v: v.tensor_tensor(
                            Xlo[rsl, 0:4, :].rearrange("p a x -> p (a x)"),
                            bX[0].ap[rsl, 0:512],
                            Xsb[rsl, 0:4, :].rearrange("p a x -> p (a x)"),
                            ALU.subtract))
                        vec(lambda v: v.tensor_tensor(
                            Xlo[rsl, 4:8, :].rearrange("p a x -> p (a x)"),
                            bX[1].ap[rsl, 0:512],
                            Xsb[rsl, 4:8, :].rearrange("p a x -> p (a x)"),
                            ALU.subtract), waits=[hi_done])
                    return [dv.evt(), ac.evt()]

                def stage_g(extra_w=()):
                    bT, bTt, bU, bQB, bQK = (banks[0], banks[1], banks[2],
                                             banks[3], banks[4])
                    for p in range(NPAIR):
                        for h2 in range(2):
                            i8 = p * 2 + h2
                            hsl = slice(h2 * 64, h2 * 64 + 64)
                            col = slice(i8 * 64, i8 * 64 + 64)
                            AFs = AFt[hsl, p, tt, rb0:rb0 + 64]
                            BFs = BFt[hsl, p, tt, rb0:rb0 + 64]
                            KFs = KFt[hsl, p, tt, rb0:rb0 + 64]
                            QFs = QFt[hsl, p, tt, rb0:rb0 + 64]
                            w0 = [tr_evt] + list(extra_w) if i8 == 0 else []
                            mm(bT, bT.ap[rsl, col], AFs, BFs, True, True,
                               tp=(h2 * 64, rb0), waits=w0)
                            mm(bTt, bTt.ap[rsl, col], BFs, AFs, True, True,
                               tp=(h2 * 64, rb0))
                            mm(bU, bU.ap[rsl, col], KFs, AFs, True, True,
                               tp=(h2 * 64, rb0))
                            mm(bQB, bQB.ap[rsl, col], BFs, QFs, True, True,
                               tp=(h2 * 64, rb0))
                            mm(bQK, bQK.ap[rsl, col], KFs, QFs, True, True,
                               tp=(h2 * 64, rb0))
                    penop()
                    gram_evt = pe.evt()
                    vec(lambda v: v.tensor_tensor(
                        Tsb[rsl, :, :], pv(bT), mbc(maskS), ALU.mult),
                        waits=[gram_evt])
                    vec(lambda v: v.tensor_tensor(
                        Ttsb[rsl, :, :], pv(bTt), mbc(maskSU), ALU.mult))
                    vec(lambda v: v.tensor_tensor(
                        Usb[rsl, :, :], pv(bU), mbc(maskSU), ALU.mult))
                    vec(lambda v: v.tensor_tensor(
                        QBsb[rsl, :, :], pv(bQB), mbc(maskIU), ALU.mult))
                    vec(lambda v: v.tensor_tensor(
                        QKsb[rsl, :, :], pv(bQK), mbc(maskIU), ALU.mult))
                    st["gram_sb"] = dv.evt()
                    st["pw"] = [st["gram_sb"]]

                def stage_x0():
                    for bXp in bX:
                        mm(bXp, bXp.ap[rsl, 0:512], z1[0:1, 0:64],
                           zr[0:1, :], True, True, tp=(0, rb0),
                           waits=[st["gram_sb"], v_evt])
                    for p in range(NPAIR):
                        bXp = bX[p // 2]
                        xo = (p % 2) * 256
                        for h2 in range(2):
                            i8 = p * 2 + h2
                            base = p * 128 + h2 * 64
                            mm(None, bXp.ap[rsl, xo + h2 * 128:
                                            xo + h2 * 128 + 64],
                               Usb[rsl, i8, :], vtok[rsl, tt, base:base + 64],
                               False, False, tp=(rb0, rb0), skip=True)
                            mm(None, bXp.ap[rsl, xo + h2 * 128 + 64:
                                            xo + h2 * 128 + 128],
                               ident[rsl, :], atil[rsl, tt, base:base + 64],
                               False, False, tp=(rb0, rb0), skip=True)
                    penop()
                    st["xe"] = evac_X([pe.evt()])

                def stage_lvl(lvl):
                    nsrc, msrc = st["nsrc"], st["msrc"]
                    for p in range(NPAIR):
                        bXp = bX[p // 2]
                        xo = (p % 2) * 256
                        for h2 in range(2):
                            i8 = p * 2 + h2
                            mm(None, bXp.ap[rsl, xo + h2 * 128:
                                            xo + h2 * 128 + 128],
                               msrc[rsl, i8, :], Xsb[rsl, i8, :],
                               False, False, tp=(rb0, rb0),
                               waits=st["xe"] + st["pw"]
                               if i8 == 0 and p == 0 else [],
                               skip=True)
                    penop()
                    ap_evt = pe.evt()
                    if lvl < 5:
                        bN, bM = banks[0], banks[1]
                        for i8 in range(8):
                            col = slice(i8 * 64, i8 * 64 + 64)
                            mm(bN, bN.ap[rsl, col], msrc[rsl, i8, :],
                               nsrc[rsl, i8, :], True, True, tp=(rb0, rb0),
                               waits=st["pw"] if i8 == 0 else [])
                            mm(bM, bM.ap[rsl, col], nsrc[rsl, i8, :],
                               msrc[rsl, i8, :], True, True, tp=(rb0, rb0))
                        penop()
                        sq_evt = pe.evt()
                        vec(lambda v: v.tensor_copy(Nsb[rsl, :, :], pv(bN)),
                            waits=[sq_evt])
                        act(lambda s: s.copy(Msb[rsl, :, :], pv(bM)),
                            waits=[sq_evt])
                        st["pw"] = [dv.evt(), ac.evt()]
                        st["nsrc"], st["msrc"] = Nsb, Msb
                    st["xe"] = evac_X([ap_evt], lo=(lvl == 5))

                def stage_wy():
                    b7 = banks[7]
                    for p in range(NPAIR):
                        for h2 in range(2):
                            i8 = p * 2 + h2
                            hsl = slice(h2 * 64, h2 * 64 + 64)
                            w0 = (st["xe"] + [st["gram_sb"], z_evt_g[0]]
                                  + ([b7hi_free[0]] if b7hi_free[0] else [])
                                  if i8 == 0 else [])
                            mm(b7, b7.ap[hsl, 256 + p * 64:
                                           256 + (p + 1) * 64],
                               Xsb[rsl, i8, 64:128], QBsb[rsl, i8, :],
                               True, True, tp=(rb0, h2 * 64), waits=w0)
                    penop()
                    vec(lambda v: v.tensor_tensor(
                        Wh[:], banks[7].ap[:, 256:512].rearrange(
                            "p (a s) -> p a s", s=64),
                        QFt[:, :, tt, rb0:rb0 + 64], ALU.add),
                        waits=[pe.evt()])
                    st["wh"] = dv.evt()
                    bY = banks[3]
                    mm(bY, bY.ap[rsl, 0:512], z1[0:1, 0:64], zr[0:1, :],
                       True, True, tp=(0, rb0),
                       waits=st["xe"] + ([y_free[0]] if y_free[0] else []))
                    for p in range(NPAIR):
                        for h2 in range(2):
                            i8 = p * 2 + h2
                            base = p * 128 + h2 * 64
                            ycol = slice(base, base + 64)
                            mm(None, bY.ap[rsl, ycol], QBsb[rsl, i8, :],
                               Xsb[rsl, i8, 0:64], False, False,
                               tp=(rb0, rb0), skip=True)
                            mm(None, bY.ap[rsl, ycol], QBsb[rsl, i8, :],
                               Xlo[rsl, i8, 0:64], False, False,
                               tp=(rb0, rb0), skip=True)
                            mm(None, bY.ap[rsl, ycol], QKsb[rsl, i8, :],
                               vtok[rsl, tt, base:base + 64], False, False,
                               tp=(rb0, rb0), skip=True)
                    for p in range(NPAIR):
                        for h2 in range(2):
                            i8 = p * 2 + h2
                            hsl = slice(h2 * 64, h2 * 64 + 64)
                            base = p * 128 + h2 * 64
                            mm(None, bY.ap[rsl, base:base + 64], Wh[hsl, p, :],
                               Zhi[hsl, p, :], False, False,
                               tp=(h2 * 64, rb0),
                               waits=[st["wh"], z_evt_g[0]] if i8 == 0
                               else [], skip=True)
                    penop()
                    act(lambda s: s.copy(ytok[rsl, tt, :],
                                         banks[3].ap[rsl, 0:512]),
                        waits=[pe.evt()])
                    y_free[0] = ac.evt()

                def stage_pz():
                    b7 = banks[7]
                    for p in range(NPAIR):
                        for h2 in range(2):
                            i8 = p * 2 + h2
                            hsl = slice(h2 * 64, h2 * 64 + 64)
                            mm(b7, b7.ap[hsl, 256 + p * 64:
                                           256 + (p + 1) * 64],
                               Xsb[rsl, i8, 64:128],
                               btil[rsl, tt, base_of(p, h2)], True, True,
                               tp=(rb0, h2 * 64),
                               waits=st["xe"] + [st["wh"]] if i8 == 0 else [])
                    penop()
                    vec(lambda v: v.tensor_tensor(
                        Mc[:], b7.ap[:, 256:512].rearrange(
                            "p (a s) -> p a s", s=64),
                        ident[:].unsqueeze(1).to_broadcast((128, 4, 64)),
                        ALU.add), waits=[pe.evt()])
                    mc_evt = dv.evt()
                    b7hi_free[0] = mc_evt
                    mm(b7, b7.ap[:, 0:256], z1[:], zr[0:1, 0:256],
                       True, True, waits=st["xe"] + [mc_evt, z_evt_g[0]])
                    for p in range(NPAIR):
                        for h2 in range(2):
                            i8 = p * 2 + h2
                            hsl = slice(h2 * 64, h2 * 64 + 64)
                            zcol = slice(p * 64, (p + 1) * 64)
                            base = p * 128 + h2 * 64
                            mm(None, b7.ap[hsl, zcol],
                               btil[rsl, tt, base:base + 64],
                               Xsb[rsl, i8, 0:64],
                               False, False, tp=(rb0, h2 * 64), skip=True)
                            mm(None, b7.ap[hsl, zcol],
                               btil[rsl, tt, base:base + 64],
                               Xlo[rsl, i8, 0:64],
                               False, False, tp=(rb0, h2 * 64), skip=True)
                            mm(None, b7.ap[hsl, zcol],
                               ktil[rsl, tt, base:base + 64],
                               vtok[rsl, tt, base:base + 64], False, False,
                               tp=(rb0, h2 * 64), skip=True)
                            mm(None, b7.ap[hsl, zcol], Mc[hsl, p, :],
                               Zhi[hsl, p, :], False, False,
                               tp=(h2 * 64, h2 * 64), skip=True)
                            mm(None, b7.ap[hsl, zcol], Mc[hsl, p, :],
                               Zlo[hsl, p, :], False, False,
                               tp=(h2 * 64, h2 * 64), skip=True)
                    penop()
                    zup_evt = pe.evt()
                    vec(lambda v: v.tensor_tensor(
                        Zf[:], b7.ap[:, 0:256].rearrange("p (a s) -> p a s",
                                                         s=64),
                        dlt[:, :, s_i:s_i + 1].to_broadcast((128, 4, 64)),
                        ALU.mult), waits=[zup_evt, dl_evt])
                    vec(lambda v: v.tensor_copy(Zhi[:], Zf[:]))
                    vec(lambda v: v.tensor_tensor(Zlo[:], Zf[:], Zhi[:],
                                                  ALU.subtract))
                    z_evt_g[0] = dv.evt()

                def base_of(p, h2):
                    base = p * 128 + h2 * 64
                    return slice(base, base + 64)

                st["g"], st["x0"], st["lvl"] = stage_g, stage_x0, stage_lvl
                st["wy"], st["pz"] = stage_wy, stage_pz
                return st

            for pk in range(0, NSUB, 2):
                sa = make_sub(pk)
                sb_ = make_sub(pk + 1)
                pair_w = [dv.evt(), ac.evt()] if pk else []
                sa["g"](pair_w)
                sb_["g"]()
                sa["x0"]()
                sb_["x0"]()
                for lvl in range(6):
                    sa["lvl"](lvl)
                    sb_["lvl"](lvl)
                sa["wy"]()
                sa["pz"]()
                sb_["wy"]()
                sb_["pz"]()

            z_evt = z_evt_g[0]
            y_all = ac.evt()

            y4 = ytok[:].rearrange("p a (h s) -> p a h s", s=64)

            def bch(t_):
                return t_[:].rearrange("p a (h o) -> p a h o",
                                       o=1).to_broadcast((128, NTT, 8, 64))

            vec(lambda v: v.tensor_reduce(mu8[:], y4, AX.X, ALU.add),
                waits=[y_all])
            vec(lambda v: v.tensor_scalar(mu8[:], mu8[:], 1.0 / HS, None,
                                          ALU.mult))
            pool(lambda g: g.tensor_tensor(scr1[:], ytok[:], ytok[:],
                                           ALU.mult), waits=[y_all, dv.evt()])
            vec(lambda v: v.tensor_reduce(
                var8[:], scr1[:].rearrange("p a (h s) -> p a h s", s=64),
                AX.X, ALU.add), waits=[po.evt()])
            vec(lambda v: v.tensor_tensor(rstd8[:], mu8[:], mu8[:], ALU.mult))
            vec(lambda v: v.scalar_tensor_tensor(
                var8[:], var8[:], 1.0 / HS, rstd8[:], ALU.mult, ALU.subtract))
            act(lambda s: s.activation(out=rstd8[:], in_=var8[:], func=AF.Ln,
                                       bias=CEG), waits=[dv.evt()])
            act(lambda s: s.activation(out=rstd8[:], in_=rstd8[:],
                                       func=AF.Exp, scale=CNH))
            rstd_evt = ac.evt()
            vec(lambda v: v.tensor_tensor(y4, y4, bch(mu8), ALU.subtract),
                waits=[rstd_evt])
            vec(lambda v: v.tensor_tensor(y4, y4, bch(rstd8), ALU.mult))
            vec(lambda v: v.tensor_tensor(
                ytok[:], ytok[:],
                LNW_BC.unsqueeze(1).to_broadcast((128, NTT, CH)), ALU.mult),
                waits=[bc_done])
            vec(lambda v: v.tensor_tensor(
                ytok[:], ytok[:],
                LNB_BC.unsqueeze(1).to_broadcast((128, NTT, CH)), ALU.add))
            pool(lambda g: g.tensor_tensor(
                scr1[:], kfin[:],
                FAA_BC.unsqueeze(1).to_broadcast((128, NTT, CH)), ALU.mult),
                waits=[kfin_evt, bc_done, dv.evt()])
            pool(lambda g: g.tensor_tensor(scr1[:], rtok[:], scr1[:],
                                           ALU.mult))
            vec(lambda v: v.tensor_reduce(
                rk8[:], scr1[:].rearrange("p a (h s) -> p a h s", s=64),
                AX.X, ALU.add), waits=[po.evt()])
            pool(lambda g: g.tensor_tensor(
                scr1[:].rearrange("p a (h s) -> p a h s", s=64),
                vtok[:].rearrange("p a (h s) -> p a h s", s=64),
                bch(rk8), ALU.mult), waits=[dv.evt()])
            vec(lambda v: v.tensor_tensor(ytok[:], ytok[:], scr1[:],
                                          ALU.add), waits=[po.evt()])
            vec(lambda v: v.tensor_tensor(ygt[:], ytok[:], gtok[:],
                                          ALU.mult))
            yg_evt = dv.evt()

            ygf = AFt
            for tt in range(NTT):
                dma("sync", ygf[:, :, tt, :],
                    ygt[:, tt, :].rearrange("p (t x) -> p t x", x=128),
                    waits=[yg_evt], transpose=True)
            ygf_evt = st.evt()
            prev_wo_store = None
            for tt in range(NTT):
                for eh in range(2):
                    bo = grab()
                    for p in range(4):
                        mm(bo, bo.ap[:, 0:512], ygf[:, p, tt, :],
                           w_wo[:, p, eh * 512:(eh + 1) * 512], p == 0,
                           p == 3, waits=[ygf_evt] if p == 0 else [])
                    penop()
                    act(lambda s, bo=bo, tt=tt, eh=eh: s.copy(
                        scr1[:, tt, :], bo.ap[:, 0:512]),
                        waits=[pe.evt(), prev_wo_store] if eh == 1
                        else [pe.evt()])
                    bo.war = ac.evt()
                    row0 = (ci % 8) * 512 + (ci // 8) * 256 + tt * 128
                    prev_wo_store = dma(
                        "sync", opart[row0:row0 + 128,
                                      eh * 512:(eh + 1) * 512],
                        scr1[:, tt, :], waits=[ac.evt()])
            wo_evt = sl.evt()

            if nchunks == NCK and ci >= NCK - 8:
                j = ci - (NCK - 8)
                P.do("gpsimd", lambda g, j=j: g.collective_compute(
                    "ReduceScatter", ALU.add,
                    replica_groups=[[0, 1], [2, 3], [4, 5], [6, 7]],
                    ins=[opart[j * 512:(j + 1) * 512, :]],
                    outs=[osum[j * 256:(j + 1) * 256, :]]),
                    [wo_evt, sg.evt()], cc)
                cc_evts.append(cc.evt())

            wo_pe = pe.evt()
            for b_ in banks:
                b_.war = [dv.evt(), ac.evt(), po.evt()] + sl.evt()
            prev_h = {"pdv": prescan_cap["pdv"], "ppe": prescan_cap["ppe"],
                      "ppo": prescan_cap["ppo"], "po": po.evt(),
                      "yg": yg_evt, "wo_pe": wo_pe, "wo_st": wo_evt}
            if ci == nchunks - 1:
                sl_c, st_c, sg_c, po_c = (sl.evt(), st.evt(), sg.evt(),
                                          po.evt())
                dv_c, ac_c = dv.evt(), ac.evt()
                P.do("tensor", lambda t: t.nop(), [dv_c, ac_c, pe.evt()], bar)
                P.do("vector", lambda v: v.engine_nop(), [ac_c], bar)
                P.do("scalar", lambda s: s.nop(), [dv_c], bar)
                P.do("gpsimd", lambda g: g.engine_nop(), [sg_c], bar)
                P.do("sync", lambda s: s.nop(), [sl_c, st_c, po_c], bar)

            if sub == 8:
                break

        opart_done = sl.evt()
        attn_bar = bar.evt()
        esA.close()

        if phases == "A":
            P.do("vector", lambda v: v.engine_nop(),
                 [opart_done, attn_bar, st.evt(), sg.evt(), ac.evt(),
                  po.evt(), pe.evt(), sw_done], dv)
            P.finalize()
            return nc

        esB = es.enter_context(contextlib.ExitStack())
        esB1 = es.enter_context(contextlib.ExitStack())
        esC = es.enter_context(contextlib.ExitStack())

        def sbb(name, shape, dt=F32):
            return esB.enter_context(nc.sbuf_tensor(name, shape, dt))

        def sbb1(name, shape, dt=F32):
            return esB1.enter_context(nc.sbuf_tensor(name, shape, dt))

        def sbc(name, shape, dt=F32):
            return esC.enter_context(nc.sbuf_tensor(name, shape, dt))

        if not cc_evts:
            P.do("gpsimd", lambda g: g.collective_compute(
                "ReduceScatter", ALU.add,
                replica_groups=[[0, 1], [2, 3], [4, 5], [6, 7]],
                ins=[opart[:, :]], outs=[osum[:, :]]),
                [opart_done, attn_bar, sg.evt()], cc)
            cc_evts = [cc.evt()] * 8

        sx_r = sbb("sx_r", [128, 16])
        hq_s = sbb("hq_s", [128, 16, 2])
        hmax = sbb("hmax", [128, 16])
        f_mu = sbb("f_mu", [128, 4])
        f_sg = sbb("f_sg", [128, 4])
        f_t1 = sbb("f_t1", [128, 4])
        fscr = sbb("fscr", [128, 4, C])
        xqT = sbc("xqT", [128, 8, TH], BF16)
        x1t = sbb1("x1t", [128, 4, C])
        xr1 = sbb1("xr1", [128, 4, C])
        xq = sbb1("xq", [128, 4, C], BF16)

        for fi4 in range(4):
            tsl = slice(fi4 * 512, (fi4 + 1) * 512)
            dma("sync", x1t[:], osum[tsl, :].rearrange("(a p) c -> p a c",
                                                       p=128),
                waits=[cc_evts[2 * fi4 + 1], attn_bar, dv.evt()])
            dma("sync", xr1[:], d["xtm"][tsl, :].rearrange(
                "(a p) c -> p a c", p=128))
            ld_evt = sl.evt()
            vec(lambda v: v.tensor_tensor(x1t[:], x1t[:], xr1[:], ALU.add),
                waits=[ld_evt, st.evt()])
            vec(lambda v: v.tensor_reduce(f_mu[:], x1t[:], AX.X, ALU.add))
            vec(lambda v: v.tensor_scalar(f_mu[:], f_mu[:], 1.0 / C, None,
                                          ALU.mult))
            pool(lambda g: g.tensor_tensor(fscr[:], x1t[:], x1t[:], ALU.mult),
                 waits=[dv.evt()])
            vec(lambda v: v.tensor_reduce(f_sg[:], fscr[:], AX.X, ALU.add),
                waits=[po.evt()])
            vec(lambda v: v.tensor_tensor(f_t1[:], f_mu[:], f_mu[:],
                                          ALU.mult))
            vec(lambda v: v.scalar_tensor_tensor(
                f_sg[:], f_sg[:], 1.0 / C, f_t1[:], ALU.mult, ALU.subtract))
            act(lambda s: s.activation(out=f_sg[:], in_=f_sg[:], func=AF.Ln,
                                       bias=CEL), waits=[dv.evt()])
            act(lambda s: s.activation(out=f_sg[:], in_=f_sg[:], func=AF.Exp,
                                       scale=CNH))
            for a_ in range(4):
                vec(lambda v, a_=a_: v.scalar_tensor_tensor(
                    x1t[:, a_, :], x1t[:, a_, :], f_mu[:, a_:a_ + 1],
                    f_sg[:, a_:a_ + 1].to_broadcast((128, C)),
                    ALU.subtract, ALU.mult), waits=[ac.evt()])
            x1_evt = dv.evt()
            x1store = dma("sync",
                          x1sp[tsl, :].rearrange("(a p) c -> p a c", p=128),
                          x1t[:], waits=[x1_evt])
            vec(lambda v: v.tensor_reduce(f_mu[:], x1t[:], AX.X, ALU.max,
                                          apply_absolute_value=True))
            vec(lambda v: v.tensor_scalar(f_mu[:], f_mu[:], 1e-5,
                                          1.0 / 127.0, ALU.max, ALU.mult))
            vec(lambda v, fi4=fi4: v.tensor_copy(
                sx_r[:, fi4 * 4:(fi4 + 1) * 4], f_mu[:]))
            vec(lambda v: v.reciprocal(f_sg[:], f_mu[:]))
            for a_ in range(4):
                vec(lambda v, a_=a_: v.tensor_scalar(
                    x1t[:, a_, :], x1t[:, a_, :], f_sg[:, a_:a_ + 1], MAGIC,
                    ALU.mult, ALU.add), waits=[x1store] if a_ == 0 else [])
            vec(lambda v: v.tensor_scalar(xq[:], x1t[:], MAGIC, None,
                                          ALU.subtract))
            xq_evt = dv.evt()
            for a_ in range(4):
                dma("sync",
                    xqT[:, :, fi4 * 512 + a_ * 128:
                        fi4 * 512 + (a_ + 1) * 128],
                    xq[:, a_, :].rearrange("p (t x) -> p t x", x=128),
                    waits=[xq_evt], sem=st, transpose=True)
        xqT_evt = st.evt()
        sx_done = dv.evt()
        esB1.close()

        if phases == "AB":
            P.do("vector", lambda v: v.engine_nop(),
                 [xqT_evt, sx_done, sl.evt(), sg.evt(), ac.evt(), po.evt(),
                  pe.evt(), sw_done], dv)
            P.finalize()
            return nc

        wg_f = sbc("wg_f", [128, 8, 512])
        wu_f = sbc("wu_f", [128, 8, 512])
        wg_t = sbc("wg_t", [128, 2, 8, 512], BF16)
        wu_t = sbc("wu_t", [128, 2, 8, 512], BF16)
        tscr = sbc("tscr", [128, 8, 512])
        hblk = sbc("hblk", [128, 512])
        m1c = sbc("m1c", [128, 4])
        sigt = sbc("sigt", [128, 512])
        m1t = sbc("m1t", [128, 512])
        vec(lambda v: v.memset(hmax[:], 0.0))

        def ternarize(dst, srcf, wi, w_extra):
            vec(lambda v: v.tensor_scalar(
                tscr[:], srcf[:], swcol[:, wi:wi + 1], MAGIC, ALU.mult,
                ALU.add), waits=[sw_done] + [w for w in w_extra if w])
            rd = dv.evt()
            vec(lambda v: v.tensor_scalar(
                tscr[:], tscr[:], MAGIC, 1.0, ALU.subtract, ALU.min))
            vec(lambda v: v.tensor_scalar(dst, tscr[:], -1.0, None,
                                          ALU.max))
            return rd, dv.evt()

        prev_rd_g = None
        prev_rd_u = None
        tern_done = [None, None]
        buf_pe2 = [None, None]
        prev_hb_store = None
        for si in range(NSLAB + 1):
            if si < NSLAB:
                sb = si % 2
                isl = slice(si * 512, (si + 1) * 512)
                dma("sync", wg_f[:],
                    d["wgT"].rearrange("(c p) i -> p c i", p=128)[:, :, isl],
                    waits=[xqT_evt] if si == 0 else [prev_rd_g])
                lg = sl.evt()
                prev_rd_g, _ = ternarize(wg_t[:, sb, :, :], wg_f, 0,
                                         [lg, buf_pe2[sb]])
                dma("sync", wu_f[:],
                    d["wuT"].rearrange("(c p) i -> p c i", p=128)[:, :, isl],
                    waits=[prev_rd_u] if prev_rd_u else [])
                lu = sl.evt()
                prev_rd_u, tern_done[sb] = ternarize(wu_t[:, sb, :, :], wu_f,
                                                     1, [lu])
            if si == 0:
                continue
            sj = si - 1
            sb2 = sj % 2
            isl2 = slice(sj * 512, (sj + 1) * 512)
            for fi in range(FTT):
                bgp = grab(); bup = grab()
                for cc2 in range(8):
                    mm(bgp, bgp.ap[:, 0:512],
                       xqT[:, cc2, fi * 128:(fi + 1) * 128],
                       wg_t[:, sb2, cc2, :], cc2 == 0, cc2 == 7,
                       waits=[tern_done[sb2], xqT_evt] if cc2 == 0 else [])
                for cc2 in range(8):
                    mm(bup, bup.ap[:, 0:512],
                       xqT[:, cc2, fi * 128:(fi + 1) * 128],
                       wu_t[:, sb2, cc2, :], cc2 == 0, cc2 == 7,
                       waits=[tern_done[sb2]] if cc2 == 0 else [])
                penop()
                gu_evt = pe.evt()
                vec(lambda v, fi=fi: v.tensor_tensor(
                    m1c[:, 0:1], sx_r[:, fi:fi + 1], swcol[:, 4:5], ALU.mult),
                    waits=[gu_evt, sx_done, sw_done])
                vec(lambda v, fi=fi: v.tensor_tensor(
                    m1c[:, 1:2], sx_r[:, fi:fi + 1], swcol[:, 5:6], ALU.mult))
                vec(lambda v: v.tensor_tensor(m1c[:, 2:3], m1c[:, 0:1],
                                              m1c[:, 1:2], ALU.mult))
                sc_evt = dv.evt()
                act(lambda s, bgp=bgp: s.activation(
                    out=sigt[:], in_=bgp.ap[:, 0:512], func=AF.Sigmoid,
                    scale=m1c[:, 0:1]), waits=[sc_evt])
                sig_evt = ac.evt()
                vec(lambda v, bgp=bgp: v.scalar_tensor_tensor(
                    m1t[:], bgp.ap[:, 0:512], m1c[:, 2:3], sigt[:], ALU.mult,
                    ALU.mult), waits=[sig_evt])
                bgp.war = dv.evt()
                vec(lambda v, bup=bup: v.tensor_tensor(
                    hblk[:], m1t[:], bup.ap[:, 0:512], ALU.mult),
                    waits=[prev_hb_store] if prev_hb_store else [])
                bup.war = dv.evt()
                vec(lambda v: v.tensor_reduce(m1c[:, 3:4], hblk[:], AX.X,
                                              ALU.max,
                                              apply_absolute_value=True))
                vec(lambda v, fi=fi: v.tensor_tensor(
                    hmax[:, fi:fi + 1], hmax[:, fi:fi + 1], m1c[:, 3:4],
                    ALU.max))
                hb_evt = dv.evt()
                prev_hb_store = dma("sync",
                                    hsp[fi * 128:(fi + 1) * 128, isl2],
                                    hblk[:], waits=[hb_evt])
            buf_pe2[sb2] = pe.evt()
        prev_slab_pe = pe.evt()
        h_spill = sl.evt()
        vec(lambda v: v.tensor_scalar(hmax[:], hmax[:], 1e-5, 1.0 / 127.0,
                                      ALU.max, ALU.mult))
        vec(lambda v: v.tensor_copy(hq_s[:, :, 1], hmax[:]))
        vec(lambda v: v.reciprocal(hmax[:], hmax[:]))
        vec(lambda v: v.tensor_copy(hq_s[:, :, 0], hmax[:]))
        s2_done = dv.evt()
        esC.close()

        if phases == "ABC":
            P.do("vector", lambda v: v.engine_nop(),
                 [s2_done, h_spill, sl.evt(), sg.evt(), ac.evt(), po.evt(),
                  pe.evt()], dv)
            P.finalize()
            return nc

        esD = es.enter_context(contextlib.ExitStack())

        def sbd(name, shape, dt=F32):
            return esD.enter_context(nc.sbuf_tensor(name, shape, dt))

        wd_t = sbd("wd_t", [128, 32, 1024], BF16)
        wdl = sbd("wdl", [128, 4096])
        tscrD = sbd("tscrD", [128, 4096])
        wdq = sbd("wdq", [128, 2, 4096], BF16)
        hqt = sbd("hqt", [128, 2, 32, 128], BF16)
        o2 = sbd("o2", [128, 4, C])
        x1d = sbd("x1d", [128, 4, C])

        for sj in range(8):
            dma("sync", wdl[:].rearrange("p (i c) -> p i c", c=1024),
                d["wdT"].rearrange("(i p) c -> p i c", p=128)[
                    :, sj * 4:(sj + 1) * 4, :],
                waits=[h_spill, s2_done, prev_slab_pe] if sj == 0
                else [dv.evt()])
            vec(lambda v: v.tensor_scalar(
                tscrD[:], wdl[:], swcol[:, 2:3], MAGIC, ALU.mult, ALU.add),
                waits=[sl.evt(), sw_done])
            vec(lambda v: v.tensor_scalar(tscrD[:], tscrD[:], MAGIC, 1.0,
                                          ALU.subtract, ALU.min))
            vec(lambda v, sj=sj: v.tensor_scalar(
                wd_t[:, sj * 4:(sj + 1) * 4, :].rearrange("p a q -> p (a q)"),
                tscrD[:], -1.0, None, ALU.max))
        wd_done = dv.evt()

        prev_hqT = [None, None]
        buf_pe = [None, None]
        st_out = None
        lx_evt = None
        ld_evt = None
        for fi in range(16):
            q4, a_ = divmod(fi, 4)
            pb = fi % 2
            if fi == 0:
                dma("gpsimd", wdl[:], hsp[0:128, :],
                    waits=[h_spill, wd_done])
                ld_evt = sg.evt()
                dma("gpsimd", x1d[:],
                    x1sp[0:512, :].rearrange("(a p) c -> p a c", p=128),
                    waits=[h_spill])
                lx_evt = sg.evt()
            vec(lambda v, fi=fi: v.tensor_scalar(
                tscrD[:], wdl[:], hq_s[:, fi, 0:1], MAGIC, ALU.mult,
                ALU.add), waits=[ld_evt, s2_done]
                + ([wd_done] if fi == 0 else []))
            rd_evt = dv.evt()
            vec(lambda v, pb=pb: v.tensor_scalar(
                wdq[:, pb, :], tscrD[:], MAGIC, None, ALU.subtract),
                waits=[prev_hqT[pb]] if prev_hqT[pb] else [])
            hqq = dv.evt()
            if fi + 1 < 16:
                dma("gpsimd", wdl[:], hsp[(fi + 1) * 128:(fi + 2) * 128, :],
                    waits=[rd_evt])
                ld_evt = sg.evt()
            dma("sync", hqt[:, pb, :, :],
                wdq[:, pb, :].rearrange("p (t x) -> p t x", x=128),
                waits=[hqq] + ([buf_pe[pb]] if buf_pe[pb] else []),
                sem=st, transpose=True)
            hqT_evt = st.evt()
            prev_hqT[pb] = hqT_evt
            for eh in range(2):
                bo = grab()
                for ic in range(32):
                    mm(bo, bo.ap[:, 0:512], hqt[:, pb, ic, :],
                       wd_t[:, ic, eh * 512:(eh + 1) * 512],
                       ic == 0, ic == 31,
                       waits=[hqT_evt, wd_done] if ic == 0 else [])
                penop()
                aw = [pe.evt()]
                if a_ == 0 and eh == 0 and st_out:
                    aw.append(st_out)
                act(lambda s, bo=bo, a_=a_, eh=eh: s.copy(
                    o2[:, a_, eh * 512:(eh + 1) * 512], bo.ap[:, 0:512]),
                    waits=aw)
                bo.war = ac.evt()
            buf_pe[pb] = pe.evt()
            if a_ < 3:
                continue
            tsl = slice(q4 * 512, (q4 + 1) * 512)
            down_evt = ac.evt()
            for a_ in range(4):
                fi = q4 * 4 + a_
                vec(lambda v, fi=fi: v.tensor_tensor(
                    f_t1[:, 0:1], hq_s[:, fi, 1:2], swcol[:, 6:7], ALU.mult),
                    waits=[down_evt, lx_evt])
                vec(lambda v, a_=a_: v.scalar_tensor_tensor(
                    o2[:, a_, :], o2[:, a_, :], f_t1[:, 0:1], x1d[:, a_, :],
                    ALU.mult, ALU.add))
            vec(lambda v: v.tensor_reduce(f_mu[:], o2[:], AX.X, ALU.add))
            vec(lambda v: v.tensor_scalar(f_mu[:], f_mu[:], 1.0 / C, None,
                                          ALU.mult))
            pool(lambda g: g.tensor_tensor(fscr[:], o2[:], o2[:], ALU.mult),
                 waits=[dv.evt()])
            vec(lambda v: v.tensor_reduce(f_sg[:], fscr[:], AX.X, ALU.add),
                waits=[po.evt()])
            vec(lambda v: v.tensor_tensor(f_t1[:], f_mu[:], f_mu[:],
                                          ALU.mult))
            vec(lambda v: v.scalar_tensor_tensor(
                f_sg[:], f_sg[:], 1.0 / C, f_t1[:], ALU.mult, ALU.subtract))
            act(lambda s: s.activation(out=f_sg[:], in_=f_sg[:], func=AF.Ln,
                                       bias=CEL), waits=[dv.evt()])
            act(lambda s: s.activation(out=f_sg[:], in_=f_sg[:], func=AF.Exp,
                                       scale=CNH))
            for a_ in range(4):
                vec(lambda v, a_=a_: v.scalar_tensor_tensor(
                    o2[:, a_, :], o2[:, a_, :], f_mu[:, a_:a_ + 1],
                    f_sg[:, a_:a_ + 1].to_broadcast((128, C)),
                    ALU.subtract, ALU.mult), waits=[ac.evt()])
            oe = dv.evt()
            dma("gpsimd", out[tsl, :].rearrange("(a p) c -> p a c", p=128),
                o2[:], waits=[oe])
            st_out = sg.evt()
            if q4 + 1 < 4:
                dma("gpsimd", x1d[:],
                    x1sp[(q4 + 1) * 512:(q4 + 2) * 512, :].rearrange(
                        "(a p) c -> p a c", p=128), waits=[dv.evt()])
                lx_evt = sg.evt()

        if "x1" in dbg_out:
            dma("gpsimd", dbg_out["x1"][:, :], x1sp[:, :],
                waits=[sl.evt()])
        if "h" in dbg_out:
            dma("gpsimd", dbg_out["h"][:, :], hsp[:, :], waits=[sl.evt()])
        final_store = sl.evt()
        P.do("vector", lambda v: v.engine_nop(),
             [final_store, sg.evt()], dv)
        esD.close()
        esB.close()

        P.finalize()

    return nc


def kernel(**inputs):
    maps = host_prep(inputs)
    nc = build_program()
    res = run_bass_kernel_spmd(nc, maps, core_ids=list(range(8)))
    outa = np.empty((B, T, C), np.float32)
    for core in range(8):
        b, hg = divmod(core, 2)
        outa[b, hg * TH:(hg + 1) * TH, :] = res.results[core]["out"]
    return outa


# version 21 (final)
# speedup vs baseline: 1.2127x; 1.0820x over previous
import contextlib
import numpy as np
import concourse.bass as bass
import concourse.mybir as mybir
from concourse.bass_utils import run_bass_kernel_spmd

F32 = mybir.dt.float32
BF16 = mybir.dt.bfloat16
AF = mybir.ActivationFunctionType
ALU = mybir.AluOpType
AX = mybir.AxisListType

B, T, C, H, HS = 4, 4096, 1024, 16, 64
CH, NH, NPAIR = 512, 8, 4
CK, NCK = 256, 16
L = 64
NSUB = CK // L
NTTG = CK // 128
TH = 2048
INNER = 4096
NSLAB, FTT = 8, 16
MAGIC = float(np.float32(1.5 * 2 ** 23))
MG4 = float(np.float32(1.5 * 2 ** 19))
EPS_GN = 0.00064
EPS_LN = 1e-5
EXPM05 = float(np.exp(np.float32(-0.5)))

ENGINES = ("tensor", "vector", "scalar", "gpsimd", "sync")


class Sem:
    __slots__ = ("h", "n", "name")

    def __init__(self, h, name):
        self.h, self.n, self.name = h, 0, name

    def evt(self):
        return (self, self.n)


class Pipe:

    def __init__(self, nc, es):
        self.nc, self.es = nc, es
        self.block = None
        self.self_sem = {}
        self.seen = {e: {} for e in ENGINES}
        self._pend = None

    def sem(self, name):
        return Sem(self.es.enter_context(self.nc.semaphore(name)), name)

    def begin(self, block):
        self.block = block

    def _emit(self, eng, fn, w, inc, amt):
        seen = self.seen[eng]

        def body(engine):
            for (s_, c_) in w:
                if seen.get(id(s_), -1) < c_:
                    engine.wait_ge(s_.h, c_)
                    seen[id(s_)] = c_
            instr = fn(engine)
            if inc is not None:
                instr.then_inc(inc.h, amt)
        getattr(self.block, eng)(body)

    def _flush(self):
        if self._pend is not None:
            fn, w, inc, amt = self._pend
            self._pend = None
            self._emit("tensor", fn, w, inc, amt)

    def do(self, eng, fn, waits=(), inc=None, amt=1):
        w = []
        for x in waits:
            if x is None:
                continue
            if isinstance(x, list):
                for (s_, c_) in x:
                    if c_ > 0:
                        w.append((s_, c_))
            else:
                (s_, c_) = x
                if c_ > 0:
                    w.append((s_, c_))
        ss = self.self_sem.get(eng)
        if ss is not None and ss.n > 0:
            w.append((ss, ss.n))
        if eng == "tensor":
            self._flush()
            self._pend = (fn, w, inc, amt)
            if inc is not None:
                inc.n += amt
                return (inc, inc.n)
            return None
        self._emit(eng, fn, w, inc, amt)
        if inc is not None:
            inc.n += amt
            return (inc, inc.n)
        return None

    def attach_pe(self, pe):
        assert self._pend is not None
        fn, w, inc, amt = self._pend
        assert inc is None
        self._pend = (fn, w, pe, 1)
        pe.n += 1
        return (pe, pe.n)

    def finalize(self, block=None):
        self._flush()


class QRing:

    def __init__(self, pipe, name, n=4):
        self.sems = [pipe.sem(f"{name}{i}") for i in range(n)]
        self.i = 0

    def next(self):
        s = self.sems[self.i % len(self.sems)]
        self.i += 1
        return s

    def evt(self):
        return [(s, s.n) for s in self.sems]


class Bank:
    __slots__ = ("ap", "war")

    def __init__(self, t):
        self.ap, self.war = t, None


def host_prep(ins):
    x = np.asarray(ins["L0_attn_x"], np.float32)
    maps = []
    wgT = np.ascontiguousarray(np.asarray(ins["L0_ffn_w_gate"], np.float32).T)
    wuT = np.ascontiguousarray(np.asarray(ins["L0_ffn_w_up"], np.float32).T)
    wdT = np.ascontiguousarray(np.asarray(ins["L0_ffn_w_down"], np.float32).T)
    for core in range(8):
        b, hg = divmod(core, 2)
        ch = slice(hg * CH, (hg + 1) * CH)
        th = slice(hg * TH, (hg + 1) * TH)
        g = lambda k: np.asarray(ins["L0_attn_" + k], np.float32)
        r16a2 = np.zeros((65, 1024), np.float32)
        r16a2[0:16, 0:512] = g("aaa_w2")[:, ch]
        r16a2[32:48, 512:1024] = g("ma_w2")[:, ch]
        r16a2[64, 0:512] = g("time_aaaaa")[ch]
        r16a2[64, 512:1024] = g("time_misc_a")[ch]
        r16b2 = np.zeros((65, 1024), np.float32)
        r16b2[0:16, 0:512] = g("kkk_w2")[:, ch]
        r16b2[32:48, 512:1024] = g("mk_w2")[:, ch]
        r16b2[64, 512:1024] = g("time_misc_k")[ch]
        r16aT = np.zeros((C, 48), np.float32)
        r16aT[:, 0:16] = g("aaa_w1").T
        r16aT[:, 32:48] = g("ma_w1").T
        r16bT = np.zeros((C, 48), np.float32)
        r16bT[:, 0:16] = g("kkk_w1").T
        r16bT[:, 32:48] = g("mk_w1").T
        dec2b = np.concatenate([g("decay_w2")[:, ch], g("time_decay")[None, ch]], 0)
        sc = slice(core * 512, (core + 1) * 512)
        wstat = np.stack([wgT[:, sc].reshape(128, 4096),
                          wuT[:, sc].reshape(128, 4096),
                          wdT[sc, :].reshape(128, 4096)], axis=1)
        m = {
            "xT": np.ascontiguousarray(x[b].T),
            "xtm": np.ascontiguousarray(x[b, th]),
            "tmx_c": np.ascontiguousarray(g("time_maa_x").reshape(8, 128).T),
            "tmg_c": np.ascontiguousarray(
                g("time_maa").reshape(4, 8, 128).transpose(2, 1, 0)),
            "maa1T": np.ascontiguousarray(g("maa_w1").T),
            "maa2": np.ascontiguousarray(g("maa_w2")),
            "wrT": np.ascontiguousarray(g("Wr").T[:, ch]),
            "wkT": np.ascontiguousarray(g("Wk").T[:, ch]),
            "wvT": np.ascontiguousarray(g("Wv").T[:, ch]),
            "woT": np.ascontiguousarray(g("Wo").T[ch, :]),
            "gate1T": np.ascontiguousarray(g("gate_w1").T),
            "gate2": np.ascontiguousarray(g("gate_w2")[:, ch]),
            "dec1T": np.ascontiguousarray(g("decay_w1").T),
            "dec2b": np.ascontiguousarray(dec2b),
            "r16aT": r16aT, "r16bT": r16bT, "r16a2": r16a2, "r16b2": r16b2,
            "faa": np.ascontiguousarray(g("time_faaaa")[None, ch]),
            "lnw": np.ascontiguousarray(g("ln_w")[None, ch]),
            "lnb": np.ascontiguousarray(g("ln_b")[None, ch]),
            "wstat": np.ascontiguousarray(wstat),
            "wgT": wgT, "wuT": wuT, "wdT": wdT,
        }
        maps.append(m)
    return maps


def build_program(dbg=(), nchunks=NCK, phases="ABCD", sub=99):
    nc = bass.Bass("TRN2", target_bir_lowering=False)
    es = contextlib.ExitStack()
    d = {}

    def din(name, shape):
        d[name] = nc.dram_tensor(name, shape, F32, kind="ExternalInput").ap()

    din("xT", [C, T]); din("xtm", [TH, C])
    din("tmx_c", [128, 8]); din("tmg_c", [128, 8, 4])
    din("maa1T", [C, 128]); din("maa2", [4, 32, C])
    din("wrT", [C, CH]); din("wkT", [C, CH]); din("wvT", [C, CH])
    din("woT", [CH, C])
    din("gate1T", [C, 128]); din("gate2", [128, CH])
    din("dec1T", [C, 64]); din("dec2b", [65, CH])
    din("r16aT", [C, 48]); din("r16bT", [C, 48])
    din("r16a2", [65, 1024]); din("r16b2", [65, 1024])
    din("faa", [1, CH]); din("lnw", [1, CH]); din("lnb", [1, CH])
    din("wstat", [128, 3, 4096])
    din("wgT", [C, INNER]); din("wuT", [C, INNER]); din("wdT", [INNER, C])
    out = nc.dram_tensor("out", [TH, C], F32, kind="ExternalOutput").ap()

    opart = nc.dram_tensor("opart", [T, C], F32).ap()
    osum = nc.dram_tensor("osum", [TH, C], F32).ap()
    x1sp = nc.dram_tensor("x1sp", [TH, C], F32).ap()
    hsp = nc.dram_tensor("hsp", [TH, INNER], F32).ap()
    stat_dram = nc.dram_tensor("stat_dram", [1, 8], F32).ap()
    stin = nc.dram_tensor("stin", [8, 8], F32).ap()
    stout = nc.dram_tensor("stout", [1, 8], F32).ap()

    dbg_out = {}
    for name, shape in dbg:
        dbg_out[name] = nc.dram_tensor("dbg_" + name, shape, F32,
                                       kind="ExternalOutput").ap()

    with es:
        P = Pipe(nc, es)

        def sb_outer(name, shape, dt=F32):
            return es.enter_context(nc.sbuf_tensor(name, shape, dt))

        sl = QRing(P, "sl"); sg = QRing(P, "sg"); st = QRing(P, "st")
        pe = P.sem("pe"); dv = P.sem("dv"); ac = P.sem("ac")
        po = P.sem("po"); cc = P.sem("cc"); bar = P.sem("bar")
        P.self_sem = {"vector": dv, "scalar": ac, "gpsimd": po}
        qrings = {"sync": sl, "gpsimd": sg, "st": st}
        P.begin(es.enter_context(nc.Block(no_gpsimd_drain=False)))

        banks = [Bank(es.enter_context(nc.psum_tensor(f"pb{i}", [128, 512],
                                                      F32)))
                 for i in range(8)]
        rr = [0]

        def grab():
            b = banks[rr[0] % 8]
            rr[0] += 1
            return b

        def mm(bank, out_ap, lhsT, rhs, start, stop, tp=None, waits=(),
               inc=None, skip=False):
            w = list(waits)
            if bank is not None and start and bank.war is not None:
                w.append(bank.war)
            def fn(t, o=out_ap, l=lhsT, r=rhs, s1=start, s2=stop, tp=tp,
                   sk=skip):
                return t.matmul(o, l, r, start=s1, stop=s2, tile_position=tp,
                                skip_group_check=sk)
            return P.do("tensor", fn, w, inc)

        def penop(waits=()):
            return P.attach_pe(pe)

        def dma(queue, out_ap, in_ap, waits=(), sem=None, transpose=False):
            ring = st if transpose else qrings[queue]
            s_ = ring.next()
            w = list(waits) + [(s_, s_.n)]
            def fn(e, o=out_ap, i=in_ap, tr=transpose):
                if tr:
                    return e.dma_start(out=o, in_=i, transpose=True)
                return e.dma_start(out=o, in_=i)
            return P.do(queue, fn, w, s_, 16)

        def act(fn, waits=()):
            return P.do("scalar", fn, waits, ac)

        def vec(fn, waits=()):
            return P.do("vector", fn, waits, dv)

        def pool(fn, waits=()):
            return P.do("gpsimd", fn, waits, po)

        tmx = sb_outer("tmx", [128, 8])
        tmg = sb_outer("tmg", [128, 8, 4])
        dma("sync", tmx[:], d["tmx_c"])
        dma("sync", tmg[:], d["tmg_c"])
        rows_bf = sb_outer("rows_bf", [1, 3, CH], BF16)
        dma("gpsimd", rows_bf[:, 0, :], d["faa"])
        dma("gpsimd", rows_bf[:, 1, :], d["lnw"])
        dma("gpsimd", rows_bf[:, 2, :], d["lnb"])
        vload = sl.evt()
        rows_ld = sg.evt()

        ident = sb_outer("ident", [128, 64], BF16)
        tri = sb_outer("tri", [128, 64], BF16)
        maskS = sb_outer("maskS", [128, 64])
        maskSU = sb_outer("maskSU", [128, 64])
        maskIU = sb_outer("maskIU", [128, 64])
        sel = sb_outer("sel", [128, NTTG, NSUB], BF16)
        iscr = sb_outer("iscr", [128, 64])
        consts = sb_outer("consts", [128, 8])
        ones1 = sb_outer("ones1", [1, 128], BF16)
        z1 = sb_outer("z1", [1, 128], BF16)
        zr = sb_outer("zr", [1, 512], BF16)
        onesc = sb_outer("onesc", [128, 4], BF16)
        rows_bc = sb_outer("rows_bc", [128, 3, CH], BF16)
        swcol = sb_outer("swcol", [128, 8])

        pool(lambda g: g.iota(iscr[0:64, :], [[-1, 64]], base=0,
                              channel_multiplier=1,
                              allow_small_or_imprecise_dtypes=True))
        pool(lambda g: g.iota(iscr[64:128, :], [[-1, 64]], base=0,
                              channel_multiplier=1,
                              allow_small_or_imprecise_dtypes=True))
        io1 = po.evt()
        vec(lambda v: v.tensor_scalar(ident[:], iscr[:], 0.0, None,
                                      ALU.is_equal), waits=[io1])
        vec(lambda v: v.tensor_scalar(tri[:], iscr[:], 0.0, None, ALU.is_le))
        vec(lambda v: v.tensor_scalar(maskS[:], iscr[:], 0.0, None, ALU.is_gt))
        vec(lambda v: v.tensor_scalar(maskSU[:], iscr[:], 0.0, None,
                                      ALU.is_lt))
        vec(lambda v: v.tensor_scalar(maskIU[:], iscr[:], 0.0, None,
                                      ALU.is_le))
        vec(lambda v: v.memset(ones1[:], 1.0))
        vec(lambda v: v.memset(z1[:], 0.0))
        vec(lambda v: v.memset(zr[:], 0.0))
        vec(lambda v: v.memset(onesc[:], 0.0))
        vec(lambda v: v.memset(onesc[:, 0:1], 1.0))
        for j, val in enumerate([-1.0, -0.5, EPS_GN, EPS_LN, 1e-24,
                                 1.0 / HS, 127.0, 1e-5]):
            vec(lambda v, j=j, val=val: v.memset(consts[:, j:j + 1], val))
        CN1 = consts[:, 0:1]; CNH = consts[:, 1:2]; CEG = consts[:, 2:3]
        CEL = consts[:, 3:4]
        pool(lambda g: g.iota(iscr[:, 0:NSUB], [[1, NSUB]], base=0,
                              channel_multiplier=0,
                              allow_small_or_imprecise_dtypes=True),
             waits=[dv.evt()])
        pool(lambda g: g.iota(iscr[:, 8:9], [[0, 1]], base=0,
                              channel_multiplier=1,
                              allow_small_or_imprecise_dtypes=True))
        io2 = po.evt()
        vec(lambda v: v.tensor_scalar(iscr[:, 9:10], iscr[:, 8:9], 63.5, None,
                                      ALU.is_gt), waits=[io2])
        for tt in range(NTTG):
            vec(lambda v, tt=tt: v.tensor_scalar(
                iscr[:, 10:11], iscr[:, 9:10], 1.0, float(2 * tt),
                ALU.mult, ALU.add))
            vec(lambda v, tt=tt: v.scalar_tensor_tensor(
                iscr[:, 11:11 + NSUB],
                iscr[:, 10:11].to_broadcast((128, NSUB)), -1.0,
                iscr[:, 0:NSUB], ALU.mult, ALU.add))
            vec(lambda v, tt=tt: v.tensor_scalar(
                sel[:, tt, :], iscr[:, 11:11 + NSUB], 0.0, None,
                ALU.is_equal))
        consts_done = dv.evt()

        if sub == -4:
            P.do("vector", lambda v: v.engine_nop(),
                 [sl.evt(), sg.evt(), st.evt(), ac.evt(), po.evt(),
                  pe.evt(), dv.evt()], dv)
            P.finalize()
            return nc

        rbf = rows_ld
        for j in range(3):
            bb = grab()
            mm(bb, bb.ap[:, 0:512], ones1[:], rows_bf[:, j, :], True, True,
               waits=[rbf, consts_done])
            penop()
            vec(lambda v, bb=bb, j=j: v.tensor_copy(rows_bc[:, j, :],
                                                    bb.ap[:, 0:512]),
                waits=[pe.evt()])
            bb.war = dv.evt()

        if sub == -3:
            P.do("vector", lambda v: v.engine_nop(),
                 [sl.evt(), sg.evt(), st.evt(), ac.evt(), po.evt(),
                  pe.evt(), dv.evt()], dv)
            P.finalize()
            return nc
        bc_done = dv.evt()
        FAA_BC = rows_bc[:, 0, :]
        LNW_BC = rows_bc[:, 1, :]
        LNB_BC = rows_bc[:, 2, :]

        wst = sb_outer("wst", [128, 512])
        stat4 = sb_outer("stat4", [128, 4])
        stat_c = sb_outer("stat_c", [128, 1])
        stat_bf = sb_outer("stat_bf", [128, 12], BF16)
        statr = sb_outer("statr", [128, 4])
        strow = sb_outer("strow", [4, 2])
        stlo = sb_outer("stlo", [4, 1])
        swsum = sb_outer("swsum", [1, 8])
        vec(lambda v: v.memset(stat4[:], 0.0))
        for wi in range(3):
            for pc in range(8):
                dma("sync", wst[:],
                    d["wstat"][:, wi, pc * 512:(pc + 1) * 512],
                    waits=[dv.evt()])
                vec(lambda v: v.tensor_reduce(
                    stat_c[:], wst[:], AX.X, ALU.add,
                    apply_absolute_value=True), waits=[sl.evt()])
                vec(lambda v, wi=wi: v.tensor_tensor(
                    stat4[:, wi:wi + 1], stat4[:, wi:wi + 1], stat_c[:],
                    ALU.add))
        vec(lambda v: v.tensor_copy(stat_bf[:, 0:4], stat4[:]))
        vec(lambda v: v.tensor_tensor(statr[:], stat4[:],
                                      stat_bf[:, 0:4], ALU.subtract))
        vec(lambda v: v.tensor_copy(stat_bf[:, 4:8], statr[:]))
        vec(lambda v: v.tensor_tensor(stat_bf[:, 8:12], statr[:],
                                      stat_bf[:, 4:8], ALU.subtract))
        bs = grab()
        mm(bs, bs.ap[0:4, 0:1], stat_bf[:, 0:4], onesc[:, 0:1], True, True,
           waits=[dv.evt(), consts_done])
        mm(bs, bs.ap[0:4, 1:2], stat_bf[:, 4:8], onesc[:, 0:1], True, True)
        mm(bs, bs.ap[0:4, 2:3], stat_bf[:, 8:12], onesc[:, 0:1], True, True)
        penop()
        vec(lambda v, bs=bs: v.tensor_scalar(
            strow[:, 0:1], bs.ap[0:4, 0:1], MG4, MG4, ALU.add, ALU.subtract),
            waits=[pe.evt()])
        vec(lambda v, bs=bs: v.tensor_tensor(
            stlo[:], bs.ap[0:4, 0:1], strow[:, 0:1], ALU.subtract))
        vec(lambda v, bs=bs: v.tensor_tensor(
            stlo[:], stlo[:], bs.ap[0:4, 1:2], ALU.add))
        vec(lambda v, bs=bs: v.tensor_tensor(
            strow[:, 1:2], stlo[:], bs.ap[0:4, 2:3], ALU.add))
        bs.war = dv.evt()
        dma("gpsimd", stat_dram[0, 0:4], strow[:, 0], waits=[dv.evt()])
        dma("gpsimd", stat_dram[0, 4:8], strow[:, 1], waits=[dv.evt()])
        dma("gpsimd", stin[:, :], stat_dram.to_broadcast((8, 8)),
            waits=[sg.evt()])
        P.do("gpsimd", lambda g: g.collective_compute(
            "ReduceScatter", ALU.add, replica_groups=[list(range(8))],
            ins=[stin[:, :]], outs=[stout[:, :]]), [sg.evt()], cc)
        dma("gpsimd", swsum[:], stout[:, :], waits=[cc.evt()])
        swr = sb_outer("swr", [1, 24], BF16)
        swf = sb_outer("swf", [1, 8])
        swfr = sb_outer("swfr", [1, 8])
        vec(lambda v: v.tensor_tensor(
            swsum[:, 0:4], swsum[:, 0:4], swsum[:, 4:8], ALU.add),
            waits=[sg.evt()])
        vec(lambda v: v.tensor_scalar(
            swsum[:, 0:4], swsum[:, 0:4], 1.0 / (C * INNER), 1e-5, ALU.mult,
            ALU.max))
        vec(lambda v: v.tensor_copy(swf[:, 4:8], swsum[:, 0:4]))
        vec(lambda v: v.reciprocal(swsum[:, 0:4], swsum[:, 0:4]))
        vec(lambda v: v.tensor_copy(swf[:, 0:4], swsum[:, 0:4]))
        vec(lambda v: v.tensor_copy(swr[:, 0:8], swf[:]))
        vec(lambda v: v.tensor_tensor(swfr[:], swf[:], swr[:, 0:8],
                                      ALU.subtract))
        vec(lambda v: v.tensor_copy(swr[:, 8:16], swfr[:]))
        vec(lambda v: v.tensor_tensor(swr[:, 16:24], swfr[:], swr[:, 8:16],
                                      ALU.subtract))
        bs2 = grab()
        mm(bs2, bs2.ap[:, 0:24], ones1[:], swr[:], True, True,
           waits=[dv.evt()])
        penop()
        vec(lambda v, bs2=bs2: v.tensor_copy(swcol[:], bs2.ap[:, 0:8]),
            waits=[pe.evt()])
        vec(lambda v, bs2=bs2: v.tensor_tensor(
            swcol[:], swcol[:], bs2.ap[:, 8:16], ALU.add))
        vec(lambda v, bs2=bs2: v.tensor_tensor(
            swcol[:], swcol[:], bs2.ap[:, 16:24], ALU.add))
        bs2.war = dv.evt()
        sw_done = dv.evt()

        if sub == -2:
            if "swc" in dbg_out:
                dma("sync", dbg_out["swc"][:, 0:8], swcol[:], waits=[sw_done])
                dma("sync", dbg_out["swc"][:, 8:12],
                    stat4[:], waits=[sw_done])
            P.do("vector", lambda v: v.engine_nop(),
                 [sl.evt(), sg.evt(), st.evt(), ac.evt(), po.evt(),
                  pe.evt(), dv.evt()], dv)
            P.finalize()
            return nc

        esA = es.enter_context(contextlib.ExitStack())

        def sba(name, shape, dt=BF16):
            return esA.enter_context(nc.sbuf_tensor(name, shape, dt))

        w_wr = sba("w_wr", [128, 8, CH]); w_wk = sba("w_wk", [128, 8, CH])
        w_wv = sba("w_wv", [128, 8, CH]); w_wo = sba("w_wo", [128, 4, C])
        w_wrl = sba("w_wrl", [128, 8, CH]); w_wkl = sba("w_wkl", [128, 8, CH])
        w_wvl = sba("w_wvl", [128, 8, CH])
        w_maa1 = sba("w_maa1", [128, 8, 128])
        w_m2A = sba("w_m2A", [64, C])
        w_m2B = sba("w_m2B", [64, C])
        w_g1 = sba("w_g1", [128, 8, 128]); w_g2 = sba("w_g2", [128, CH])
        w_d1 = sba("w_d1", [128, 8, 64]); w_d2 = sba("w_d2", [65, CH])
        w_ra1 = sba("w_ra1", [128, 8, 48]); w_rb1 = sba("w_rb1", [128, 8, 48])
        w_ra2 = sba("w_ra2", [65, 1024]); w_rb2 = sba("w_rb2", [65, 1024])
        for dst, name in ((w_wr, "wrT"), (w_wk, "wkT"), (w_wv, "wvT"),
                          (w_wo, "woT"), (w_maa1, "maa1T"), (w_g1, "gate1T"),
                          (w_d1, "dec1T"), (w_ra1, "r16aT"), (w_rb1, "r16bT")):
            dma("gpsimd", dst[:], d[name].rearrange("(c p) o -> p c o", p=128))
        dma("gpsimd", w_m2A[:], d["maa2"][0:2].rearrange("g k c -> (g k) c"))
        dma("gpsimd", w_m2B[:], d["maa2"][2:4].rearrange("g k c -> (g k) c"))
        dma("gpsimd", w_g2[:], d["gate2"])
        dma("gpsimd", w_d2[:], d["dec2b"])
        dma("gpsimd", w_ra2[:], d["r16a2"])
        dma("gpsimd", w_rb2[:], d["r16b2"])
        esA0 = contextlib.ExitStack()
        wfp = esA0.enter_context(nc.sbuf_tensor("wfp", [128, 8, CH], F32))
        for hi_t, lo_t, name in ((w_wr, w_wrl, "wrT"), (w_wk, w_wkl, "wkT"),
                                 (w_wv, w_wvl, "wvT")):
            dma("sync", wfp[:], d[name].rearrange("(c p) o -> p c o", p=128),
                waits=[dv.evt()])
            vec(lambda v, hi_t=hi_t, lo_t=lo_t: v.tensor_tensor(
                lo_t[:], wfp[:], hi_t[:], ALU.subtract),
                waits=[sl.evt(), sg.evt()])
        esA0.close()
        wld = sg.evt()
        wlo_done = dv.evt()

        if sub == -1:
            P.do("vector", lambda v: v.engine_nop(),
                 [sl.evt(), sg.evt(), st.evt(), ac.evt(), po.evt(),
                  pe.evt(), dv.evt()], dv)
            P.finalize()
            return nc

        NTT = CK // 128
        xf = sba("xf", [128, 8, CK + 1], F32)
        xm0 = sba("xm0", [128, 8, CK], F32)
        xm1 = sba("xm1", [128, 8, CK], F32)
        xxt = sba("xxt", [128, 8, CK])
        xmh = sba("xmh", [128, 8, CK])
        xml = sba("xml", [128, 8, CK])
        loA = sba("loA", [64, CK]); loB = sba("loB", [64, CK])
        g_in = sba("g_in", [128, CK])
        dec_in = sba("dec_in", [65, CK]); ra_in = sba("ra_in", [65, CK])
        rb_in = sba("rb_in", [65, CK])
        dec_inl = sba("dec_inl", [65, CK])
        dec_f = sba("dec_f", [64, CK], F32)
        vec(lambda v: v.memset(dec_in[:], 0.0))
        vec(lambda v: v.memset(ra_in[:], 0.0))
        vec(lambda v: v.memset(rb_in[:], 0.0))
        vec(lambda v: v.memset(dec_inl[:], 0.0))
        vec(lambda v: v.memset(dec_in[64:65, :], 1.0))
        vec(lambda v: v.memset(ra_in[64:65, :], 1.0))
        vec(lambda v: v.memset(rb_in[64:65, :], 1.0))
        inner_init = dv.evt()

        def tokf(name):
            return sba(name, [128, NTT, CH], F32)

        def tokb(name):
            return sba(name, [128, NTT, CH])

        rtok = tokf("rtok"); gtok = tokb("gtok")
        ktmp = tokf("ktmp"); kkn = tokf("kkn"); kfin = tokf("kfin")
        asig = tokf("asig"); masig = tokf("masig"); mksig = tokf("mksig")
        ept = tokb("ept"); emt = tokb("emt"); epvt = tokb("epvt")
        sdec = tokf("sdec"); scr1 = tokf("scr1")
        ke = mksig
        ytok = sdec
        vtok = tokb("vtok")
        gamh = tokb("gamh"); gaml = tokb("gaml")
        atil = tokb("atil"); btil = tokb("btil")
        ktil = tokb("ktil"); qtil = tokb("qtil")
        ygt = tokb("ygt")

        s8 = lambda n: sba(n, [128, NTT, 8], F32)
        mu8, var8, rstd8, rk8 = s8("mu8"), s8("var8"), s8("rstd8"), s8("rk8")
        dlt = sba("dlt", [128, 4, NSUB], F32)

        AFt = sba("AFt", [128, 4, NTT, 128])
        BFt = sba("BFt", [128, 4, NTT, 128])
        KFt = sba("KFt", [128, 4, NTT, 128])
        QFt = sba("QFt", [128, 4, NTT, 128])

        m8 = lambda n: sba(n, [128, 8, 64])
        Tsb, Ttsb, Usb, QBsb, QKsb, Nsb, Msb = map(
            m8, ["Tsb", "Ttsb", "Usb", "QBsb", "QKsb", "Nsb", "Msb"])
        Xsb = sba("Xsb", [128, 8, 128])
        Xlo = sba("Xlo", [128, 8, 128])
        Wh = sba("Wh", [128, 4, 64]); Mc = sba("Mc", [128, 4, 64])
        Zf = esA.enter_context(nc.sbuf_tensor("Zf", [128, 4, 64], F32))
        Zhi = sba("Zhi", [128, 4, 64]); Zlo = sba("Zlo", [128, 4, 64])
        vec(lambda v: v.memset(Zf[:], 0.0))
        vec(lambda v: v.memset(Zhi[:], 0.0))
        vec(lambda v: v.memset(Zlo[:], 0.0))
        z_evt = dv.evt()

        NBAR = 5
        cc_evts = []
        prev_h = None

        for ci in range(nchunks):
            if prev_h is None:
                bw_load, bw_mix, bw_act = [], [], []
            else:
                bw_load = [prev_h["pdv"]]
                bw_mix = [prev_h["ppe"], prev_h["ppo"]]
                bw_act = [prev_h["pdv"]]
                pool(lambda g: g.engine_nop(), waits=[prev_h["pdv"]])
            if ci == 0:
                vec(lambda v: v.memset(xf[:, :, 0:1], 0.0))
                dma("gpsimd", xf[:, :, 1:CK + 1],
                    d["xT"].rearrange("(c p) t -> p c t", p=128)[:, :, 0:CK],
                    waits=[inner_init])
            else:
                dma("gpsimd", xf[:, :, 0:CK + 1],
                    d["xT"].rearrange("(c p) t -> p c t", p=128)[
                        :, :, ci * CK - 1:ci * CK + CK], waits=bw_load)
            xr = sg.evt()

            vec(lambda v: v.tensor_tensor(xxt[:], xf[:, :, 0:CK],
                                          xf[:, :, 1:CK + 1], ALU.subtract),
                waits=[xr, consts_done, bc_done] + bw_mix)
            for cc2 in range(8):
                vec(lambda v, cc2=cc2: v.scalar_tensor_tensor(
                    xmh[:, cc2, :], xxt[:, cc2, :], tmx[:, cc2:cc2 + 1],
                    xf[:, cc2, 1:CK + 1], ALU.mult, ALU.add), waits=[vload])
            mix_evt = dv.evt()

            blo = grab()
            for cc2 in range(8):
                mm(blo, blo.ap[:, 0:CK], w_maa1[:, cc2, :], xmh[:, cc2, :],
                   cc2 == 0, cc2 == 7,
                   waits=[mix_evt, wld] if cc2 == 0 else [])
            penop()
            act(lambda s, blo=blo: s.activation(
                out=loA[:], in_=blo.ap[0:64, 0:CK], func=AF.Tanh),
                waits=[pe.evt()] + bw_act)
            act(lambda s, blo=blo: s.activation(
                out=loB[:], in_=blo.ap[64:128, 0:CK], func=AF.Tanh))
            blo.war = ac.evt()
            lo_evt = ac.evt()
            lo_done_pe = pe.evt()

            def emit_lo2_xm(gi, xmbuf, extra_wait):
                lot = loA if gi < 2 else loB
                ksl = slice((gi % 2) * 32, (gi % 2) * 32 + 32)
                for cc2 in range(8):
                    bb = grab()
                    wm2 = w_m2A if gi < 2 else w_m2B
                    mm(bb, bb.ap[:, 0:CK],
                       wm2[(gi % 2) * 32:(gi % 2) * 32 + 32,
                           cc2 * 128:(cc2 + 1) * 128],
                       lot[ksl, :], True, True, waits=[lo_evt])
                    penop()
                    w0 = [pe.evt()]
                    if cc2 == 0:
                        w0.append(lo_done_pe)
                        if extra_wait is not None:
                            w0.append(extra_wait)
                    vec(lambda v, bb=bb, cc2=cc2, gi=gi, xmbuf=xmbuf:
                        v.tensor_scalar(
                            xmbuf[:, cc2, :], bb.ap[:, 0:CK],
                            tmg[:, cc2, gi:gi + 1], None, ALU.add),
                        waits=w0)
                    bb.war = dv.evt()
                lo2_evt = dv.evt()
                pool(lambda g, xmbuf=xmbuf: g.tensor_tensor(
                    xmbuf[:], xxt[:], xmbuf[:], ALU.mult), waits=[lo2_evt])
                vec(lambda v, xmbuf=xmbuf: v.tensor_tensor(
                    xmbuf[:], xf[:, :, 1:CK + 1], xmbuf[:], ALU.add),
                    waits=[po.evt()])
                xm_f = dv.evt()
                vec(lambda v, xmbuf=xmbuf: v.tensor_copy(xmh[:], xmbuf[:]),
                    waits=[xm_f] + ([extra_wait] if extra_wait else []))
                vec(lambda v, xmbuf=xmbuf: v.tensor_tensor(
                    xml[:], xmbuf[:], xmh[:], ALU.subtract))
                return dv.evt()

            def proj512(xmbuf, wtile, dst, e_in, wlo=None):
                for tt in range(NTT):
                    bp = grab()
                    for cc2 in range(8):
                        mm(bp, bp.ap[:, 0:512],
                           xmh[:, cc2, tt * 128:(tt + 1) * 128],
                           wtile[:, cc2, :], cc2 == 0, False,
                           waits=[e_in, wlo_done] if cc2 == 0 else [])
                        mm(bp, bp.ap[:, 0:512],
                           xml[:, cc2, tt * 128:(tt + 1) * 128],
                           wtile[:, cc2, :], False, False)
                        if wlo is not None:
                            mm(bp, bp.ap[:, 0:512],
                               xmh[:, cc2, tt * 128:(tt + 1) * 128],
                               wlo[:, cc2, :], False, cc2 == 7)
                        elif cc2 == 7:
                            mm(bp, bp.ap[:, 0:512],
                               xml[:, cc2, tt * 128:(tt + 1) * 128],
                               wtile[:, cc2, :], False, True, skip=True)
                    penop()
                    vec(lambda v, bp=bp, tt=tt, dst=dst: v.tensor_copy(
                        dst[:, tt, :], bp.ap[:, 0:512]), waits=[pe.evt()])
                    bp.war = dv.evt()
                return dv.evt()

            e0 = emit_lo2_xm(0, xm1, lo_done_pe)
            r_evt = proj512(xm1, w_wr, rtok, e0, w_wrl)
            bg = grab()
            for cc2 in range(8):
                mm(bg, bg.ap[:, 0:CK], w_g1[:, cc2, :], xmh[:, cc2, :],
                   cc2 == 0, cc2 == 7, waits=[e0] if cc2 == 0 else [])
            penop()
            act(lambda s, bg=bg: s.activation(out=g_in[:], in_=bg.ap[:, 0:CK],
                                              func=AF.Tanh), waits=[pe.evt()])
            bg.war = ac.evt()
            gin_evt = ac.evt()
            for tt in range(NTT):
                bgo = grab()
                mm(bgo, bgo.ap[:, 0:512], g_in[:, tt * 128:(tt + 1) * 128],
                   w_g2[:], True, True, waits=[gin_evt])
                penop()
                vec(lambda v, bgo=bgo, tt=tt: v.tensor_copy(
                    gtok[:, tt, :], bgo.ap[:, 0:512]), waits=[pe.evt()])
                bgo.war = dv.evt()
            xm1_free = pe.evt()

            e1 = emit_lo2_xm(1, xm0, xm1_free)
            bd = grab()
            for cc2 in range(8):
                mm(bd, bd.ap[0:64, 0:CK], w_d1[:, cc2, :], xmh[:, cc2, :],
                   cc2 == 0, cc2 == 7, waits=[e1] if cc2 == 0 else [])
            penop()
            act(lambda s, bd=bd: s.activation(out=dec_f[:],
                                              in_=bd.ap[0:64, 0:CK],
                                              func=AF.Tanh), waits=[pe.evt()])
            bd.war = ac.evt()
            act(lambda s: s.copy(dec_in[0:64, :], dec_f[:]))
            vec(lambda v: v.tensor_tensor(dec_inl[0:64, :], dec_f[:],
                                          dec_in[0:64, :], ALU.subtract),
                waits=[ac.evt()])
            din_evt = dv.evt()
            bra = grab()
            for cc2 in range(8):
                mm(bra, bra.ap[0:48, 0:CK], w_ra1[:, cc2, :], xmh[:, cc2, :],
                   cc2 == 0, cc2 == 7, waits=[e1] if cc2 == 0 else [])
            penop()
            vec(lambda v, bra=bra: v.tensor_copy(ra_in[0:48, :],
                                                 bra.ap[0:48, 0:CK]),
                waits=[pe.evt()])
            bra.war = dv.evt()
            rain_evt = dv.evt()
            for tt in range(NTT):
                bdo = grab()
                mm(bdo, bdo.ap[:, 0:512], dec_in[:, tt * 128:(tt + 1) * 128],
                   w_d2[:], True, False, waits=[din_evt])
                mm(bdo, bdo.ap[:, 0:512], dec_inl[:, tt * 128:(tt + 1) * 128],
                   w_d2[:], False, True)
                penop()
                act(lambda s, bdo=bdo, tt=tt: s.activation(
                    out=sdec[:, tt, :], in_=bdo.ap[:, 0:512],
                    func=AF.Sigmoid), waits=[pe.evt()])
                bdo.war = ac.evt()
            for tt in range(NTT):
                ba1 = grab()
                mm(ba1, ba1.ap[:, 0:512], ra_in[:, tt * 128:(tt + 1) * 128],
                   w_ra2[:, 0:512], True, True, waits=[rain_evt])
                penop()
                act(lambda s, ba1=ba1, tt=tt: s.activation(
                    out=asig[:, tt, :], in_=ba1.ap[:, 0:512],
                    func=AF.Sigmoid), waits=[pe.evt()])
                ba1.war = ac.evt()
                ba2 = grab()
                mm(ba2, ba2.ap[:, 0:512], ra_in[:, tt * 128:(tt + 1) * 128],
                   w_ra2[:, 512:1024], True, True)
                penop()
                act(lambda s, ba2=ba2, tt=tt: s.activation(
                    out=masig[:, tt, :], in_=ba2.ap[:, 0:512],
                    func=AF.Sigmoid), waits=[pe.evt()])
                ba2.war = ac.evt()
            sdec_evt = ac.evt()
            xm0_free = pe.evt()

            vec(lambda v: v.tensor_scalar(gamh[:], sdec[:], -EXPM05, None,
                                          ALU.mult), waits=[sdec_evt])
            vec(lambda v: v.scalar_tensor_tensor(
                gaml[:], sdec[:], -EXPM05, gamh[:], ALU.mult, ALU.subtract))
            gam_evt = dv.evt()

            e2 = emit_lo2_xm(2, xm1, xm0_free)
            k_evt = proj512(xm1, w_wk, ktmp, e2, w_wkl)
            brb = grab()
            for cc2 in range(8):
                mm(brb, brb.ap[0:48, 0:CK], w_rb1[:, cc2, :], xmh[:, cc2, :],
                   cc2 == 0, cc2 == 7, waits=[e2] if cc2 == 0 else [])
            penop()
            act(lambda s, brb=brb: s.activation(out=rb_in[0:16, :],
                                                in_=brb.ap[0:16, 0:CK],
                                                func=AF.Tanh),
                waits=[pe.evt()])
            act(lambda s, brb=brb: s.copy(rb_in[32:48, :],
                                          brb.ap[32:48, 0:CK]))
            brb.war = ac.evt()
            rbin_evt = ac.evt()
            for tt in range(NTT):
                bb1 = grab()
                mm(bb1, bb1.ap[:, 0:512], rb_in[:, tt * 128:(tt + 1) * 128],
                   w_rb2[:, 0:512], True, True, waits=[rbin_evt])
                penop()
                vec(lambda v, bb1=bb1, tt=tt: v.tensor_copy(
                    scr1[:, tt, :], bb1.ap[:, 0:512]), waits=[pe.evt()])
                bb1.war = dv.evt()
                bb2 = grab()
                mm(bb2, bb2.ap[:, 0:512], rb_in[:, tt * 128:(tt + 1) * 128],
                   w_rb2[:, 512:1024], True, True)
                penop()
                act(lambda s, bb2=bb2, tt=tt: s.activation(
                    out=mksig[:, tt, :], in_=bb2.ap[:, 0:512],
                    func=AF.Sigmoid), waits=[pe.evt()])
                bb2.war = ac.evt()
            kkk_evt = ac.evt()
            xm1_free = pe.evt()

            e3 = emit_lo2_xm(3, xm0, xm1_free)
            v_evt = proj512(xm0, w_wv, vtok, e3, w_wvl)
            xm0_free = pe.evt()

            vec(lambda v: v.tensor_tensor(kkn[:], ktmp[:], scr1[:], ALU.add),
                waits=[k_evt, kkk_evt]
                + ([prev_h["po"], prev_h["yg"]] if prev_h else []))
            vec(lambda v: v.tensor_tensor(scr1[:], kkn[:], kkn[:], ALU.mult))
            vec(lambda v: v.tensor_reduce(
                rk8[:], scr1[:].rearrange("p a (h s) -> p a h s", s=64),
                AX.X, ALU.add))
            vec(lambda v: v.tensor_scalar(rk8[:], rk8[:], 1e-24, None,
                                          ALU.max))
            act(lambda s: s.activation(out=rk8[:], in_=rk8[:], func=AF.Ln),
                waits=[dv.evt()])
            act(lambda s: s.activation(out=rk8[:], in_=rk8[:], func=AF.Exp,
                                       scale=CNH))
            vec(lambda v: v.tensor_tensor(
                kkn[:].rearrange("p a (h s) -> p a h s", s=64),
                kkn[:].rearrange("p a (h s) -> p a h s", s=64),
                rk8[:].rearrange("p a (h o) -> p a h o", o=1).to_broadcast(
                    (128, NTT, 8, 64)), ALU.mult), waits=[ac.evt()])
            kkn_evt = dv.evt()
            act(lambda s: s.activation(out=sdec[:], in_=sdec[:], func=AF.Ln),
                waits=[gam_evt])
            lns_evt = ac.evt()
            vec(lambda v: v.scalar_tensor_tensor(
                scr1[:], sdec[:], -0.5, mksig[:], ALU.add, ALU.mult),
                waits=[lns_evt, kkn_evt])
            act(lambda s: s.activation(out=ke[:], in_=scr1[:], func=AF.Exp),
                waits=[dv.evt()])
            ke_evt = ac.evt()
            vec(lambda v: v.tensor_scalar(scr1[:], masig[:], -1.0, 1.0,
                                          ALU.mult, ALU.add),
                waits=[ke_evt])
            pool(lambda g: g.tensor_tensor(scr1[:], asig[:], scr1[:],
                                           ALU.mult), waits=[dv.evt()])
            vec(lambda v: v.tensor_tensor(scr1[:], masig[:], scr1[:],
                                          ALU.add), waits=[po.evt()])
            pool(lambda g: g.tensor_tensor(kfin[:], ktmp[:], scr1[:],
                                           ALU.mult), waits=[dv.evt()])
            vec(lambda v: v.tensor_tensor(kfin[:], kfin[:], ke[:], ALU.mult),
                waits=[po.evt(), ke_evt])
            kfin_evt = dv.evt()

            for tt in range(NTT):
                bG = grab()
                for hf in range(2):
                    rb0 = hf * 64
                    rsl0 = slice(rb0, rb0 + 64)
                    mm(bG, bG.ap[rsl0, 0:512], tri[rsl0, :],
                       gamh[rsl0, tt, :], True, False, tp=(rb0, rb0),
                       waits=[gam_evt] if hf == 0 else [])
                    mm(bG, bG.ap[rsl0, 0:512], tri[rsl0, :],
                       gaml[rsl0, tt, :], False, True, tp=(rb0, rb0))
                penop()
                gG = pe.evt()
                act(lambda s, bG=bG, tt=tt: s.activation(
                    out=ept[:, tt, :], in_=bG.ap[:, 0:512], func=AF.Exp),
                    waits=[gG])
                act(lambda s, bG=bG, tt=tt: s.activation(
                    out=emt[:, tt, :], in_=bG.ap[:, 0:512], func=AF.Exp,
                    scale=CN1))
                vec(lambda v, bG=bG, tt=tt: v.scalar_tensor_tensor(
                    ytok[:, tt, :], gamh[:, tt, :], -1.0, bG.ap[:, 0:512],
                    ALU.mult, ALU.add), waits=[gG])
                vec(lambda v, tt=tt: v.tensor_tensor(
                    ytok[:, tt, :], ytok[:, tt, :], gaml[:, tt, :],
                    ALU.subtract))
                act(lambda s, tt=tt: s.activation(
                    out=epvt[:, tt, :], in_=ytok[:, tt, :], func=AF.Exp),
                    waits=[dv.evt()])
                bG.war = ac.evt()
            ef_evt = ac.evt()

            bDL = grab()
            for p in range(NPAIR):
                csl = slice(p * 128, (p + 1) * 128)
                for tt in range(NTT):
                    mm(bDL, bDL.ap[:, p * NSUB:(p + 1) * NSUB],
                       gamh[:, tt, csl], sel[:, tt, :], tt == 0, False,
                       waits=[gam_evt, consts_done]
                       if (p == 0 and tt == 0) else [])
                    mm(bDL, bDL.ap[:, p * NSUB:(p + 1) * NSUB],
                       gaml[:, tt, csl], sel[:, tt, :], False, tt == NTT - 1)
            penop()
            act(lambda s, bDL=bDL: s.activation(
                out=dlt[:].rearrange("p a e -> p (a e)"),
                in_=bDL.ap[:, 0:4 * NSUB], func=AF.Exp), waits=[pe.evt()])
            bDL.war = ac.evt()
            dl_evt = ac.evt()

            vec(lambda v: v.tensor_tensor(atil[:], kkn[:], epvt[:], ALU.mult),
                waits=[ef_evt, kkn_evt])
            vec(lambda v: v.scalar_tensor_tensor(
                scr1[:], kkn[:], -1.0, asig[:], ALU.mult, ALU.mult),
                waits=[kfin_evt])
            pool(lambda g: g.tensor_tensor(btil[:], scr1[:], emt[:],
                                           ALU.mult),
                 waits=[ef_evt, dv.evt(), po.evt()])
            btil_evt = po.evt()
            vec(lambda v: v.tensor_tensor(ktil[:], kfin[:], emt[:],
                                          ALU.mult), waits=[kfin_evt])
            pool(lambda g: g.tensor_tensor(qtil[:], rtok[:], ept[:],
                                           ALU.mult), waits=[r_evt, ef_evt])
            sc_dv = dv.evt()
            sc_po = po.evt()

            for (src_, dst) in ((atil, AFt), (btil, BFt), (ktil, KFt),
                                (qtil, QFt)):
                for tt in range(NTT):
                    dma("sync", dst[:, :, tt, :],
                        src_[:, tt, :].rearrange("p (t x) -> p t x", x=128),
                        waits=[sc_dv, sc_po]
                        + ([prev_h["wo_pe"]] if prev_h else []),
                        transpose=True)
            tr_evt = st.evt()

            prescan_cap = {"pdv": dv.evt(), "ppe": pe.evt(),
                           "ppo": po.evt()}

            z_evt_g = [z_evt]
            b7hi_free = [None]
            y_free = [prev_h["yg"] if prev_h else None]

            def make_sub(s_i):
                tt = s_i // 2
                rb0 = (s_i % 2) * 64
                rsl = slice(rb0, rb0 + 64)
                bX = [banks[5], banks[6]]
                st = {"nsrc": Tsb, "msrc": Ttsb}

                def mbc(mk):
                    return mk[rsl, :].unsqueeze(1).to_broadcast((64, 8, 64))

                def pv(bb):
                    return bb.ap[rsl, 0:512].rearrange("p (a s) -> p a s",
                                                       s=64)

                def evac_X(w, lo=False):
                    vec(lambda v: v.tensor_copy(
                        Xsb[rsl, 0:4, :].rearrange("p a x -> p (a x)"),
                        bX[0].ap[rsl, 0:512]), waits=w)
                    act(lambda s: s.copy(
                        Xsb[rsl, 4:8, :].rearrange("p a x -> p (a x)"),
                        bX[1].ap[rsl, 0:512]), waits=w)
                    if lo:
                        hi_done = ac.evt()
                        vec(lambda v: v.tensor_tensor(
                            Xlo[rsl, 0:4, :].rearrange("p a x -> p (a x)"),
                            bX[0].ap[rsl, 0:512],
                            Xsb[rsl, 0:4, :].rearrange("p a x -> p (a x)"),
                            ALU.subtract))
                        vec(lambda v: v.tensor_tensor(
                            Xlo[rsl, 4:8, :].rearrange("p a x -> p (a x)"),
                            bX[1].ap[rsl, 0:512],
                            Xsb[rsl, 4:8, :].rearrange("p a x -> p (a x)"),
                            ALU.subtract), waits=[hi_done])
                    return [dv.evt(), ac.evt()]

                def stage_g(extra_w=()):
                    bT, bTt, bU, bQB, bQK = (banks[0], banks[1], banks[2],
                                             banks[3], banks[4])
                    for p in range(NPAIR):
                        for h2 in range(2):
                            i8 = p * 2 + h2
                            hsl = slice(h2 * 64, h2 * 64 + 64)
                            col = slice(i8 * 64, i8 * 64 + 64)
                            AFs = AFt[hsl, p, tt, rb0:rb0 + 64]
                            BFs = BFt[hsl, p, tt, rb0:rb0 + 64]
                            KFs = KFt[hsl, p, tt, rb0:rb0 + 64]
                            QFs = QFt[hsl, p, tt, rb0:rb0 + 64]
                            w0 = [tr_evt] + list(extra_w) if i8 == 0 else []
                            mm(bT, bT.ap[rsl, col], AFs, BFs, True, True,
                               tp=(h2 * 64, rb0), waits=w0)
                            mm(bTt, bTt.ap[rsl, col], BFs, AFs, True, True,
                               tp=(h2 * 64, rb0))
                            mm(bU, bU.ap[rsl, col], KFs, AFs, True, True,
                               tp=(h2 * 64, rb0))
                            mm(bQB, bQB.ap[rsl, col], BFs, QFs, True, True,
                               tp=(h2 * 64, rb0))
                            mm(bQK, bQK.ap[rsl, col], KFs, QFs, True, True,
                               tp=(h2 * 64, rb0))
                    penop()
                    gram_evt = pe.evt()
                    vec(lambda v: v.tensor_tensor(
                        Tsb[rsl, :, :], pv(bT), mbc(maskS), ALU.mult),
                        waits=[gram_evt])
                    vec(lambda v: v.tensor_tensor(
                        Ttsb[rsl, :, :], pv(bTt), mbc(maskSU), ALU.mult))
                    vec(lambda v: v.tensor_tensor(
                        Usb[rsl, :, :], pv(bU), mbc(maskSU), ALU.mult))
                    vec(lambda v: v.tensor_tensor(
                        QBsb[rsl, :, :], pv(bQB), mbc(maskIU), ALU.mult))
                    vec(lambda v: v.tensor_tensor(
                        QKsb[rsl, :, :], pv(bQK), mbc(maskIU), ALU.mult))
                    st["gram_sb"] = dv.evt()
                    st["pw"] = [st["gram_sb"]]

                def stage_x0():
                    for bXp in bX:
                        mm(bXp, bXp.ap[rsl, 0:512], z1[0:1, 0:64],
                           zr[0:1, :], True, True, tp=(0, rb0),
                           waits=[st["gram_sb"], v_evt])
                    for p in range(NPAIR):
                        bXp = bX[p // 2]
                        xo = (p % 2) * 256
                        for h2 in range(2):
                            i8 = p * 2 + h2
                            base = p * 128 + h2 * 64
                            mm(None, bXp.ap[rsl, xo + h2 * 128:
                                            xo + h2 * 128 + 64],
                               Usb[rsl, i8, :], vtok[rsl, tt, base:base + 64],
                               False, False, tp=(rb0, rb0), skip=True)
                            mm(None, bXp.ap[rsl, xo + h2 * 128 + 64:
                                            xo + h2 * 128 + 128],
                               ident[rsl, :], atil[rsl, tt, base:base + 64],
                               False, False, tp=(rb0, rb0), skip=True)
                    penop()
                    st["xe"] = evac_X([pe.evt()])

                def stage_lvl(lvl):
                    nsrc, msrc = st["nsrc"], st["msrc"]
                    for p in range(NPAIR):
                        bXp = bX[p // 2]
                        xo = (p % 2) * 256
                        for h2 in range(2):
                            i8 = p * 2 + h2
                            mm(None, bXp.ap[rsl, xo + h2 * 128:
                                            xo + h2 * 128 + 128],
                               msrc[rsl, i8, :], Xsb[rsl, i8, :],
                               False, False, tp=(rb0, rb0),
                               waits=st["xe"] + st["pw"]
                               if i8 == 0 and p == 0 else [],
                               skip=True)
                    penop()
                    ap_evt = pe.evt()
                    if lvl < 5:
                        bN, bM = banks[0], banks[1]
                        for i8 in range(8):
                            col = slice(i8 * 64, i8 * 64 + 64)
                            mm(bN, bN.ap[rsl, col], msrc[rsl, i8, :],
                               nsrc[rsl, i8, :], True, True, tp=(rb0, rb0),
                               waits=st["pw"] if i8 == 0 else [])
                            mm(bM, bM.ap[rsl, col], nsrc[rsl, i8, :],
                               msrc[rsl, i8, :], True, True, tp=(rb0, rb0))
                        penop()
                        sq_evt = pe.evt()
                        vec(lambda v: v.tensor_copy(Nsb[rsl, :, :], pv(bN)),
                            waits=[sq_evt])
                        act(lambda s: s.copy(Msb[rsl, :, :], pv(bM)),
                            waits=[sq_evt])
                        st["pw"] = [dv.evt(), ac.evt()]
                        st["nsrc"], st["msrc"] = Nsb, Msb
                    st["xe"] = evac_X([ap_evt], lo=(lvl == 5))

                def stage_wy():
                    b7 = banks[7]
                    for p in range(NPAIR):
                        for h2 in range(2):
                            i8 = p * 2 + h2
                            hsl = slice(h2 * 64, h2 * 64 + 64)
                            w0 = (st["xe"] + [st["gram_sb"], z_evt_g[0]]
                                  + ([b7hi_free[0]] if b7hi_free[0] else [])
                                  if i8 == 0 else [])
                            mm(b7, b7.ap[hsl, 256 + p * 64:
                                           256 + (p + 1) * 64],
                               Xsb[rsl, i8, 64:128], QBsb[rsl, i8, :],
                               True, True, tp=(rb0, h2 * 64), waits=w0)
                    penop()
                    vec(lambda v: v.tensor_tensor(
                        Wh[:], banks[7].ap[:, 256:512].rearrange(
                            "p (a s) -> p a s", s=64),
                        QFt[:, :, tt, rb0:rb0 + 64], ALU.add),
                        waits=[pe.evt()])
                    st["wh"] = dv.evt()
                    bY = banks[3]
                    mm(bY, bY.ap[rsl, 0:512], z1[0:1, 0:64], zr[0:1, :],
                       True, True, tp=(0, rb0),
                       waits=st["xe"] + ([y_free[0]] if y_free[0] else []))
                    for p in range(NPAIR):
                        for h2 in range(2):
                            i8 = p * 2 + h2
                            base = p * 128 + h2 * 64
                            ycol = slice(base, base + 64)
                            mm(None, bY.ap[rsl, ycol], QBsb[rsl, i8, :],
                               Xsb[rsl, i8, 0:64], False, False,
                               tp=(rb0, rb0), skip=True)
                            mm(None, bY.ap[rsl, ycol], QBsb[rsl, i8, :],
                               Xlo[rsl, i8, 0:64], False, False,
                               tp=(rb0, rb0), skip=True)
                            mm(None, bY.ap[rsl, ycol], QKsb[rsl, i8, :],
                               vtok[rsl, tt, base:base + 64], False, False,
                               tp=(rb0, rb0), skip=True)
                    for p in range(NPAIR):
                        for h2 in range(2):
                            i8 = p * 2 + h2
                            hsl = slice(h2 * 64, h2 * 64 + 64)
                            base = p * 128 + h2 * 64
                            mm(None, bY.ap[rsl, base:base + 64], Wh[hsl, p, :],
                               Zhi[hsl, p, :], False, False,
                               tp=(h2 * 64, rb0),
                               waits=[st["wh"], z_evt_g[0]] if i8 == 0
                               else [], skip=True)
                    penop()
                    act(lambda s: s.copy(ytok[rsl, tt, :],
                                         banks[3].ap[rsl, 0:512]),
                        waits=[pe.evt()])
                    y_free[0] = ac.evt()

                def stage_pz():
                    b7 = banks[7]
                    for p in range(NPAIR):
                        for h2 in range(2):
                            i8 = p * 2 + h2
                            hsl = slice(h2 * 64, h2 * 64 + 64)
                            mm(b7, b7.ap[hsl, 256 + p * 64:
                                           256 + (p + 1) * 64],
                               Xsb[rsl, i8, 64:128],
                               btil[rsl, tt, base_of(p, h2)], True, True,
                               tp=(rb0, h2 * 64),
                               waits=st["xe"] + [st["wh"]] if i8 == 0 else [])
                    penop()
                    vec(lambda v: v.tensor_tensor(
                        Mc[:], b7.ap[:, 256:512].rearrange(
                            "p (a s) -> p a s", s=64),
                        ident[:].unsqueeze(1).to_broadcast((128, 4, 64)),
                        ALU.add), waits=[pe.evt()])
                    mc_evt = dv.evt()
                    b7hi_free[0] = mc_evt
                    mm(b7, b7.ap[:, 0:256], z1[:], zr[0:1, 0:256],
                       True, True, waits=st["xe"] + [mc_evt, z_evt_g[0]])
                    for p in range(NPAIR):
                        for h2 in range(2):
                            i8 = p * 2 + h2
                            hsl = slice(h2 * 64, h2 * 64 + 64)
                            zcol = slice(p * 64, (p + 1) * 64)
                            base = p * 128 + h2 * 64
                            mm(None, b7.ap[hsl, zcol],
                               btil[rsl, tt, base:base + 64],
                               Xsb[rsl, i8, 0:64],
                               False, False, tp=(rb0, h2 * 64), skip=True)
                            mm(None, b7.ap[hsl, zcol],
                               btil[rsl, tt, base:base + 64],
                               Xlo[rsl, i8, 0:64],
                               False, False, tp=(rb0, h2 * 64), skip=True)
                            mm(None, b7.ap[hsl, zcol],
                               ktil[rsl, tt, base:base + 64],
                               vtok[rsl, tt, base:base + 64], False, False,
                               tp=(rb0, h2 * 64), skip=True)
                            mm(None, b7.ap[hsl, zcol], Mc[hsl, p, :],
                               Zhi[hsl, p, :], False, False,
                               tp=(h2 * 64, h2 * 64), skip=True)
                            mm(None, b7.ap[hsl, zcol], Mc[hsl, p, :],
                               Zlo[hsl, p, :], False, False,
                               tp=(h2 * 64, h2 * 64), skip=True)
                    penop()
                    zup_evt = pe.evt()
                    vec(lambda v: v.tensor_tensor(
                        Zf[:], b7.ap[:, 0:256].rearrange("p (a s) -> p a s",
                                                         s=64),
                        dlt[:, :, s_i:s_i + 1].to_broadcast((128, 4, 64)),
                        ALU.mult), waits=[zup_evt, dl_evt])
                    vec(lambda v: v.tensor_copy(Zhi[:], Zf[:]))
                    vec(lambda v: v.tensor_tensor(Zlo[:], Zf[:], Zhi[:],
                                                  ALU.subtract))
                    z_evt_g[0] = dv.evt()

                def base_of(p, h2):
                    base = p * 128 + h2 * 64
                    return slice(base, base + 64)

                st["g"], st["x0"], st["lvl"] = stage_g, stage_x0, stage_lvl
                st["wy"], st["pz"] = stage_wy, stage_pz
                return st

            for pk in range(0, NSUB, 2):
                sa = make_sub(pk)
                sb_ = make_sub(pk + 1)
                pair_w = [dv.evt(), ac.evt()] if pk else []
                sa["g"](pair_w)
                sb_["g"]()
                sa["x0"]()
                sb_["x0"]()
                for lvl in range(6):
                    sa["lvl"](lvl)
                    sb_["lvl"](lvl)
                sa["wy"]()
                sa["pz"]()
                sb_["wy"]()
                sb_["pz"]()

            z_evt = z_evt_g[0]
            y_all = ac.evt()

            y4 = ytok[:].rearrange("p a (h s) -> p a h s", s=64)

            def bch(t_):
                return t_[:].rearrange("p a (h o) -> p a h o",
                                       o=1).to_broadcast((128, NTT, 8, 64))

            vec(lambda v: v.tensor_reduce(mu8[:], y4, AX.X, ALU.add),
                waits=[y_all])
            vec(lambda v: v.tensor_scalar(mu8[:], mu8[:], 1.0 / HS, None,
                                          ALU.mult))
            pool(lambda g: g.tensor_tensor(scr1[:], ytok[:], ytok[:],
                                           ALU.mult), waits=[y_all, dv.evt()])
            vec(lambda v: v.tensor_reduce(
                var8[:], scr1[:].rearrange("p a (h s) -> p a h s", s=64),
                AX.X, ALU.add), waits=[po.evt()])
            vec(lambda v: v.tensor_tensor(rstd8[:], mu8[:], mu8[:], ALU.mult))
            vec(lambda v: v.scalar_tensor_tensor(
                var8[:], var8[:], 1.0 / HS, rstd8[:], ALU.mult, ALU.subtract))
            act(lambda s: s.activation(out=rstd8[:], in_=var8[:], func=AF.Ln,
                                       bias=CEG), waits=[dv.evt()])
            act(lambda s: s.activation(out=rstd8[:], in_=rstd8[:],
                                       func=AF.Exp, scale=CNH))
            rstd_evt = ac.evt()
            vec(lambda v: v.tensor_tensor(y4, y4, bch(mu8), ALU.subtract),
                waits=[rstd_evt])
            vec(lambda v: v.tensor_tensor(y4, y4, bch(rstd8), ALU.mult))
            vec(lambda v: v.tensor_tensor(
                ytok[:], ytok[:],
                LNW_BC.unsqueeze(1).to_broadcast((128, NTT, CH)), ALU.mult),
                waits=[bc_done])
            vec(lambda v: v.tensor_tensor(
                ytok[:], ytok[:],
                LNB_BC.unsqueeze(1).to_broadcast((128, NTT, CH)), ALU.add))
            pool(lambda g: g.tensor_tensor(
                scr1[:], kfin[:],
                FAA_BC.unsqueeze(1).to_broadcast((128, NTT, CH)), ALU.mult),
                waits=[kfin_evt, bc_done, dv.evt()])
            pool(lambda g: g.tensor_tensor(scr1[:], rtok[:], scr1[:],
                                           ALU.mult))
            vec(lambda v: v.tensor_reduce(
                rk8[:], scr1[:].rearrange("p a (h s) -> p a h s", s=64),
                AX.X, ALU.add), waits=[po.evt()])
            pool(lambda g: g.tensor_tensor(
                scr1[:].rearrange("p a (h s) -> p a h s", s=64),
                vtok[:].rearrange("p a (h s) -> p a h s", s=64),
                bch(rk8), ALU.mult), waits=[dv.evt()])
            vec(lambda v: v.tensor_tensor(ytok[:], ytok[:], scr1[:],
                                          ALU.add), waits=[po.evt()])
            vec(lambda v: v.tensor_tensor(ygt[:], ytok[:], gtok[:],
                                          ALU.mult))
            yg_evt = dv.evt()

            ygf = AFt
            for tt in range(NTT):
                dma("sync", ygf[:, :, tt, :],
                    ygt[:, tt, :].rearrange("p (t x) -> p t x", x=128),
                    waits=[yg_evt], transpose=True)
            ygf_evt = st.evt()
            prev_wo_store = None
            for tt in range(NTT):
                for eh in range(2):
                    bo = grab()
                    for p in range(4):
                        mm(bo, bo.ap[:, 0:512], ygf[:, p, tt, :],
                           w_wo[:, p, eh * 512:(eh + 1) * 512], p == 0,
                           p == 3, waits=[ygf_evt] if p == 0 else [])
                    penop()
                    act(lambda s, bo=bo, tt=tt, eh=eh: s.copy(
                        scr1[:, tt, :], bo.ap[:, 0:512]),
                        waits=[pe.evt(), prev_wo_store] if eh == 1
                        else [pe.evt()])
                    bo.war = ac.evt()
                    row0 = (ci % 8) * 512 + (ci // 8) * 256 + tt * 128
                    prev_wo_store = dma(
                        "sync", opart[row0:row0 + 128,
                                      eh * 512:(eh + 1) * 512],
                        scr1[:, tt, :], waits=[ac.evt()])
            wo_evt = sl.evt()

            if nchunks == NCK and ci >= NCK - 8:
                j = ci - (NCK - 8)
                P.do("gpsimd", lambda g, j=j: g.collective_compute(
                    "ReduceScatter", ALU.add,
                    replica_groups=[[0, 1], [2, 3], [4, 5], [6, 7]],
                    ins=[opart[j * 512:(j + 1) * 512, :]],
                    outs=[osum[j * 256:(j + 1) * 256, :]]),
                    [wo_evt, sg.evt()], cc)
                cc_evts.append(cc.evt())

            wo_pe = pe.evt()
            for b_ in banks:
                b_.war = [dv.evt(), ac.evt(), po.evt()] + sl.evt()
            prev_h = {"pdv": prescan_cap["pdv"], "ppe": prescan_cap["ppe"],
                      "ppo": prescan_cap["ppo"], "po": po.evt(),
                      "yg": yg_evt, "wo_pe": wo_pe, "wo_st": wo_evt}
            if ci == nchunks - 1:
                sl_c, st_c, sg_c, po_c = (sl.evt(), st.evt(), sg.evt(),
                                          po.evt())
                dv_c, ac_c = dv.evt(), ac.evt()
                P.do("tensor", lambda t: t.nop(), [dv_c, ac_c, pe.evt()], bar)
                P.do("vector", lambda v: v.engine_nop(), [ac_c], bar)
                P.do("scalar", lambda s: s.nop(), [dv_c], bar)
                P.do("gpsimd", lambda g: g.engine_nop(), [sg_c], bar)
                P.do("sync", lambda s: s.nop(), [sl_c, st_c, po_c], bar)

            if sub == 8:
                break

        opart_done = sl.evt()
        attn_bar = bar.evt()
        esA.close()

        if phases == "A":
            P.do("vector", lambda v: v.engine_nop(),
                 [opart_done, attn_bar, st.evt(), sg.evt(), ac.evt(),
                  po.evt(), pe.evt(), sw_done], dv)
            P.finalize()
            return nc

        esB = es.enter_context(contextlib.ExitStack())
        esB1 = es.enter_context(contextlib.ExitStack())
        esC = es.enter_context(contextlib.ExitStack())

        def sbb(name, shape, dt=F32):
            return esB.enter_context(nc.sbuf_tensor(name, shape, dt))

        def sbb1(name, shape, dt=F32):
            return esB1.enter_context(nc.sbuf_tensor(name, shape, dt))

        def sbc(name, shape, dt=F32):
            return esC.enter_context(nc.sbuf_tensor(name, shape, dt))

        if not cc_evts:
            P.do("gpsimd", lambda g: g.collective_compute(
                "ReduceScatter", ALU.add,
                replica_groups=[[0, 1], [2, 3], [4, 5], [6, 7]],
                ins=[opart[:, :]], outs=[osum[:, :]]),
                [opart_done, attn_bar, sg.evt()], cc)
            cc_evts = [cc.evt()] * 8

        sx_r = sbb("sx_r", [128, 16])
        hq_s = sbb("hq_s", [128, 16, 2])
        hmax = sbb("hmax", [128, 16])
        f_mu = sbb("f_mu", [128, 4])
        f_sg = sbb("f_sg", [128, 4])
        f_t1 = sbb("f_t1", [128, 4])
        fscr = sbb("fscr", [128, 4, C])
        xqT = sbc("xqT", [128, 8, TH], BF16)
        x1t = sbb1("x1t", [128, 4, C])
        xr1 = sbb1("xr1", [128, 4, C])
        xq = sbb1("xq", [128, 4, C], BF16)

        for fi4 in range(4):
            tsl = slice(fi4 * 512, (fi4 + 1) * 512)
            dma("sync", x1t[:], osum[tsl, :].rearrange("(a p) c -> p a c",
                                                       p=128),
                waits=[cc_evts[2 * fi4 + 1], attn_bar, dv.evt()])
            dma("sync", xr1[:], d["xtm"][tsl, :].rearrange(
                "(a p) c -> p a c", p=128))
            ld_evt = sl.evt()
            vec(lambda v: v.tensor_tensor(x1t[:], x1t[:], xr1[:], ALU.add),
                waits=[ld_evt, st.evt()])
            vec(lambda v: v.tensor_reduce(f_mu[:], x1t[:], AX.X, ALU.add))
            vec(lambda v: v.tensor_scalar(f_mu[:], f_mu[:], 1.0 / C, None,
                                          ALU.mult))
            pool(lambda g: g.tensor_tensor(fscr[:], x1t[:], x1t[:], ALU.mult),
                 waits=[dv.evt()])
            vec(lambda v: v.tensor_reduce(f_sg[:], fscr[:], AX.X, ALU.add),
                waits=[po.evt()])
            vec(lambda v: v.tensor_tensor(f_t1[:], f_mu[:], f_mu[:],
                                          ALU.mult))
            vec(lambda v: v.scalar_tensor_tensor(
                f_sg[:], f_sg[:], 1.0 / C, f_t1[:], ALU.mult, ALU.subtract))
            act(lambda s: s.activation(out=f_sg[:], in_=f_sg[:], func=AF.Ln,
                                       bias=CEL), waits=[dv.evt()])
            act(lambda s: s.activation(out=f_sg[:], in_=f_sg[:], func=AF.Exp,
                                       scale=CNH))
            for a_ in range(4):
                vec(lambda v, a_=a_: v.scalar_tensor_tensor(
                    x1t[:, a_, :], x1t[:, a_, :], f_mu[:, a_:a_ + 1],
                    f_sg[:, a_:a_ + 1].to_broadcast((128, C)),
                    ALU.subtract, ALU.mult), waits=[ac.evt()])
            x1_evt = dv.evt()
            x1store = dma("sync",
                          x1sp[tsl, :].rearrange("(a p) c -> p a c", p=128),
                          x1t[:], waits=[x1_evt])
            vec(lambda v: v.tensor_reduce(f_mu[:], x1t[:], AX.X, ALU.max,
                                          apply_absolute_value=True))
            vec(lambda v: v.tensor_scalar(f_mu[:], f_mu[:], 1e-5,
                                          1.0 / 127.0, ALU.max, ALU.mult))
            vec(lambda v, fi4=fi4: v.tensor_copy(
                sx_r[:, fi4 * 4:(fi4 + 1) * 4], f_mu[:]))
            vec(lambda v: v.reciprocal(f_sg[:], f_mu[:]))
            for a_ in range(4):
                vec(lambda v, a_=a_: v.tensor_scalar(
                    x1t[:, a_, :], x1t[:, a_, :], f_sg[:, a_:a_ + 1], MAGIC,
                    ALU.mult, ALU.add), waits=[x1store] if a_ == 0 else [])
            vec(lambda v: v.tensor_scalar(xq[:], x1t[:], MAGIC, None,
                                          ALU.subtract))
            xq_evt = dv.evt()
            for a_ in range(4):
                dma("sync",
                    xqT[:, :, fi4 * 512 + a_ * 128:
                        fi4 * 512 + (a_ + 1) * 128],
                    xq[:, a_, :].rearrange("p (t x) -> p t x", x=128),
                    waits=[xq_evt], sem=st, transpose=True)
        xqT_evt = st.evt()
        sx_done = dv.evt()
        esB1.close()

        if phases == "AB":
            P.do("vector", lambda v: v.engine_nop(),
                 [xqT_evt, sx_done, sl.evt(), sg.evt(), ac.evt(), po.evt(),
                  pe.evt(), sw_done], dv)
            P.finalize()
            return nc

        wg_f = sbc("wg_f", [128, 8, 512])
        wu_f = sbc("wu_f", [128, 8, 512])
        wg_t = sbc("wg_t", [128, 2, 8, 512], BF16)
        wu_t = sbc("wu_t", [128, 2, 8, 512], BF16)
        tscr = sbc("tscr", [128, 8, 512])
        hblk = sbc("hblk", [128, 512])
        m1c = sbc("m1c", [128, 4])
        sigt = sbc("sigt", [128, 512])
        m1t = sbc("m1t", [128, 512])
        vec(lambda v: v.memset(hmax[:], 0.0))

        def ternarize(dst, srcf, wi, w_extra):
            vec(lambda v: v.tensor_scalar(
                tscr[:], srcf[:], swcol[:, wi:wi + 1], MAGIC, ALU.mult,
                ALU.add), waits=[sw_done] + [w for w in w_extra if w])
            rd = dv.evt()
            vec(lambda v: v.tensor_scalar(
                tscr[:], tscr[:], MAGIC, 1.0, ALU.subtract, ALU.min))
            vec(lambda v: v.tensor_scalar(dst, tscr[:], -1.0, None,
                                          ALU.max))
            return rd, dv.evt()

        prev_rd_g = None
        prev_rd_u = None
        tern_done = [None, None]
        buf_pe2 = [None, None]
        prev_hb_store = None
        for si in range(NSLAB + 1):
            if si < NSLAB:
                sb = si % 2
                isl = slice(si * 512, (si + 1) * 512)
                dma("sync", wg_f[:],
                    d["wgT"].rearrange("(c p) i -> p c i", p=128)[:, :, isl],
                    waits=[xqT_evt] if si == 0 else [prev_rd_g])
                lg = sl.evt()
                prev_rd_g, _ = ternarize(wg_t[:, sb, :, :], wg_f, 0,
                                         [lg, buf_pe2[sb]])
                dma("sync", wu_f[:],
                    d["wuT"].rearrange("(c p) i -> p c i", p=128)[:, :, isl],
                    waits=[prev_rd_u] if prev_rd_u else [])
                lu = sl.evt()
                prev_rd_u, tern_done[sb] = ternarize(wu_t[:, sb, :, :], wu_f,
                                                     1, [lu])
            if si == 0:
                continue
            sj = si - 1
            sb2 = sj % 2
            isl2 = slice(sj * 512, (sj + 1) * 512)
            for fi in range(FTT):
                bgp = grab(); bup = grab()
                for cc2 in range(8):
                    mm(bgp, bgp.ap[:, 0:512],
                       xqT[:, cc2, fi * 128:(fi + 1) * 128],
                       wg_t[:, sb2, cc2, :], cc2 == 0, cc2 == 7,
                       waits=[tern_done[sb2], xqT_evt] if cc2 == 0 else [])
                for cc2 in range(8):
                    mm(bup, bup.ap[:, 0:512],
                       xqT[:, cc2, fi * 128:(fi + 1) * 128],
                       wu_t[:, sb2, cc2, :], cc2 == 0, cc2 == 7,
                       waits=[tern_done[sb2]] if cc2 == 0 else [])
                penop()
                gu_evt = pe.evt()
                vec(lambda v, fi=fi: v.tensor_tensor(
                    m1c[:, 0:1], sx_r[:, fi:fi + 1], swcol[:, 4:5], ALU.mult),
                    waits=[gu_evt, sx_done, sw_done])
                vec(lambda v, fi=fi: v.tensor_tensor(
                    m1c[:, 1:2], sx_r[:, fi:fi + 1], swcol[:, 5:6], ALU.mult))
                vec(lambda v: v.tensor_tensor(m1c[:, 2:3], m1c[:, 0:1],
                                              m1c[:, 1:2], ALU.mult))
                sc_evt = dv.evt()
                act(lambda s, bgp=bgp: s.activation(
                    out=sigt[:], in_=bgp.ap[:, 0:512], func=AF.Sigmoid,
                    scale=m1c[:, 0:1]), waits=[sc_evt])
                sig_evt = ac.evt()
                vec(lambda v, bgp=bgp: v.scalar_tensor_tensor(
                    m1t[:], bgp.ap[:, 0:512], m1c[:, 2:3], sigt[:], ALU.mult,
                    ALU.mult), waits=[sig_evt])
                bgp.war = dv.evt()
                vec(lambda v, bup=bup: v.tensor_tensor(
                    hblk[:], m1t[:], bup.ap[:, 0:512], ALU.mult),
                    waits=[prev_hb_store] if prev_hb_store else [])
                bup.war = dv.evt()
                vec(lambda v: v.tensor_reduce(m1c[:, 3:4], hblk[:], AX.X,
                                              ALU.max,
                                              apply_absolute_value=True))
                vec(lambda v, fi=fi: v.tensor_tensor(
                    hmax[:, fi:fi + 1], hmax[:, fi:fi + 1], m1c[:, 3:4],
                    ALU.max))
                hb_evt = dv.evt()
                prev_hb_store = dma("sync",
                                    hsp[fi * 128:(fi + 1) * 128, isl2],
                                    hblk[:], waits=[hb_evt])
            buf_pe2[sb2] = pe.evt()
        prev_slab_pe = pe.evt()
        h_spill = sl.evt()
        vec(lambda v: v.tensor_scalar(hmax[:], hmax[:], 1e-5, 1.0 / 127.0,
                                      ALU.max, ALU.mult))
        vec(lambda v: v.tensor_copy(hq_s[:, :, 1], hmax[:]))
        vec(lambda v: v.reciprocal(hmax[:], hmax[:]))
        vec(lambda v: v.tensor_copy(hq_s[:, :, 0], hmax[:]))
        s2_done = dv.evt()
        esC.close()

        if phases == "ABC":
            P.do("vector", lambda v: v.engine_nop(),
                 [s2_done, h_spill, sl.evt(), sg.evt(), ac.evt(), po.evt(),
                  pe.evt()], dv)
            P.finalize()
            return nc

        esD = es.enter_context(contextlib.ExitStack())

        def sbd(name, shape, dt=F32):
            return esD.enter_context(nc.sbuf_tensor(name, shape, dt))

        wd_t = sbd("wd_t", [128, 32, 1024], BF16)
        wdl = sbd("wdl", [128, 4096])
        tscrD = sbd("tscrD", [128, 4096])
        wdq = sbd("wdq", [128, 2, 4096], BF16)
        hqt = sbd("hqt", [128, 2, 32, 128], BF16)
        o2 = sbd("o2", [128, 4, C])
        x1d = sbd("x1d", [128, 4, C])

        for sj in range(8):
            dma("sync", wdl[:].rearrange("p (i c) -> p i c", c=1024),
                d["wdT"].rearrange("(i p) c -> p i c", p=128)[
                    :, sj * 4:(sj + 1) * 4, :],
                waits=[h_spill, s2_done, prev_slab_pe] if sj == 0
                else [dv.evt()])
            vec(lambda v: v.tensor_scalar(
                tscrD[:], wdl[:], swcol[:, 2:3], MAGIC, ALU.mult, ALU.add),
                waits=[sl.evt(), sw_done])
            vec(lambda v: v.tensor_scalar(tscrD[:], tscrD[:], MAGIC, 1.0,
                                          ALU.subtract, ALU.min))
            vec(lambda v, sj=sj: v.tensor_scalar(
                wd_t[:, sj * 4:(sj + 1) * 4, :].rearrange("p a q -> p (a q)"),
                tscrD[:], -1.0, None, ALU.max))
        wd_done = dv.evt()

        prev_hqT = [None, None]
        buf_pe = [None, None]
        st_out = None
        lx_evt = None
        ld_evt = None
        for fi in range(16):
            q4, a_ = divmod(fi, 4)
            pb = fi % 2
            if fi == 0:
                dma("gpsimd", wdl[:], hsp[0:128, :],
                    waits=[h_spill, wd_done])
                ld_evt = sg.evt()
                dma("gpsimd", x1d[:],
                    x1sp[0:512, :].rearrange("(a p) c -> p a c", p=128),
                    waits=[h_spill])
                lx_evt = sg.evt()
            vec(lambda v, fi=fi: v.tensor_scalar(
                tscrD[:], wdl[:], hq_s[:, fi, 0:1], MAGIC, ALU.mult,
                ALU.add), waits=[ld_evt, s2_done]
                + ([wd_done] if fi == 0 else []))
            rd_evt = dv.evt()
            vec(lambda v, pb=pb: v.tensor_scalar(
                wdq[:, pb, :], tscrD[:], MAGIC, None, ALU.subtract),
                waits=[prev_hqT[pb]] if prev_hqT[pb] else [])
            hqq = dv.evt()
            if fi + 1 < 16:
                dma("gpsimd", wdl[:], hsp[(fi + 1) * 128:(fi + 2) * 128, :],
                    waits=[rd_evt])
                ld_evt = sg.evt()
            dma("sync", hqt[:, pb, :, :],
                wdq[:, pb, :].rearrange("p (t x) -> p t x", x=128),
                waits=[hqq] + ([buf_pe[pb]] if buf_pe[pb] else []),
                sem=st, transpose=True)
            hqT_evt = st.evt()
            prev_hqT[pb] = hqT_evt
            for eh in range(2):
                bo = grab()
                for ic in range(32):
                    mm(bo, bo.ap[:, 0:512], hqt[:, pb, ic, :],
                       wd_t[:, ic, eh * 512:(eh + 1) * 512],
                       ic == 0, ic == 31,
                       waits=[hqT_evt, wd_done] if ic == 0 else [])
                penop()
                aw = [pe.evt()]
                if a_ == 0 and eh == 0 and st_out:
                    aw.append(st_out)
                act(lambda s, bo=bo, a_=a_, eh=eh: s.copy(
                    o2[:, a_, eh * 512:(eh + 1) * 512], bo.ap[:, 0:512]),
                    waits=aw)
                bo.war = ac.evt()
            buf_pe[pb] = pe.evt()
            if a_ < 3:
                continue
            tsl = slice(q4 * 512, (q4 + 1) * 512)
            down_evt = ac.evt()
            for a_ in range(4):
                fi = q4 * 4 + a_
                vec(lambda v, fi=fi: v.tensor_tensor(
                    f_t1[:, 0:1], hq_s[:, fi, 1:2], swcol[:, 6:7], ALU.mult),
                    waits=[down_evt, lx_evt])
                vec(lambda v, a_=a_: v.scalar_tensor_tensor(
                    o2[:, a_, :], o2[:, a_, :], f_t1[:, 0:1], x1d[:, a_, :],
                    ALU.mult, ALU.add))
            vec(lambda v: v.tensor_reduce(f_mu[:], o2[:], AX.X, ALU.add))
            vec(lambda v: v.tensor_scalar(f_mu[:], f_mu[:], 1.0 / C, None,
                                          ALU.mult))
            pool(lambda g: g.tensor_tensor(fscr[:], o2[:], o2[:], ALU.mult),
                 waits=[dv.evt()])
            vec(lambda v: v.tensor_reduce(f_sg[:], fscr[:], AX.X, ALU.add),
                waits=[po.evt()])
            vec(lambda v: v.tensor_tensor(f_t1[:], f_mu[:], f_mu[:],
                                          ALU.mult))
            vec(lambda v: v.scalar_tensor_tensor(
                f_sg[:], f_sg[:], 1.0 / C, f_t1[:], ALU.mult, ALU.subtract))
            act(lambda s: s.activation(out=f_sg[:], in_=f_sg[:], func=AF.Ln,
                                       bias=CEL), waits=[dv.evt()])
            act(lambda s: s.activation(out=f_sg[:], in_=f_sg[:], func=AF.Exp,
                                       scale=CNH))
            for a_ in range(4):
                vec(lambda v, a_=a_: v.scalar_tensor_tensor(
                    o2[:, a_, :], o2[:, a_, :], f_mu[:, a_:a_ + 1],
                    f_sg[:, a_:a_ + 1].to_broadcast((128, C)),
                    ALU.subtract, ALU.mult), waits=[ac.evt()])
            oe = dv.evt()
            dma("gpsimd", out[tsl, :].rearrange("(a p) c -> p a c", p=128),
                o2[:], waits=[oe])
            st_out = sg.evt()
            if q4 + 1 < 4:
                dma("gpsimd", x1d[:],
                    x1sp[(q4 + 1) * 512:(q4 + 2) * 512, :].rearrange(
                        "(a p) c -> p a c", p=128), waits=[dv.evt()])
                lx_evt = sg.evt()

        if "x1" in dbg_out:
            dma("gpsimd", dbg_out["x1"][:, :], x1sp[:, :],
                waits=[sl.evt()])
        if "h" in dbg_out:
            dma("gpsimd", dbg_out["h"][:, :], hsp[:, :], waits=[sl.evt()])
        final_store = sl.evt()
        P.do("vector", lambda v: v.engine_nop(),
             [final_store, sg.evt()], dv)
        esD.close()
        esB.close()

        P.finalize()

    return nc


def kernel(**inputs):
    maps = host_prep(inputs)
    nc = build_program()
    res = run_bass_kernel_spmd(nc, maps, core_ids=list(range(8)))
    outa = np.empty((B, T, C), np.float32)
    for core in range(8):
        b, hg = divmod(core, 2)
        outa[b, hg * TH:(hg + 1) * TH, :] = res.results[core]["out"]
    return outa
